# version 14
# baseline (speedup 1.0000x reference)
import math
import numpy as np
import concourse.bass as bass
import concourse.mybir as mybir
from concourse.bass_utils import run_bass_kernel_spmd

F32 = mybir.dt.float32
BF16 = mybir.dt.bfloat16
AF = mybir.ActivationFunctionType
ALU = mybir.AluOpType
AX = mybir.AxisListType

D = 2048
NH = 16
DH = 64
DV = 128
FF = 1024
NE = 8
NF = FF // 128
NC = D // 128
LAMBDA_INIT = 0.8 - 0.6 * math.exp(-0.3)
EPS_LN = 1e-5
EPS_RMS_ATTN = 1e-5
EPS_RMS_FFN = 1e-6
VCH = 272


class Ch:
    def __init__(self, nc, name, step=1):
        self.sem = nc.alloc_semaphore(name)
        self.n = 0
        self.step = step

    def bump(self):
        self.n += self.step
        return self.n


def build(T=2048, dumps=()):
    nc = bass.Bass("TRN2", target_bir_lowering=False)
    NT = T // 128
    JW = min(512, T)
    NJ = T // JW
    HALF = T // 2
    NTH = HALF // 128
    TCW = min(512, HALF)
    n_tch = HALF // TCW
    QH = D // 2

    x_in = nc.dram_tensor("x_in", [T, D], F32, kind="ExternalInput")
    wqkv = nc.dram_tensor("wqkv", [D, 3 * D], F32, kind="ExternalInput")
    wproj = nc.dram_tensor("wproj", [D, D], F32, kind="ExternalInput")
    rmsw_in = nc.dram_tensor("rmsw_in", [128, D], F32, kind="ExternalInput")
    lam_in = nc.dram_tensor("lam_in", [128, 4 * DH], F32, kind="ExternalInput")
    routw_in = nc.dram_tensor("routw_in", [D, NE], F32, kind="ExternalInput")
    wg_in = nc.dram_tensor("wg_in", [NE, NF, NC, 128, 128], F32, kind="ExternalInput")
    wu_in = nc.dram_tensor("wu_in", [NE, NF, NC, 128, 128], F32, kind="ExternalInput")
    wd_in = nc.dram_tensor("wd_in", [NE, FF, D], F32, kind="ExternalInput")
    swg_in = nc.dram_tensor("swg_in", [NF, NC, 128, 128], F32, kind="ExternalInput")
    swu_in = nc.dram_tensor("swu_in", [NF, NC, 128, 128], F32, kind="ExternalInput")
    swd_in = nc.dram_tensor("swd_in", [FF, D], F32, kind="ExternalInput")
    out_t = nc.dram_tensor("out", [T, D], F32, kind="ExternalOutput")

    sk = lambda nm: "ExternalOutput" if nm in dumps else "Internal"
    lnT_hbm = nc.dram_tensor("lnT_hbm", [NC, 128, T], BF16, kind=sk("lnT_hbm"))
    dnT_hbm = nc.dram_tensor("dnT_hbm", [NC, 128, T], BF16, kind=sk("dnT_hbm"))
    hT_hbm = nc.dram_tensor("hT_hbm", [NC, 128, T], BF16, kind=sk("hT_hbm"))
    x1_hbm = nc.dram_tensor("x1_hbm", [T, D], F32, kind=sk("x1_hbm"))
    dump_sroute = None
    if "s_route" in dumps:
        dump_sroute = nc.dram_tensor("s_route_d", [128, NT * 8], F32,
                                     kind="ExternalOutput")

    ident = nc.alloc_sbuf_tensor("ident", [128, 128], BF16).ap()
    mask01 = nc.alloc_sbuf_tensor("mask01", [128, 128], BF16).ap()
    rmsw = nc.alloc_sbuf_tensor("rmsw", [128, D], F32).ap()
    lamv = nc.alloc_sbuf_tensor("lamv", [128, 4 * DH], F32).ap()
    tiny = nc.alloc_sbuf_tensor("tiny", [128, 128], F32).ap()
    i8 = nc.alloc_sbuf_tensor("i8", [128, 48], F32).ap()
    routw = nc.alloc_sbuf_tensor("routw", [128, NC * 8], BF16).ap()
    s_route = nc.alloc_sbuf_tensor("s_route", [128, NT * 8], F32).ap()
    NSL = 16
    dnscr = [nc.alloc_sbuf_tensor(f"dnscr{i}", [128, 128], BF16).ap() for i in range(NSL)]
    asb = [nc.alloc_sbuf_tensor(f"asb{i}", [128, 264], F32).ap() for i in range(NSL)]
    sqjunk = nc.alloc_sbuf_tensor("sqjunk", [128, D], BF16).ap()

    GCOL = dict(lam=0, dot1=1, dot2=2, e1=3, e2=4, m1=5, m2=6, dlt=7, w1=8,
                w2=9, eps_ln=10, eps_ffn=11, eps11=12)
    LNCOL = dict(sum=0, sumsq=1, varD=2, rstd=3, bias=4, vv=5, ta=6, q=7, r1b=8)

    def tc(name, par=0):
        if name in GCOL:
            c = GCOL[name]
        else:
            c = 32 + 16 * par + LNCOL[name]
        return tiny[:, c:c + 1]

    def ln1c(name, p):
        c = 64 + 16 * p + LNCOL[name]
        return tiny[:, c:c + 1]

    def t2(name, sl):
        c = 64 + 4 * sl + {"r1": 0, "r2": 1, "ms": 2, "rstd2": 3}[name]
        return tiny[:, c:c + 1]

    base = nc.sbuf_base
    avail = nc.sbuf_top - base

    def at(name, shape, dtype, off):
        sz = shape[1] * (4 if dtype == F32 else 2)
        assert off >= 0 and off + sz <= avail, (name, off, sz, avail)
        return nc.alloc_sbuf_tensor_at(name, shape, dtype,
                                       offset=base + off).ap()

    lnTsz = NC * JW * 2
    o = 0
    lnTc = [at(f"lnTc{i}", [128, NC * JW], BF16, o + i * lnTsz) for i in range(2)]
    o += 2 * lnTsz
    o_wqk = o
    wqk_st = [at(f"wqk{i}", [128, NC * 512], BF16, o + i * NC * 1024) for i in range(2)]
    o += 2 * NC * 1024
    o_wv = o
    wv_st = [at(f"wv{i}", [128, NC * 256], BF16, o + i * NC * 512) for i in range(2)]
    o += 2 * NC * 512
    o_qkT = o
    qkT = at("qkT", [128, 4 * T], BF16, o)
    o += 4 * T * 2
    o_pT = o
    pT = []
    for a in range(2):
        pT.append(at(f"pT{a}", [128, NT * 2 * JW], BF16, o))
        o += NT * 2 * JW * 2
    o_vbuf = o
    vbuf = at("vbuf", [128, NT * VCH], BF16, o)
    o += NT * VCH * 2
    dstage = at("dstage", [128, 2 * JW], BF16, o)
    o += 2 * JW * 2
    p12_top = o
    o_dstage_end = o
    if 2 * D * 2 <= o_dstage_end - o_vbuf:
        o_lnscr = o_vbuf
    else:
        o_lnscr = p12_top
        p12_top += 2 * D * 2
    lnscr = [at(f"lnscr{i}", [128, D], BF16, o_lnscr + i * D * 2) for i in range(2)]
    if 2 * NC * 256 <= 2 * NT * JW * 2:
        o_lnts = o_pT + 2 * NT * JW * 2
    else:
        o_lnts = p12_top
        p12_top += 2 * NC * 256
    lnT_stage = [at(f"lnTs{i}", [128, NC * 128], BF16,
                    o_lnts + i * NC * 256) for i in range(2)]

    wproj_sb = at("wproj_sb", [128, 4 * NC * 512], BF16, 0)
    o3 = 4 * NC * 512 * 2
    o_dT = o3
    dTbuf = [at(f"dT{i}", [128, NC * JW], BF16, o3 + i * lnTsz) for i in range(2)]
    o3 += 2 * lnTsz
    o_hth = o3
    hT_half = at("hT_half", [128, NC * HALF], BF16, o3)
    o3 += NC * HALF * 2
    o_x1t = o3
    x1t = [at(f"x1t{i}", [128, D], F32, o3 + i * D * 4) for i in range(2)]
    o3 += 2 * D * 4
    hbuf = [at(f"hbuf{i}", [128, D], BF16, o3 + i * D * 2) for i in range(2)]
    o3 += 2 * D * 2
    o_hTs = o3
    hTstage = [at(f"hTs{i}", [128, NC * 128], BF16, o3 + i * NC * 256) for i in range(2)]
    o3 += 2 * NC * 256
    p3_top = o3

    ysz = NTH * D * 4
    assert ysz <= o_dT, (ysz, o_dT)
    ybuf = at("ybuf", [128, NTH * D], F32, 0)
    asz = NF * HALF * 2
    sgsz = 2 * TCW * 2
    if asz + 4 * NC * 256 <= o_hth - o_dT:
        o_a = o_dT
        o_wgu = o_dT + asz
        o_wd = o_x1t
        if o_wgu + 4 * NC * 256 + sgsz <= o_hth:
            o_sg = o_wgu + 4 * NC * 256
        else:
            o_sg = o_wd + NF * D * 2
        assert o_wgu + 4 * NC * 256 <= o_hth, (o_wgu, o_hth)
        assert o_wd >= o_hth + NC * HALF * 2
    else:
        o_wd = o_x1t
        o_sg = o_wd + NF * D * 2
        o_a = max(p12_top, p3_top, o_sg + sgsz)
        o_wgu = o_a + asz
    abuf = at("abuf", [128, NF * HALF], BF16, o_a)
    wgu_st = []
    for nmi, nm in enumerate(("wgst", "wust")):
        wgu_st.append([at(f"{nm}{i}", [128, NC * 128], BF16,
                          o_wgu + (2 * nmi + i) * NC * 256) for i in range(2)])
    wd_sb = at("wd_sb", [128, NF * D], BF16, o_wd)
    sgscr = [at(f"sg{i}", [128, TCW], BF16, o_sg + i * TCW * 2) for i in range(2)]
    p4_top = max(o_sg + sgsz, o_wgu + 4 * NC * 256, o_wd + NF * D * 2)
    if 4 * D * 4 <= 2 * NT * 2 * JW * 2 and o_pT == o_hth and 4 * D * 4 <= NC * HALF * 2:
        o_xb = o_pT
    else:
        o_xb = avail - 4 * D * 4
        assert o_xb >= max(p12_top, p3_top, p4_top), \
            (o_xb, p12_top, p3_top, p4_top)
    xbuf = [at(f"xbuf{i}", [128, D], F32, o_xb + i * D * 4) for i in range(4)]
    assert max(p12_top, p3_top, p4_top) <= avail, (p12_top, p3_top, p4_top, avail)

    pA = nc.alloc_psum_tensor("pA", [128, 1024], F32).ap()
    pB = nc.alloc_psum_tensor("pB", [128, 1024], F32).ap()
    pCD = nc.alloc_psum_tensor("pCD", [128, 1024], F32).ap()
    pbf = [nc.alloc_psum_tensor(f"pbf{i}", [128, 1024], BF16).ap() for i in range(2)]
    bank512 = [pA[:, 0:512], pA[:, 512:1024], pB[:, 0:512], pB[:, 512:1024],
               pCD[:, 0:512], pCD[:, 512:1024]]
    spair = [pA, pB]

    s_init = Ch(nc, "s_init")
    s_tn = Ch(nc, "s_tn")
    s_vp = Ch(nc, "s_vp")
    s_sp = Ch(nc, "s_sp")
    dch = {}
    for nm in ("xl0", "xl1", "xl2", "xl3", "lns0", "lns1", "lnc0", "lnc1", "wq0", "wq1",
               "dst0", "dst1", "wp", "rw", "dT0", "dT1", "x30", "x31", "x32", "x33", "x1s0",
               "x1s1", "hts0", "hts1", "hth", "wg0", "wg1", "wd", "yini",
               "yst", "pre"):
        dch[nm] = Ch(nc, "d_" + nm, 16)

    ops = {k: [] for k in ("tensor", "vector", "scalar", "gpsimd", "sync")}

    def em(eng, fn):
        ops[eng].append(fn)

    last_wait = {}

    def wait(eng, ch, val):
        assert val <= ch.n, (eng, val, ch.n)
        key = (eng, ch.sem.num)
        if val <= last_wait.get(key, 0):
            return
        last_wait[key] = val
        em(eng, lambda e, s=ch.sem, v=val: e.wait_ge(s, v))

    def g_init(g):
        steps = [
            lambda: g.memset(ident[:, :], 0.0),
            lambda: g.affine_select(out=ident[:, :], in_=ident[:, :],
                                    compare_op=ALU.not_equal, fill=1.0, base=0,
                                    pattern=[[-1, 128]], channel_multiplier=1),
            lambda: g.memset(mask01[:, :], 0.0),
            lambda: g.affine_select(out=mask01[:, :], in_=mask01[:, :],
                                    compare_op=ALU.is_gt, fill=1.0, base=0,
                                    pattern=[[-1, 128]], channel_multiplier=1),
        ]
        n = 0
        for st in steps:
            st().then_inc(s_init.sem, 1)
            n += 1
            g.wait_ge(s_init.sem, n)
    em("gpsimd", g_init)
    for _ in range(4):
        s_init.bump()
    init_done = s_init.n

    def sy_pre(sy):
        sy.dma_start(rmsw[:, :], rmsw_in.ap()).then_inc(dch["pre"].sem, 16)
        sy.dma_start(lamv[:, :], lam_in.ap()).then_inc(dch["pre"].sem, 16)
    em("sync", sy_pre)
    dch["pre"].bump(); dch["pre"].bump()
    pre_done = dch["pre"].n

    def g_rw(g):
        g.dma_start(routw.rearrange("p (c e) -> p c e", e=8),
                    routw_in.ap().rearrange("(c p) e -> p c e", p=128)
                    ).then_inc(dch["rw"].sem, 16)
    em("gpsimd", g_rw)
    dch["rw"].bump()
    routw_done = dch["rw"].n

    def v_consts(v):
        v.memset(tc("eps_ln"), EPS_LN)
        v.memset(tc("eps_ffn"), EPS_RMS_FFN)
        v.memset(tc("eps11"), EPS_LN * EPS_RMS_FFN).then_inc(s_vp.sem, 1)
    em("vector", v_consts); s_vp.bump()
    wait("vector", s_vp, s_vp.n)

    wait("vector", dch["pre"], pre_done)
    wait("vector", s_init, init_done)

    def vchain(v_ops):
        for fn in v_ops:
            em("vector", fn)
            s_vp.bump()
            wait("vector", s_vp, s_vp.n)

    vchain([
        lambda v: v.tensor_mul(lamv[:, 0:DH], lamv[:, 0:DH],
                               lamv[:, DH:2 * DH]).then_inc(s_vp.sem, 1),
        lambda v: v.reduce_sum(tc("dot1"), lamv[:, 0:DH], axis=AX.X).then_inc(s_vp.sem, 1),
        lambda v: v.tensor_mul(lamv[:, 2 * DH:3 * DH], lamv[:, 2 * DH:3 * DH],
                               lamv[:, 3 * DH:4 * DH]).then_inc(s_vp.sem, 1),
        lambda v: v.reduce_sum(tc("dot2"), lamv[:, 2 * DH:3 * DH],
                               axis=AX.X).then_inc(s_vp.sem, 1),
    ])
    wait("scalar", s_vp, s_vp.n)

    def s_lam(s):
        s.activation(tc("e1"), tc("dot1"), AF.Exp).then_inc(s_sp.sem, 1)
        s.activation(tc("e2"), tc("dot2"), AF.Exp).then_inc(s_sp.sem, 1)
    em("scalar", s_lam); s_sp.bump(); s_sp.bump()
    wait("vector", s_sp, s_sp.n)
    vchain([
        lambda v: v.tensor_sub(tc("lam"), tc("e1"), tc("e2")).then_inc(s_vp.sem, 1),
        lambda v: v.tensor_single_scalar(tc("lam"), tc("lam"), LAMBDA_INIT, ALU.add).then_inc(s_vp.sem, 1),
    ])
    lam_done_v = s_vp.n

    tp_n = [0]
    tpe_done = []

    def transpose_tile(src, dst_stage, first_evac_guards=(), dst_fn=None):
        for c0 in range(0, NC, 8):
            nn = min(8, NC - c0)
            k = tp_n[0]
            if k >= 2:
                wait("tensor", s_vp, tpe_done[k - 2])
            for ci in range(nn):
                def t_tr(t, src=src, c=c0 + ci, ci=ci, k=k, last=(ci == nn - 1)):
                    ins = t.transpose(pbf[k % 2][:, ci * 128:(ci + 1) * 128],
                                      src[:, c * 128:(c + 1) * 128], ident[:, :])
                    if last:
                        ins.then_inc(s_tn.sem, 1)
                em("tensor", t_tr)
            s_tn.bump()
            wait("vector", s_tn, s_tn.n)
            if c0 == 0:
                for g in first_evac_guards:
                    wait("vector", *g)
            if dst_fn is not None:
                for ci in range(nn):
                    def v_ev(v, dst=dst_fn(c0 + ci), ci=ci, k=k,
                             last=(ci == nn - 1)):
                        ins = v.tensor_copy(dst, pbf[k % 2][:, ci * 128:(ci + 1) * 128])
                        if last:
                            ins.then_inc(s_vp.sem, 1)
                    em("vector", v_ev)
                s_vp.bump()
            else:
                def v_ev(v, dst=dst_stage[:, c0 * 128:(c0 + nn) * 128], k=k, nn=nn):
                    v.tensor_copy(dst, pbf[k % 2][:, 0:nn * 128]).then_inc(s_vp.sem, 1)
                em("vector", v_ev); s_vp.bump()
            tpe_done.append(s_vp.n)
            tp_n[0] += 1

    wait("tensor", s_init, init_done)
    app_done = {}
    red_done = {}
    lnscr_free = {}
    lns_guard = [0, 0]
    lnTc_ready = [0, 0]
    xl_n = {}
    bsn1 = {}

    def p1_loads(grp):
        for i in grp:
            chx = dch["xl%d" % (i % 4)]
            if i >= 4:
                wait("sync", s_sp, app_done[i - 4])
                wait("sync", s_vp, red_done[i - 4])
            def sy_x(sy, i=i, ch=chx):
                sy.dma_start(xbuf[i % 4][:, :],
                             x_in.ap()[i * 128:(i + 1) * 128, :]).then_inc(ch.sem, 16)
            em("sync", sy_x); chx.bump()
            xl_n[i] = (chx, chx.n)

    def p1_stats(grp):
        sqn = {}; stv = {}
        for i in grp:
            wait("scalar", *xl_n[i])
            if i > 0:
                wait("scalar", s_sp, s_sp.n)
            def s_sq(s, i=i):
                s.activation(sqjunk[:, :], xbuf[i % 4][:, :], AF.Square,
                             accum_out=ln1c("sumsq", i % 4)).then_inc(s_sp.sem, 1)
            em("scalar", s_sq); s_sp.bump()
            sqn[i] = s_sp.n
        for i in grp:
            wait("vector", *xl_n[i])
            def v_red(v, i=i):
                v.reduce_sum(ln1c("sum", i % 4), xbuf[i % 4][:, :],
                             axis=AX.X).then_inc(s_vp.sem, 1)
            em("vector", v_red); s_vp.bump()
            red_done[i] = s_vp.n
        for i in grp:
            wait("vector", s_vp, red_done[i])
            def v_msq(v, i=i):
                v.tensor_scalar(ln1c("varD", i % 4), ln1c("sum", i % 4),
                                ln1c("sum", i % 4), 1.0 / D, ALU.mult,
                                ALU.mult).then_inc(s_vp.sem, 1)
            em("vector", v_msq); s_vp.bump()
            stv[i] = s_vp.n
        for i in grp:
            wait("vector", s_vp, stv[i])
            wait("vector", s_sp, sqn[i])
            def v_var(v, i=i):
                v.tensor_sub(ln1c("varD", i % 4), ln1c("sumsq", i % 4),
                             ln1c("varD", i % 4)).then_inc(s_vp.sem, 1)
            em("vector", v_var); s_vp.bump()
            stv[i] = s_vp.n
        lnn = {}
        for i in grp:
            wait("scalar", s_vp, stv[i])
            def s_ln(s, i=i):
                s.activation(ln1c("rstd", i % 4), ln1c("varD", i % 4), AF.Ln,
                             bias=tc("eps_ln"), scale=1.0 / D).then_inc(s_sp.sem, 1)
            em("scalar", s_ln); s_sp.bump()
            lnn[i] = s_sp.n
        for i in grp:
            wait("scalar", s_sp, lnn[i])
            def s_ex(s, i=i):
                s.activation(ln1c("rstd", i % 4), ln1c("rstd", i % 4), AF.Exp,
                             scale=-0.5).then_inc(s_sp.sem, 1)
            em("scalar", s_ex); s_sp.bump()
            lnn[i] = s_sp.n
        for i in grp:
            wait("vector", s_sp, lnn[i])
            def v_bias(v, i=i):
                v.tensor_scalar(ln1c("bias", i % 4), ln1c("sum", i % 4),
                                ln1c("rstd", i % 4), -1.0 / D, ALU.mult,
                                ALU.mult).then_inc(s_vp.sem, 1)
            em("vector", v_bias); s_vp.bump()
            bsn1[i] = s_vp.n

    def p1_apply_tr(grp):
        for i in grp:
            wait("scalar", s_vp, bsn1[i])
            if i >= 2:
                wait("scalar", s_tn, lnscr_free[i - 2])
            def s_ap(s, i=i):
                s.activation(lnscr[i % 2][:, :], xbuf[i % 4][:, :], AF.Identity,
                             bias=ln1c("bias", i % 4),
                             scale=ln1c("rstd", i % 4)).then_inc(s_sp.sem, 1)
            em("scalar", s_ap); s_sp.bump()
            app_done[i] = s_sp.n
            wait("tensor", s_sp, app_done[i])
            direct = (i < 8 and JW == 512 and NJ >= 4)
            if direct:
                transpose_tile(
                    lnscr[i % 2], None,
                    dst_fn=lambda c, i=i: lnTc[i // 4][
                        :, c * JW + (i % 4) * 128: c * JW + (i % 4) * 128 + 128])
                lnTc_ready[i // 4] = s_vp.n
            else:
                gmin = 10 if (JW == 512 and NJ >= 4) else 2
                g = [(dch["lns%d" % (i % 2)], lns_guard[i % 2])] if i >= gmin else []
                transpose_tile(lnscr[i % 2], lnT_stage[i % 2], first_evac_guards=g)
            lnscr_free[i] = s_tn.n
            wait("sync", s_vp, s_vp.n)
            chs = dch["lns%d" % (i % 2)]
            def sy_st(sy, i=i, ch=chs, direct=direct):
                dst = lnT_hbm.ap()[:, :, i * 128:(i + 1) * 128].rearrange("c p w -> p c w")
                if direct:
                    src = lnTc[i // 4].rearrange("q (c w) -> q c w", w=JW)[
                        :, :, (i % 4) * 128:(i % 4) * 128 + 128]
                else:
                    src = lnT_stage[i % 2].rearrange("p (c w) -> p c w", w=128)
                sy.dma_start(dst, src).then_inc(ch.sem, 16)
            em("sync", sy_st); chs.bump()
            lns_guard[i % 2] = chs.n

    pend1 = None
    for g0 in range(0, NT, 4):
        grp = list(range(g0, min(g0 + 4, NT)))
        if pend1 is not None:
            p1_apply_tr(pend1)
        p1_loads(grp)
        p1_stats(grp)
        pend1 = grp
    p1_apply_tr(pend1)
    phase1_vp_end = s_vp.n

    wait("gpsimd", s_tn, lnscr_free[NT - 1])
    def g_vinit(g):
        vr = vbuf.rearrange("p (c w) -> p c w", w=VCH)
        for i, st in enumerate([
            lambda: g.memset(vbuf[:, :], 0.0),
            lambda: g.memset(vr[:, :, 128:129], 1.0),
            lambda: g.memset(vr[:, :, 264:265], 1.0),
        ]):
            st().then_inc(s_init.sem, 1)
            g.wait_ge(s_init.sem, init_done + i + 1)
    em("gpsimd", g_vinit)
    for _ in range(3):
        s_init.bump()
    vbuf_ready = s_init.n

    col0 = {"q1": 0, "k1": QH, "q2": 2 * QH, "k2": 3 * QH}
    wq_pair_n = []
    wfree_n = []
    lnc_guard = [0, 0]
    qk_evac_n = []
    v_evac_n = []
    s_ev_n = []
    s_cnt = [0]
    qk_cnt = [0]
    v_cnt = [0]
    pv_hist = [0]
    vd1_n = [0]
    dtr_slot = {i: 0 for i in range(NSL)}
    ln5_guard = {i: 0 for i in range(NSL)}
    st7_guard = {i: 0 for i in range(NSL)}
    dst_guard = [0, 0]
    pv_groups = [0]
    gslot = [0]
    pend = []
    pend_sc = [None]

    def flush_chunk(hbase, pj, dnn, slots):
        ntile = JW // 128
        k = tp_n[0]
        if k >= 2:
            wait("tensor", s_vp, tpe_done[k - 2])
        wait("tensor", s_vp, dnn[-1])
        nsl_c = len(slots)
        for hl in range(2):
            for tl in range(ntile):
                sl = slots[tl * 2 + hl]
                si = hl * ntile + tl
                def t_dtr(t, sl=sl, si=si, k=k, last=(hl == 1 and tl == ntile - 1)):
                    ins = t.transpose(pbf[k % 2][:, si * 128:(si + 1) * 128],
                                      dnscr[sl][:, :], ident[:, :])
                    if last:
                        ins.then_inc(s_tn.sem, 1)
                em("tensor", t_dtr)
                dtr_slot[sl] = s_tn.n + 1
        s_tn.bump()
        wait("vector", s_tn, s_tn.n)
        wait("vector", dch["dst0"], dst_guard[0])
        wait("vector", dch["dst1"], dst_guard[1])
        def v_dst(v, k=k, nsl_c=nsl_c):
            v.tensor_copy(dstage[:, 0:nsl_c * 128],
                          pbf[k % 2][:, 0:nsl_c * 128]).then_inc(s_vp.sem, 1)
        em("vector", v_dst); s_vp.bump()
        tpe_done.append(s_vp.n)
        tp_n[0] += 1
        wait("gpsimd", s_vp, s_vp.n)
        for hl in range(2):
            chd = dch["dst%d" % hl]
            def sy_dst(sy, hbase=hbase, hl=hl, pj=pj, ch=chd, ntile=ntile):
                sy.dma_start(dnT_hbm.ap()[hbase + hl, :, pj * JW:(pj + 1) * JW],
                             dstage[:, hl * ntile * 128:(hl + 1) * ntile * 128]
                             ).then_inc(ch.sem, 16)
            em("gpsimd", sy_dst); chd.bump()
            dst_guard[hl] = chd.n

    def emit_chain_tail(hbase, j, slots, st4v):
        nidx = len(slots)
        ln5 = []
        for ix in range(nidx):
            sl = slots[ix]
            wait("scalar", s_vp, st4v[ix])
            wait("scalar", s_vp, st7_guard[sl])
            def s5(s, sl=sl):
                s.activation(t2("rstd2", sl), t2("ms", sl), AF.Ln,
                             bias=tc("eps_ln"), scale=1.0 / DV).then_inc(s_sp.sem, 1)
            em("scalar", s5); s_sp.bump()
            ln5_guard[sl] = s_sp.n
            ln5.append(s_sp.n)
        ex6 = []
        for ix in range(nidx):
            sl = slots[ix]
            wait("scalar", s_sp, ln5[ix])
            def s6(s, sl=sl):
                s.activation(t2("rstd2", sl), t2("rstd2", sl), AF.Exp,
                             scale=-0.5).then_inc(s_sp.sem, 1)
            em("scalar", s6); s_sp.bump()
            ex6.append(s_sp.n)
        dnn = []
        for ix in range(nidx):
            sl = slots[ix]
            wait("vector", s_sp, ex6[ix])
            wait("vector", s_tn, dtr_slot[sl])
            def v7(v, sl=sl):
                v.tensor_scalar(dnscr[sl][:, :], asb[sl][:, 0:128],
                                t2("rstd2", sl), 1.0 - LAMBDA_INIT,
                                ALU.mult, ALU.mult).then_inc(s_vp.sem, 1)
            em("vector", v7); s_vp.bump()
            st7_guard[sl] = s_vp.n
            dnn.append(s_vp.n)
        pend.append((hbase, j, dnn, slots))

    def emit_wq(p):
        chw = dch["wq%d" % (p % 2)]
        if p >= 2:
            wait("gpsimd", s_tn, wfree_n[p - 2])
        def g_wq(g, p=p, ch=chw):
            for jt, nm in enumerate(("q1", "k1", "q2", "k2")):
                src = wqkv.ap()[:, col0[nm] + p * 128: col0[nm] + (p + 1) * 128]
                src = src.rearrange("(c q) j -> q c j", q=128)
                dst = wqk_st[p % 2].rearrange("q (c j) -> q c j", j=512)
                g.dma_start(dst[:, :, jt * 128:(jt + 1) * 128], src).then_inc(ch.sem, 16)
            srcv = wqkv.ap()[:, 4 * QH + p * 256: 4 * QH + (p + 1) * 256]
            srcv = srcv.rearrange("(c q) j -> q c j", q=128)
            dstv = wv_st[p % 2].rearrange("q (c j) -> q c j", j=256)
            g.dma_start(dstv, srcv).then_inc(ch.sem, 16)
        em("gpsimd", g_wq)
        for _ in range(5):
            chw.bump()
        wq_pair_n.append((chw, chw.n))

    emit_wq(0)
    emit_wq(1)
    for p in range(8):
        wait("tensor", *wq_pair_n[p])
        if p == 0:
            wait("tensor", s_vp, phase1_vp_end)
        else:
            wait("tensor", s_vp, qk_evac_n[-1])
        for tcx in range(NJ):
            glob_tc = p * NJ + tcx
            direct = (glob_tc < 2 and JW == 512 and NJ >= 4)
            chl = dch["lnc%d" % (glob_tc % 2)]
            if not direct:
                if glob_tc >= 2:
                    wait("sync", s_tn, lnc_guard[glob_tc % 2])
                wait("sync", dch["lns0"], dch["lns0"].n)
                wait("sync", dch["lns1"], dch["lns1"].n)
                def sy_ln(sy, tcx=tcx, glob_tc=glob_tc, ch=chl):
                    src = lnT_hbm.ap()[:, :, tcx * JW:(tcx + 1) * JW].rearrange("c p w -> p c w")
                    dst = lnTc[glob_tc % 2].rearrange("q (c w) -> q c w", w=JW)
                    sy.dma_start(dst, src).then_inc(ch.sem, 16)
                em("sync", sy_ln); chl.bump()
                wait("tensor", chl, chl.n)
            else:
                wait("tensor", s_vp, lnTc_ready[glob_tc])
            for jt in range(4):
                k = qk_cnt[0]
                bank = bank512[k % 2]
                if k >= 2:
                    wait("tensor", s_vp, qk_evac_n[k - 2])
                for c in range(NC):
                    def t_qk(t, jt=jt, c=c, gp=glob_tc % 2, bank=bank, p=p):
                        ins = t.matmul(bank[:, 0:JW],
                                       wqk_st[p % 2][:, c * 512 + jt * 128: c * 512 + jt * 128 + 128],
                                       lnTc[gp][:, c * JW:(c + 1) * JW],
                                       start=(c == 0), stop=(c == NC - 1))
                        if c == NC - 1:
                            ins.then_inc(s_tn.sem, 1)
                    em("tensor", t_qk)
                s_tn.bump()
                wait("vector", s_tn, s_tn.n)
                sc = 0.125 if jt in (0, 2) else 1.0
                def v_qke(v, jt=jt, tcx=tcx, bank=bank, sc=sc):
                    dst = qkT[:, jt * T + tcx * JW: jt * T + tcx * JW + JW]
                    if sc == 1.0:
                        v.tensor_copy(dst, bank[:, 0:JW]).then_inc(s_vp.sem, 1)
                    else:
                        v.tensor_single_scalar(dst, bank[:, 0:JW], sc,
                                               ALU.mult).then_inc(s_vp.sem, 1)
                em("vector", v_qke); s_vp.bump()
                qk_evac_n.append(s_vp.n)
                qk_cnt[0] += 1
            for tl in range(JW // 128):
                ti = tcx * (JW // 128) + tl
                k = v_cnt[0]
                bank = bank512[2 + k % 2]
                if k >= 2:
                    wait("tensor", s_vp, v_evac_n[k - 2])
                for c in range(NC):
                    def t_v(t, c=c, tl=tl, gp=glob_tc % 2, bank=bank, p=p):
                        ins = t.matmul(bank[:, 0:256],
                                       lnTc[gp][:, c * JW + tl * 128: c * JW + tl * 128 + 128],
                                       wv_st[p % 2][:, c * 256:(c + 1) * 256],
                                       start=(c == 0), stop=(c == NC - 1))
                        if c == NC - 1:
                            ins.then_inc(s_tn.sem, 1)
                    em("tensor", t_v)
                s_tn.bump()
                wait("vector", s_tn, s_tn.n)
                if p == 0 and v_cnt[0] == 0:
                    wait("vector", s_init, vbuf_ready)
                def v_ev(v, ti=ti, bank=bank):
                    v.tensor_copy(vbuf[:, ti * VCH: ti * VCH + 128],
                                  bank[:, 0:128]).then_inc(s_vp.sem, 1)
                    v.tensor_copy(vbuf[:, ti * VCH + 136: ti * VCH + 264],
                                  bank[:, 128:256]).then_inc(s_vp.sem, 1)
                em("vector", v_ev); s_vp.bump(); s_vp.bump()
                v_evac_n.append(s_vp.n)
                v_cnt[0] += 1
            lnc_guard[glob_tc % 2] = s_tn.n
        wfree_n.append(s_tn.n)
        if p + 2 < 8:
            emit_wq(p + 2)

        wait("tensor", s_vp, qk_evac_n[-1])
        wait("tensor", s_vp, v_evac_n[-1])
        if p == 0:
            wait("tensor", s_init, vbuf_ready)
            wait("scalar", s_vp, red_done[NT - 1])
        for j in range(NJ):
            cmax = min(4 * j + 3, NT - 1)
            gates = [None, None]
            for a in range(2):
                qs = 0 if a == 0 else 2
                ks = 1 if a == 0 else 3
                pv_guard = pv_hist[0]
                for cc in range(cmax + 1):
                    k = s_cnt[0]
                    bankp = spair[k % 2]
                    if k >= 2:
                        wait("tensor", s_sp, s_ev_n[k - 2])
                    for hl in range(2):
                        def t_s(t, cc=cc, j=j, qs=qs, ks=ks, hl=hl,
                                bankp=bankp, last=(hl == 1)):
                            qoff = 64 * hl
                            ins = t.matmul(bankp[:, hl * JW: (hl + 1) * JW],
                                           qkT[qoff:qoff + 64, ks * T + cc * 128: ks * T + cc * 128 + 128],
                                           qkT[qoff:qoff + 64, qs * T + j * JW: qs * T + (j + 1) * JW],
                                           start=True, stop=True,
                                           tile_position=(64 * hl, 0))
                            if last:
                                ins.then_inc(s_tn.sem, 1)
                        em("tensor", t_s)
                    s_tn.bump()
                    wait("scalar", s_tn, s_tn.n)
                    if pv_guard:
                        wait("scalar", s_tn, pv_guard)
                        pv_guard = 0
                    def s_exp(s, a=a, cc=cc, bankp=bankp):
                        s.activation(pT[a][:, cc * 2 * JW: (cc + 1) * 2 * JW],
                                     bankp[:, 0:2 * JW],
                                     AF.Exp).then_inc(s_sp.sem, 1)
                    em("scalar", s_exp); s_sp.bump()
                    s_ev_n.append(s_sp.n)
                    s_cnt[0] += 1
                    if 4 * j <= cc and (cc - 4 * j) * 128 < JW:
                        s0 = cc - 4 * j
                        wait("vector", s_sp, s_sp.n)
                        for hl in range(2):
                            def v_mask(v, a=a, cc=cc, s0=s0, hl=hl):
                                sl_ = pT[a][:, cc * 2 * JW + hl * JW + s0 * 128:
                                            cc * 2 * JW + hl * JW + s0 * 128 + 128]
                                v.tensor_mul(sl_, sl_, mask01[:, :]).then_inc(s_vp.sem, 1)
                            em("vector", v_mask); s_vp.bump()
                gates[a] = (s_sp.n, s_vp.n)
            ntile = JW // 128
            copy_n = []
            slots = []
            for tl in range(ntile):
                gidx = j * ntile + tl
                for hl in range(2):
                    voff = 136 * hl
                    apar = pv_groups[0] % 2
                    base = apar * 512
                    if len(vd1_n) >= 2:
                        wait("tensor", s_vp, vd1_n[-2])
                    for a in range(2):
                        sev_after, pm_after = gates[a]
                        wait("tensor", s_sp, sev_after)
                        wait("tensor", s_vp, pm_after)
                        for cc in range(gidx + 1):
                            def t_pv(t, a=a, cc=cc, tl=tl, hl=hl, voff=voff,
                                     base=base, gidx=gidx,
                                     last=(a == 1 and cc == gidx)):
                                ins = t.matmul(pCD[:, base + a * 132: base + a * 132 + 129],
                                               pT[a][:, cc * 2 * JW + hl * JW + tl * 128:
                                                      cc * 2 * JW + hl * JW + tl * 128 + 128],
                                               vbuf[:, cc * VCH + voff: cc * VCH + voff + 129],
                                               start=(cc == 0), stop=(cc == gidx))
                                if last:
                                    ins.then_inc(s_tn.sem, 1)
                            em("tensor", t_pv)
                    s_tn.bump()
                    pv_groups[0] += 1
                    pv_hist[0] = s_tn.n
                    wait("vector", s_tn, s_tn.n)
                    sl = gslot[0] % NSL
                    gslot[0] += 1
                    slots.append(sl)
                    def v_cp(v, sl=sl, base=base):
                        v.tensor_copy(
                            asb[sl].rearrange("p (t c) -> p t c", c=132)[:, :, 0:129],
                            pCD[:, base:base + 264].rearrange("p (t c) -> p t c", c=132)[:, :, 0:129]
                        ).then_inc(s_vp.sem, 1)
                    em("vector", v_cp); s_vp.bump()
                    vd1_n.append(s_vp.n)
                    copy_n.append(s_vp.n)

            if len(pend) >= 2:
                flush_chunk(*pend.pop(0))

            nidx = len(slots)
            st_n = [list(copy_n)]
            cur = []
            for ix in range(nidx):
                sl = slots[ix]
                wait("vector", s_vp, st_n[0][ix])
                def v0(v, sl=sl):
                    v.reciprocal(tiny[:, 64 + 4 * sl: 64 + 4 * sl + 2],
                                 asb[sl].rearrange("p (t c) -> p t c", c=132)[:, :, 128:129]
                                 ).then_inc(s_vp.sem, 1)
                em("vector", v0); s_vp.bump()
                cur.append(s_vp.n)
            st_n.append(cur)
            cur = []
            for ix in range(nidx):
                sl = slots[ix]
                wait("vector", s_vp, st_n[1][ix])
                def v1(v, sl=sl):
                    v.tensor_scalar(t2("r2", sl), t2("r2", sl), tc("lam"), -1.0,
                                    ALU.mult, ALU.mult).then_inc(s_vp.sem, 1)
                em("vector", v1); s_vp.bump()
                cur.append(s_vp.n)
            st_n.append(cur)
            cur = []
            for ix in range(nidx):
                sl = slots[ix]
                wait("vector", s_vp, st_n[1][ix])
                def v2(v, sl=sl):
                    v.tensor_scalar(asb[sl][:, 0:128], asb[sl][:, 0:128],
                                    t2("r1", sl), None, ALU.mult).then_inc(s_vp.sem, 1)
                em("vector", v2); s_vp.bump()
                cur.append(s_vp.n)
            st_n.append(cur)
            cur = []
            for ix in range(nidx):
                sl = slots[ix]
                wait("vector", s_vp, max(st_n[2][ix], st_n[3][ix]))
                def v3(v, sl=sl):
                    v.scalar_tensor_tensor(asb[sl][:, 0:128], asb[sl][:, 132:260],
                                           t2("r2", sl), asb[sl][:, 0:128],
                                           ALU.mult, ALU.add).then_inc(s_vp.sem, 1)
                em("vector", v3); s_vp.bump()
                cur.append(s_vp.n)
            st_n.append(cur)
            cur = []
            for ix in range(nidx):
                sl = slots[ix]
                wait("vector", s_vp, st_n[4][ix])
                wait("vector", s_sp, ln5_guard[sl])
                def v4(v, sl=sl):
                    v.scalar_tensor_tensor(asb[sl][:, 132:260], asb[sl][:, 0:128],
                                           1.0, asb[sl][:, 0:128], ALU.mult,
                                           ALU.mult,
                                           accum_out=t2("ms", sl)).then_inc(s_vp.sem, 1)
                em("vector", v4); s_vp.bump()
                cur.append(s_vp.n)
            st_n.append(cur)
            if pend_sc[0] is not None:
                emit_chain_tail(*pend_sc[0])
            pend_sc[0] = (2 * p, j, slots, list(st_n[5]))

    while len(pend) >= 2:
        flush_chunk(*pend.pop(0))
    if pend_sc[0] is not None:
        emit_chain_tail(*pend_sc[0])
        pend_sc[0] = None
    while pend:
        flush_chunk(*pend.pop(0))
    phase2_tn_end = s_tn.n
    phase2_vp_end = s_vp.n
    phase2_sp_end = s_sp.n

    wait("gpsimd", s_tn, wfree_n[-1])
    def g_wp(g):
        src = wproj.ap().rearrange("(c q) (dc d) -> q dc c d", q=128, d=512)
        dst = wproj_sb.rearrange("q (dc c d) -> q dc c d", dc=4, d=512)
        g.dma_start(dst, src).then_inc(dch["wp"].sem, 16)
    em("gpsimd", g_wp)
    dch["wp"].bump()
    wp_done = dch["wp"].n

    projev_n = []
    proj_last_tn = 0
    dT_guard = [0, 0]
    dT_n = None
    x1s_guard = [0, 0]
    hts_guard = [0, 0]
    rtv_n = []
    h_sp_done = {}
    h_vp_done = {}
    bias3_n = {}
    htr3_done = {}

    def apply3(pj):
        par = pj % 2
        wait("scalar", s_vp, bias3_n[pj])
        def s3_ap(s, par=par, xpar=pj % 4):
            s.activation(xbuf[xpar][:, :], x1t[par][:, :], AF.Identity,
                         bias=tc("bias", par),
                         scale=tc("rstd", par)).then_inc(s_sp.sem, 1)
        em("scalar", s3_ap); s_sp.bump()
        h_sp_done[pj] = s_sp.n
        wait("vector", s_sp, s_sp.n)
        wait("vector", dch["pre"], pre_done)
        if pj >= 2:
            wait("vector", s_tn, htr3_done[pj - 2])
        def v3_h(v, par=par, xpar=pj % 4):
            v.tensor_mul(hbuf[par][:, :], xbuf[xpar][:, :],
                         rmsw[:, :]).then_inc(s_vp.sem, 1)
        em("vector", v3_h); s_vp.bump()
        h_vp_done[pj] = s_vp.n

    def flush3(pj):
        wait("tensor", s_vp, h_vp_done[pj])
        g = [(dch["hts%d" % (pj % 2)], hts_guard[pj % 2])] if pj >= 2 else []
        transpose_tile(hbuf[pj % 2], hTstage[pj % 2], first_evac_guards=g)
        htr3_done[pj] = s_tn.n
        hT_ready = s_vp.n
        wait("sync", s_vp, hT_ready)
        chh = dch["hts%d" % (pj % 2)]
        def sy_hT(sy, pj=pj, ch=chh):
            dst = hT_hbm.ap()[:, :, pj * 128:(pj + 1) * 128].rearrange("c p w -> p c w")
            src = hTstage[pj % 2].rearrange("p (c w) -> p c w", w=128)
            sy.dma_start(dst, src).then_inc(ch.sem, 16)
        em("sync", sy_hT); chh.bump()
        hts_guard[pj % 2] = chh.n
        wait("tensor", s_vp, hT_ready)
        wait("tensor", dch["rw"], routw_done)
        if pj >= 1:
            wait("tensor", s_vp, rtv_n[pj - 1])
        for c in range(NC):
            def t_rt(t, c=c, pj=pj):
                ins = t.matmul(pCD[:, 0:8], hTstage[pj % 2][:, c * 128:(c + 1) * 128],
                               routw[:, c * 8:(c + 1) * 8],
                               start=(c == 0), stop=(c == NC - 1))
                if c == NC - 1:
                    ins.then_inc(s_tn.sem, 1)
            em("tensor", t_rt)
        s_tn.bump()
        wait("vector", s_tn, s_tn.n)
        vchain([
            lambda v: v.reduce_max(tc("m1"), pCD[:, 0:8], axis=AX.X).then_inc(s_vp.sem, 1),
            lambda v: v.tensor_scalar(i8[:, 0:8], pCD[:, 0:8], tc("m1"), None,
                                      ALU.is_ge).then_inc(s_vp.sem, 1),
            lambda v: v.scalar_tensor_tensor(i8[:, 8:16], i8[:, 0:8], -1e30,
                                             pCD[:, 0:8], ALU.mult,
                                             ALU.add).then_inc(s_vp.sem, 1),
            lambda v: v.reduce_max(tc("m2"), i8[:, 8:16], axis=AX.X).then_inc(s_vp.sem, 1),
            lambda v: v.tensor_sub(tc("dlt"), tc("m1"), tc("m2")).then_inc(s_vp.sem, 1),
        ])
        def v_i2(v):
            v.tensor_scalar(i8[:, 16:24], i8[:, 8:16], tc("m2"), None,
                            ALU.is_ge).then_inc(s_vp.sem, 1)
        em("vector", v_i2); s_vp.bump()
        wait("scalar", s_vp, s_vp.n)
        def s_w1(s):
            s.activation(tc("w1"), tc("dlt"), AF.Exp,
                         scale=-1.0).then_inc(s_sp.sem, 1)
        em("scalar", s_w1); s_sp.bump()
        wait("vector", s_sp, s_sp.n)
        vchain([
            lambda v: v.tensor_single_scalar(tc("w1"), tc("w1"), 1.0,
                                             ALU.add).then_inc(s_vp.sem, 1),
            lambda v: v.reciprocal(tc("w1"), tc("w1")).then_inc(s_vp.sem, 1),
            lambda v: v.tensor_scalar(tc("w2"), tc("w1"), -1.0, 1.0, ALU.mult,
                                      ALU.add).then_inc(s_vp.sem, 1),
            lambda v: v.tensor_scalar(i8[:, 24:32], i8[:, 0:8], tc("w1"), None,
                                      ALU.mult).then_inc(s_vp.sem, 1),
            lambda v, pj=pj: v.scalar_tensor_tensor(s_route[:, pj * 8:(pj + 1) * 8],
                                                    i8[:, 16:24], tc("w2"),
                                                    i8[:, 24:32], ALU.mult,
                                                    ALU.add).then_inc(s_vp.sem, 1),
        ])
        rtv_n.append(s_vp.n)

    for j in range(NT):
        cx = j // (JW // 128)
        tl = j % (JW // 128)
        if tl == 0:
            chD = dch["dT%d" % (cx % 2)]
            wait("sync", dch["dst0"], dch["dst0"].n)
            wait("sync", dch["dst1"], dch["dst1"].n)
            if cx >= 2:
                wait("sync", s_tn, dT_guard[cx % 2])
            def sy_dT(sy, cx=cx, ch=chD):
                src = dnT_hbm.ap()[:, :, cx * JW:(cx + 1) * JW].rearrange("c p w -> p c w")
                dst = dTbuf[cx % 2].rearrange("q (c w) -> q c w", w=JW)
                sy.dma_start(dst, src).then_inc(ch.sem, 16)
            em("sync", sy_dT); chD.bump()
            dT_n = (chD, chD.n)
        chx = dch["x3%d" % (j % 4)]
        if j >= 4:
            wait("sync", s_vp, projev_n[4 * (j - 4) + 3])
            wait("sync", s_vp, h_vp_done[j - 4])
            wait("sync", s_sp, h_sp_done[j - 4])
        def sy_x(sy, j=j, ch=chx):
            sy.dma_start(xbuf[j % 4][:, :],
                         x_in.ap()[j * 128:(j + 1) * 128, :]).then_inc(ch.sem, 16)
        em("sync", sy_x); chx.bump()
        x3j = (chx, chx.n)
        wait("tensor", dT_n[0], dT_n[1])
        if j == 0:
            wait("tensor", s_vp, phase2_vp_end)
            wait("tensor", s_sp, phase2_sp_end)
        for dcx in range(4):
            kk = 4 * j + dcx
            wait("tensor", dch["wp"], wp_done)
            bank = bank512[kk % 2]
            if kk >= 2:
                wait("tensor", s_vp, projev_n[kk - 2])
            for c in range(NC):
                def t_pj(t, c=c, dcx=dcx, tl=tl, cx=cx, bank=bank):
                    ins = t.matmul(bank[:, 0:512],
                                   dTbuf[cx % 2][:, c * JW + tl * 128: c * JW + tl * 128 + 128],
                                   wproj_sb[:, dcx * NC * 512 + c * 512: dcx * NC * 512 + (c + 1) * 512],
                                   start=(c == 0), stop=(c == NC - 1))
                    if c == NC - 1:
                        ins.then_inc(s_tn.sem, 1)
                em("tensor", t_pj)
            s_tn.bump()
            wait("vector", s_tn, s_tn.n)
            wait("vector", *x3j)
            if j >= 2 and dcx == 0:
                wait("vector", dch["x1s%d" % (j % 2)], x1s_guard[j % 2])
                wait("vector", s_sp, h_sp_done[j - 2])
            def v_pe(v, j=j, dcx=dcx, bank=bank):
                v.tensor_add(x1t[j % 2][:, dcx * 512:(dcx + 1) * 512], bank[:, 0:512],
                             xbuf[j % 4][:, dcx * 512:(dcx + 1) * 512]).then_inc(s_vp.sem, 1)
            em("vector", v_pe); s_vp.bump()
            projev_n.append(s_vp.n)
        dT_guard[cx % 2] = s_tn.n
        wait("sync", s_vp, projev_n[-1])
        chs = dch["x1s%d" % (j % 2)]
        def sy_x1(sy, j=j, ch=chs):
            sy.dma_start(x1_hbm.ap()[j * 128:(j + 1) * 128, :],
                         x1t[j % 2][:, :]).then_inc(ch.sem, 16)
        em("sync", sy_x1); chs.bump()
        x1s_guard[j % 2] = chs.n
        if j >= 2:
            flush3(j - 2)
        if j >= 1:
            apply3(j - 1)
        wait("scalar", s_vp, projev_n[-1])
        wait("vector", s_vp, projev_n[-1])
        par = j % 2
        def s3_sq(s, j=j, par=par):
            s.activation(sqjunk[:, :], x1t[par][:, :], AF.Square,
                         accum_out=tc("sumsq", par)).then_inc(s_sp.sem, 1)
        em("scalar", s3_sq); s_sp.bump()
        sq3_n = s_sp.n
        def v3_red(v, j=j, par=par):
            v.reduce_sum(tc("sum", par), x1t[par][:, :], axis=AX.X).then_inc(s_vp.sem, 1)
        em("vector", v3_red); s_vp.bump()
        wait("vector", s_vp, s_vp.n)
        def v3_msq(v, par=par):
            v.tensor_scalar(tc("varD", par), tc("sum", par), tc("sum", par),
                            1.0 / D, ALU.mult, ALU.mult).then_inc(s_vp.sem, 1)
        em("vector", v3_msq); s_vp.bump()
        wait("vector", s_vp, s_vp.n)
        wait("vector", s_sp, sq3_n)
        def v3_var(v, par=par):
            v.tensor_sub(tc("varD", par), tc("sumsq", par),
                         tc("varD", par)).then_inc(s_vp.sem, 1)
        em("vector", v3_var); s_vp.bump()
        wait("scalar", s_vp, s_vp.n)
        def s3_ln(s, par=par):
            s.activation(tc("rstd", par), tc("varD", par), AF.Ln,
                         bias=tc("eps11"),
                         scale=(1.0 + EPS_RMS_FFN) / D).then_inc(s_sp.sem, 1)
        em("scalar", s3_ln); s_sp.bump()
        wait("scalar", s_sp, s_sp.n)
        def s3_ex(s, par=par):
            s.activation(tc("rstd", par), tc("rstd", par), AF.Exp,
                         scale=-0.5).then_inc(s_sp.sem, 1)
        em("scalar", s3_ex); s_sp.bump()
        wait("vector", s_sp, s_sp.n)
        def v3_bias(v, par=par):
            v.tensor_scalar(tc("bias", par), tc("sum", par), tc("rstd", par),
                            -1.0 / D, ALU.mult, ALU.mult).then_inc(s_vp.sem, 1)
        em("vector", v3_bias); s_vp.bump()
        bias3_n[j] = s_vp.n
    apply3(NT - 1)
    flush3(NT - 2)
    flush3(NT - 1)

    phase3_tn_end = s_tn.n
    phase3_vp_end = s_vp.n
    phase3_sp_end = s_sp.n
    if dump_sroute is not None:
        wait("sync", s_vp, s_vp.n)
        def sy_sr(sy):
            sy.dma_start(dump_sroute.ap(), s_route[:, :]).then_inc(dch["pre"].sem, 16)
        em("sync", sy_sr); dch["pre"].bump()

    g_n = []
    u_n = []
    sil_n = []
    amul_n = []
    dn_n = []
    dnev_n = []
    yst_n = [0, 0]
    u_half_end = 0

    def wg_src(e, ft, which):
        if e < NE:
            b = wg_in if which == 0 else wu_in
            return b.ap()[e, ft].rearrange("c p f -> p c f")
        b = swg_in if which == 0 else swu_in
        return b.ap()[ft].rearrange("c p f -> p c f")

    hth2_n = [0]
    for hf in range(2):
        if hf == 0:
            chh = dch["hth"]
            wait("sync", dch["hts0"], dch["hts0"].n)
            wait("sync", dch["hts1"], dch["hts1"].n)
            def sy_hth(sy, ch=chh):
                src = hT_hbm.ap()[:, :, 0:HALF].rearrange("c p w -> p c w")
                dst = hT_half.rearrange("q (c w) -> q c w", w=HALF)
                sy.dma_start(dst, src).then_inc(ch.sem, 16)
            em("sync", sy_hth); chh.bump()
            hth_n = chh.n
        else:
            hth_n = hth2_n[0]
        wait("sync", dch["x1s0"], dch["x1s0"].n)
        wait("sync", dch["x1s1"], dch["x1s1"].n)
        if hf == 0:
            wait("sync", s_tn, phase3_tn_end)
        else:
            wait("sync", dch["yst"], yst_n[0])
        def sy_yi(sy, hf=hf):
            src = x1_hbm.ap()[hf * HALF:(hf + 1) * HALF, :]
            src = src.rearrange("(tt p) d -> p tt d", p=128)
            dst = ybuf.rearrange("p (tt d) -> p tt d", d=D)
            sy.dma_start(dst, src).then_inc(dch["yini"].sem, 16)
        em("sync", sy_yi); dch["yini"].bump()
        yini_n = dch["yini"].n

        for e in range(NE + 1):
            eg = hf * (NE + 1) + e
            if eg == 0:
                wait("gpsimd", s_tn, phase3_tn_end)
                wait("gpsimd", s_sp, phase3_sp_end)
                wait("gpsimd", s_vp, phase3_vp_end)
                wait("gpsimd", dch["x1s0"], dch["x1s0"].n)
                wait("gpsimd", dch["x1s1"], dch["x1s1"].n)
                wait("gpsimd", dch["dT0"], dch["dT0"].n)
                wait("gpsimd", dch["dT1"], dch["dT1"].n)
                wait("gpsimd", dch["hts0"], dch["hts0"].n)
                wait("gpsimd", dch["hts1"], dch["hts1"].n)
            wd_emitted = [False]

            def emit_wd(e=e, eg=eg):
                if eg >= 1:
                    wait("gpsimd", s_tn, dn_n[-1])
                def g_wd(g, e=e):
                    if e < NE:
                        src = wd_in.ap()[e].rearrange("(fc p) d -> p fc d", p=128)
                    else:
                        src = swd_in.ap().rearrange("(fc p) d -> p fc d", p=128)
                    g.dma_start(wd_sb.rearrange("p (fc d) -> p fc d", d=D),
                                src).then_inc(dch["wd"].sem, 16)
                em("gpsimd", g_wd); dch["wd"].bump()
            for ft in range(NF):
                blk = eg * NF + ft
                chb = dch["wg%d" % (blk % 2)]
                if blk >= 2:
                    wait("gpsimd", s_tn, u_n[(blk - 2 + 1) * n_tch - 1])
                def g_blk(g, e=e, ft=ft, blk=blk, ch=chb):
                    g.dma_start(wgu_st[0][blk % 2].rearrange("p (c f) -> p c f", f=128),
                                wg_src(e, ft, 0)).then_inc(ch.sem, 16)
                    g.dma_start(wgu_st[1][blk % 2].rearrange("p (c f) -> p c f", f=128),
                                wg_src(e, ft, 1)).then_inc(ch.sem, 16)
                em("gpsimd", g_blk)
                chb.bump(); chb.bump()
                blk_n = chb.n
                if ft == 1 and not wd_emitted[0]:
                    emit_wd()
                    wd_emitted[0] = True
                wait("tensor", chb, blk_n)
                wait("tensor", dch["hth"], hth_n)
                for tcx in range(n_tch):
                    kk = blk * n_tch + tcx
                    bg = bank512[kk % 2]
                    bu = bank512[2 + kk % 2]
                    if len(sil_n) >= 2:
                        wait("tensor", s_sp, sil_n[-2])
                    if len(amul_n) >= 2:
                        wait("tensor", s_vp, amul_n[-2])
                    if kk == 0:
                        wait("tensor", s_vp, phase3_vp_end)
                        wait("tensor", s_sp, phase3_sp_end)
                    for c in range(NC):
                        def t_g(t, c=c, tcx=tcx, blk=blk, bg=bg):
                            ins = t.matmul(bg[:, 0:TCW],
                                           wgu_st[0][blk % 2][:, c * 128:(c + 1) * 128],
                                           hT_half[:, c * HALF + tcx * TCW: c * HALF + tcx * TCW + TCW],
                                           start=(c == 0), stop=(c == NC - 1))
                            if c == NC - 1:
                                ins.then_inc(s_tn.sem, 1)
                        em("tensor", t_g)
                    s_tn.bump()
                    g_n.append(s_tn.n)
                    for c in range(NC):
                        def t_u(t, c=c, tcx=tcx, blk=blk, bu=bu):
                            ins = t.matmul(bu[:, 0:TCW],
                                           wgu_st[1][blk % 2][:, c * 128:(c + 1) * 128],
                                           hT_half[:, c * HALF + tcx * TCW: c * HALF + tcx * TCW + TCW],
                                           start=(c == 0), stop=(c == NC - 1))
                            if c == NC - 1:
                                ins.then_inc(s_tn.sem, 1)
                        em("tensor", t_u)
                    s_tn.bump()
                    u_n.append(s_tn.n)
                    wait("scalar", s_tn, g_n[-1])
                    spar = len(sil_n) % 2
                    if not sil_n:
                        wait("scalar", dch["hts0"], dch["hts0"].n)
                        wait("scalar", dch["hts1"], dch["hts1"].n)
                    if len(sil_n) >= 2:
                        wait("scalar", s_vp, amul_n[-2])
                    def s_sil(s, spar=spar, bg=bg):
                        s.activation(sgscr[spar][:, :], bg[:, 0:TCW],
                                     AF.Silu).then_inc(s_sp.sem, 1)
                    em("scalar", s_sil); s_sp.bump()
                    sil_n.append(s_sp.n)
                    wait("vector", s_sp, sil_n[-1])
                    wait("vector", s_tn, u_n[-1])
                    if eg >= 1 and ft == 0 and tcx == 0:
                        wait("vector", s_tn, dn_n[-1])
                    if eg == 0 and ft == 0 and tcx == 0:
                        wait("vector", dch["hts0"], dch["hts0"].n)
                        wait("vector", dch["hts1"], dch["hts1"].n)
                        wait("vector", s_tn, phase3_tn_end)
                        wait("vector", dch["dT0"], dch["dT0"].n)
                        wait("vector", dch["dT1"], dch["dT1"].n)
                    def v_am(v, ft=ft, tcx=tcx, spar=spar, bu=bu):
                        v.tensor_mul(abuf[:, ft * HALF + tcx * TCW: ft * HALF + tcx * TCW + TCW],
                                     sgscr[spar][:, :], bu[:, 0:TCW]).then_inc(s_vp.sem, 1)
                    em("vector", v_am); s_vp.bump()
                    amul_n.append(s_vp.n)
            a_ready = s_vp.n
            wait("tensor", dch["wd"], dch["wd"].n)
            wait("tensor", s_vp, a_ready)
            for tl in range(NTH):
                for dcx in range(4):
                    kk = len(dn_n)
                    bank = bank512[4 + kk % 2]
                    if kk >= 2:
                        wait("tensor", s_vp, dnev_n[kk - 2])
                    if kk == 0:
                        wait("tensor", s_vp, phase3_vp_end)
                    for fc in range(NF):
                        def t_dn(t, fc=fc, tl=tl, dcx=dcx, eg=eg, bank=bank):
                            ins = t.matmul(bank[:, 0:512],
                                           abuf[:, fc * HALF + tl * 128: fc * HALF + tl * 128 + 128],
                                           wd_sb[:, fc * D + dcx * 512: fc * D + (dcx + 1) * 512],
                                           start=(fc == 0), stop=(fc == NF - 1))
                            if fc == NF - 1:
                                ins.then_inc(s_tn.sem, 1)
                        em("tensor", t_dn)
                    s_tn.bump()
                    dn_n.append(s_tn.n)
                    wait("vector", s_tn, dn_n[-1])
                    if e == 0 and tl == 0 and dcx == 0:
                        wait("vector", dch["yini"], yini_n)
                    def v_dne(v, e=e, tl=tl, dcx=dcx, hf=hf, bank=bank):
                        ysl = ybuf[:, tl * D + dcx * 512: tl * D + (dcx + 1) * 512]
                        if e < NE:
                            cix = (hf * NTH + tl) * 8 + e
                            v.scalar_tensor_tensor(ysl, bank[:, 0:512],
                                                   s_route[:, cix:cix + 1], ysl,
                                                   ALU.mult, ALU.add).then_inc(s_vp.sem, 1)
                        else:
                            v.tensor_add(ysl, ysl, bank[:, 0:512]).then_inc(s_vp.sem, 1)
                    em("vector", v_dne); s_vp.bump()
                    dnev_n.append(s_vp.n)
            if hf == 0 and e == NE:
                u_half_end = u_n[-1]
        if hf == 0:
            chh2 = dch["hth"]
            wait("sync", s_tn, u_n[-1])
            def sy_hth2(sy, ch=chh2):
                src_ = hT_hbm.ap()[:, :, HALF:2 * HALF].rearrange("c p w -> p c w")
                dst_ = hT_half.rearrange("q (c w) -> q c w", w=HALF)
                sy.dma_start(dst_, src_).then_inc(ch.sem, 16)
            em("sync", sy_hth2); chh2.bump()
            hth2_n[0] = chh2.n
        wait("sync", s_vp, dnev_n[-1])
        def sy_yst(sy, hf=hf):
            src = ybuf.rearrange("p (tt d) -> p tt d", d=D)
            dst = out_t.ap()[hf * HALF:(hf + 1) * HALF, :]
            dst = dst.rearrange("(tt p) d -> p tt d", p=128)
            sy.dma_start(dst, src).then_inc(dch["yst"].sem, 16)
        em("sync", sy_yst); dch["yst"].bump()
        yst_n[hf] = dch["yst"].n

    for nm, ch in dch.items():
        if nm != "yst" and ch.n > 0:
            wait("sync", ch, ch.n)
    wait("sync", dch["yst"], dch["yst"].n)

    with nc.Block() as block:
        @block.tensor
        def _(e):
            for f in ops["tensor"]:
                f(e)

        @block.vector
        def _(e):
            for f in ops["vector"]:
                f(e)

        @block.scalar
        def _(e):
            for f in ops["scalar"]:
                f(e)

        @block.gpsimd
        def _(e):
            for f in ops["gpsimd"]:
                f(e)

        @block.sync
        def _(e):
            for f in ops["sync"]:
                f(e)

    return nc


def _prep_host(inputs):
    f32 = lambda a: np.ascontiguousarray(np.asarray(a, dtype=np.float32))
    x = f32(inputs["L0_attn_x"])
    rmsw = np.ascontiguousarray(
        np.broadcast_to(f32(inputs["L0_ffn_rms_weight"])[None, :], (128, D)))
    lam = np.ascontiguousarray(np.concatenate([
        np.broadcast_to(f32(inputs[k])[None, :], (128, DH))
        for k in ("L0_attn_lambda_q1", "L0_attn_lambda_k1",
                  "L0_attn_lambda_q2", "L0_attn_lambda_k2")], axis=1))
    tile_gu = lambda w: np.ascontiguousarray(
        w.reshape(-1, NC, 128, NF, 128).transpose(0, 3, 1, 2, 4))
    wg_t = tile_gu(f32(inputs["L0_ffn_w_gate"]))
    wu_t = tile_gu(f32(inputs["L0_ffn_w_up"]))
    swg_t = np.ascontiguousarray(
        f32(inputs["L0_ffn_sw_gate"]).reshape(NC, 128, NF, 128).transpose(2, 0, 1, 3))
    swu_t = np.ascontiguousarray(
        f32(inputs["L0_ffn_sw_up"]).reshape(NC, 128, NF, 128).transpose(2, 0, 1, 3))
    common = dict(wqkv=f32(inputs["L0_attn_Wqkv"]),
                  wproj=f32(inputs["L0_attn_Wproj"]),
                  rmsw_in=rmsw, lam_in=lam,
                  routw_in=f32(inputs["L0_ffn_router_w"]),
                  wg_in=wg_t, wu_in=wu_t,
                  wd_in=f32(inputs["L0_ffn_w_down"]),
                  swg_in=swg_t, swu_in=swu_t,
                  swd_in=f32(inputs["L0_ffn_sw_down"]))
    return x, common


def kernel(**inputs):
    x, common = _prep_host(inputs)
    B = x.shape[0]
    nc = build(T=x.shape[1])
    in_maps = []
    for b in range(B):
        m = dict(common)
        m["x_in"] = np.ascontiguousarray(x[b])
        in_maps.append(m)
    res = run_bass_kernel_spmd(nc, in_maps, core_ids=list(range(B)))
    out = np.stack([np.asarray(res.results[b]["out"]) for b in range(B)], axis=0)
    return out.astype(np.float32)


if __name__ == "__main__":
    nc = build()
    print("instructions:", nc.n_instructions())


# version 15
# speedup vs baseline: 1.0185x; 1.0185x over previous
import math
import numpy as np
import concourse.bass as bass
import concourse.mybir as mybir
from concourse.bass_utils import run_bass_kernel_spmd

F32 = mybir.dt.float32
BF16 = mybir.dt.bfloat16
AF = mybir.ActivationFunctionType
ALU = mybir.AluOpType
AX = mybir.AxisListType

D = 2048
NH = 16
DH = 64
DV = 128
FF = 1024
NE = 8
NF = FF // 128
NC = D // 128
LAMBDA_INIT = 0.8 - 0.6 * math.exp(-0.3)
EPS_LN = 1e-5
EPS_RMS_ATTN = 1e-5
EPS_RMS_FFN = 1e-6
VCH = 272


class Ch:
    def __init__(self, nc, name, step=1):
        self.sem = nc.alloc_semaphore(name)
        self.n = 0
        self.step = step

    def bump(self):
        self.n += self.step
        return self.n


def build(T=2048, dumps=()):
    nc = bass.Bass("TRN2", target_bir_lowering=False)
    NT = T // 128
    JW = min(512, T)
    NJ = T // JW
    HALF = T // 2
    NTH = HALF // 128
    TCW = min(512, HALF)
    n_tch = HALF // TCW
    QH = D // 2

    x_in = nc.dram_tensor("x_in", [T, D], F32, kind="ExternalInput")
    wqkv = nc.dram_tensor("wqkv", [D, 3 * D], F32, kind="ExternalInput")
    wproj = nc.dram_tensor("wproj", [D, D], F32, kind="ExternalInput")
    rmsw_in = nc.dram_tensor("rmsw_in", [128, D], F32, kind="ExternalInput")
    lam_in = nc.dram_tensor("lam_in", [128, 4 * DH], F32, kind="ExternalInput")
    routw_in = nc.dram_tensor("routw_in", [D, NE], F32, kind="ExternalInput")
    wg_in = nc.dram_tensor("wg_in", [NE, NF, NC, 128, 128], F32, kind="ExternalInput")
    wu_in = nc.dram_tensor("wu_in", [NE, NF, NC, 128, 128], F32, kind="ExternalInput")
    wd_in = nc.dram_tensor("wd_in", [NE, FF, D], F32, kind="ExternalInput")
    swg_in = nc.dram_tensor("swg_in", [NF, NC, 128, 128], F32, kind="ExternalInput")
    swu_in = nc.dram_tensor("swu_in", [NF, NC, 128, 128], F32, kind="ExternalInput")
    swd_in = nc.dram_tensor("swd_in", [FF, D], F32, kind="ExternalInput")
    out_t = nc.dram_tensor("out", [T, D], F32, kind="ExternalOutput")

    sk = lambda nm: "ExternalOutput" if nm in dumps else "Internal"
    lnT_hbm = nc.dram_tensor("lnT_hbm", [NC, 128, T], BF16, kind=sk("lnT_hbm"))
    dnT_hbm = nc.dram_tensor("dnT_hbm", [NC, 128, T], BF16, kind=sk("dnT_hbm"))
    hT_hbm = nc.dram_tensor("hT_hbm", [NC, 128, T], BF16, kind=sk("hT_hbm"))
    x1_hbm = nc.dram_tensor("x1_hbm", [T, D], F32, kind=sk("x1_hbm"))
    dump_sroute = None
    if "s_route" in dumps:
        dump_sroute = nc.dram_tensor("s_route_d", [128, NT * 8], F32,
                                     kind="ExternalOutput")

    ident = nc.alloc_sbuf_tensor("ident", [128, 128], BF16).ap()
    mask01 = nc.alloc_sbuf_tensor("mask01", [128, 128], BF16).ap()
    rmsw = nc.alloc_sbuf_tensor("rmsw", [128, D], F32).ap()
    lamv = nc.alloc_sbuf_tensor("lamv", [128, 4 * DH], F32).ap()
    tiny = nc.alloc_sbuf_tensor("tiny", [128, 128], F32).ap()
    i8 = nc.alloc_sbuf_tensor("i8", [128, 48], F32).ap()
    routw = nc.alloc_sbuf_tensor("routw", [128, NC * 8], BF16).ap()
    s_route = nc.alloc_sbuf_tensor("s_route", [128, NT * 8], F32).ap()
    NSL = 16
    dnscr = [nc.alloc_sbuf_tensor(f"dnscr{i}", [128, 128], BF16).ap() for i in range(NSL)]
    asb = [nc.alloc_sbuf_tensor(f"asb{i}", [128, 264], F32).ap() for i in range(NSL)]
    sqjunk = nc.alloc_sbuf_tensor("sqjunk", [128, D], BF16).ap()

    GCOL = dict(lam=0, dot1=1, dot2=2, e1=3, e2=4, m1=5, m2=6, dlt=7, w1=8,
                w2=9, eps_ln=10, eps_ffn=11, eps11=12)
    LNCOL = dict(sum=0, sumsq=1, varD=2, rstd=3, bias=4, vv=5, ta=6, q=7, r1b=8)

    def tc(name, par=0):
        if name in GCOL:
            c = GCOL[name]
        else:
            c = 32 + 16 * par + LNCOL[name]
        return tiny[:, c:c + 1]

    def ln1c(name, p):
        c = 64 + 16 * p + LNCOL[name]
        return tiny[:, c:c + 1]

    def t2(name, sl):
        c = 64 + 4 * sl + {"r1": 0, "r2": 1, "ms": 2, "rstd2": 3}[name]
        return tiny[:, c:c + 1]

    base = nc.sbuf_base
    avail = nc.sbuf_top - base

    def at(name, shape, dtype, off):
        sz = shape[1] * (4 if dtype == F32 else 2)
        assert off >= 0 and off + sz <= avail, (name, off, sz, avail)
        return nc.alloc_sbuf_tensor_at(name, shape, dtype,
                                       offset=base + off).ap()

    lnTsz = NC * JW * 2
    o = 0
    lnTc = [at(f"lnTc{i}", [128, NC * JW], BF16, o + i * lnTsz) for i in range(2)]
    o += 2 * lnTsz
    o_wqk = o
    wqk_st = [at(f"wqk{i}", [128, NC * 512], BF16, o + i * NC * 1024) for i in range(2)]
    o += 2 * NC * 1024
    o_wv = o
    wv_st = [at(f"wv{i}", [128, NC * 256], BF16, o + i * NC * 512) for i in range(2)]
    o += 2 * NC * 512
    o_qkT = o
    qkT = at("qkT", [128, 4 * T], BF16, o)
    o += 4 * T * 2
    o_pT = o
    pT = []
    for a in range(2):
        pT.append(at(f"pT{a}", [128, NT * 2 * JW], BF16, o))
        o += NT * 2 * JW * 2
    o_vbuf = o
    vbuf = at("vbuf", [128, NT * VCH], BF16, o)
    o += NT * VCH * 2
    dstage = at("dstage", [128, 2 * JW], BF16, o)
    o += 2 * JW * 2
    p12_top = o
    o_dstage_end = o
    if 2 * D * 2 <= o_dstage_end - o_vbuf:
        o_lnscr = o_vbuf
    else:
        o_lnscr = p12_top
        p12_top += 2 * D * 2
    lnscr = [at(f"lnscr{i}", [128, D], BF16, o_lnscr + i * D * 2) for i in range(2)]
    if 2 * NC * 256 <= 2 * NT * JW * 2:
        o_lnts = o_pT + 2 * NT * JW * 2
    else:
        o_lnts = p12_top
        p12_top += 2 * NC * 256
    lnT_stage = [at(f"lnTs{i}", [128, NC * 128], BF16,
                    o_lnts + i * NC * 256) for i in range(2)]

    wproj_sb = at("wproj_sb", [128, 4 * NC * 512], BF16, 0)
    o3 = 4 * NC * 512 * 2
    o_dT = o3
    dTbuf = [at(f"dT{i}", [128, NC * JW], BF16, o3 + i * lnTsz) for i in range(2)]
    o3 += 2 * lnTsz
    o_hth = o3
    hT_half = at("hT_half", [128, NC * HALF], BF16, o3)
    o3 += NC * HALF * 2
    o_x1t = o3
    x1t = [at(f"x1t{i}", [128, D], F32, o3 + i * D * 4) for i in range(2)]
    o3 += 2 * D * 4
    hbuf = [at(f"hbuf{i}", [128, D], BF16, o3 + i * D * 2) for i in range(2)]
    o3 += 2 * D * 2
    o_hTs = o3
    hTstage = [at(f"hTs{i}", [128, NC * 128], BF16, o3 + i * NC * 256) for i in range(2)]
    o3 += 2 * NC * 256
    p3_top = o3

    ysz = NTH * D * 4
    assert ysz <= o_dT, (ysz, o_dT)
    ybuf = at("ybuf", [128, NTH * D], F32, 0)
    asz = NF * HALF * 2
    sgsz = 2 * TCW * 2
    if asz + 4 * NC * 256 <= o_hth - o_dT:
        o_a = o_dT
        o_wgu = o_dT + asz
        o_wd = o_x1t
        if o_wgu + 4 * NC * 256 + sgsz <= o_hth:
            o_sg = o_wgu + 4 * NC * 256
        else:
            o_sg = o_wd + NF * D * 2
        assert o_wgu + 4 * NC * 256 <= o_hth, (o_wgu, o_hth)
        assert o_wd >= o_hth + NC * HALF * 2
    else:
        o_wd = o_x1t
        o_sg = o_wd + NF * D * 2
        o_a = max(p12_top, p3_top, o_sg + sgsz)
        o_wgu = o_a + asz
    abuf = at("abuf", [128, NF * HALF], BF16, o_a)
    wgu_st = []
    for nmi, nm in enumerate(("wgst", "wust")):
        wgu_st.append([at(f"{nm}{i}", [128, NC * 128], BF16,
                          o_wgu + (2 * nmi + i) * NC * 256) for i in range(2)])
    wd_sb = at("wd_sb", [128, NF * D], BF16, o_wd)
    sgscr = [at(f"sg{i}", [128, TCW], BF16, o_sg + i * TCW * 2) for i in range(2)]
    p4_top = max(o_sg + sgsz, o_wgu + 4 * NC * 256, o_wd + NF * D * 2)
    if 4 * D * 4 <= 2 * NT * 2 * JW * 2 and o_pT == o_hth and 4 * D * 4 <= NC * HALF * 2:
        o_xb = o_pT
    else:
        o_xb = avail - 4 * D * 4
        assert o_xb >= max(p12_top, p3_top, p4_top), \
            (o_xb, p12_top, p3_top, p4_top)
    xbuf = [at(f"xbuf{i}", [128, D], F32, o_xb + i * D * 4) for i in range(4)]
    assert max(p12_top, p3_top, p4_top) <= avail, (p12_top, p3_top, p4_top, avail)

    pA = nc.alloc_psum_tensor("pA", [128, 1024], F32).ap()
    pB = nc.alloc_psum_tensor("pB", [128, 1024], F32).ap()
    pCD = nc.alloc_psum_tensor("pCD", [128, 1024], F32).ap()
    pbf = [nc.alloc_psum_tensor(f"pbf{i}", [128, 1024], BF16).ap() for i in range(2)]
    bank512 = [pA[:, 0:512], pA[:, 512:1024], pB[:, 0:512], pB[:, 512:1024],
               pCD[:, 0:512], pCD[:, 512:1024]]
    spair = [pA, pB]

    s_init = Ch(nc, "s_init")
    s_tn = Ch(nc, "s_tn")
    s_vp = Ch(nc, "s_vp")
    s_sp = Ch(nc, "s_sp")
    dch = {}
    for nm in ("xl0", "xl1", "xl2", "xl3", "lns0", "lns1", "lnc0", "lnc1", "wq0", "wq1",
               "dst0", "dst1", "wp", "rw", "dT0", "dT1", "x30", "x31", "x32", "x33", "x1s0",
               "x1s1", "hts0", "hts1", "hth", "wg0", "wg1", "wd", "yini",
               "yst", "pre"):
        dch[nm] = Ch(nc, "d_" + nm, 16)

    ops = {k: [] for k in ("tensor", "vector", "scalar", "gpsimd", "sync")}

    def em(eng, fn):
        ops[eng].append(fn)

    last_wait = {}

    def wait(eng, ch, val):
        assert val <= ch.n, (eng, val, ch.n)
        key = (eng, ch.sem.num)
        if val <= last_wait.get(key, 0):
            return
        last_wait[key] = val
        em(eng, lambda e, s=ch.sem, v=val: e.wait_ge(s, v))

    def g_init(g):
        steps = [
            lambda: g.memset(ident[:, :], 0.0),
            lambda: g.affine_select(out=ident[:, :], in_=ident[:, :],
                                    compare_op=ALU.not_equal, fill=1.0, base=0,
                                    pattern=[[-1, 128]], channel_multiplier=1),
            lambda: g.memset(mask01[:, :], 0.0),
            lambda: g.affine_select(out=mask01[:, :], in_=mask01[:, :],
                                    compare_op=ALU.is_gt, fill=1.0, base=0,
                                    pattern=[[-1, 128]], channel_multiplier=1),
        ]
        n = 0
        for st in steps:
            st().then_inc(s_init.sem, 1)
            n += 1
            g.wait_ge(s_init.sem, n)
    em("gpsimd", g_init)
    for _ in range(4):
        s_init.bump()
    init_done = s_init.n

    def sy_pre(sy):
        sy.dma_start(rmsw[:, :], rmsw_in.ap()).then_inc(dch["pre"].sem, 16)
        sy.dma_start(lamv[:, :], lam_in.ap()).then_inc(dch["pre"].sem, 16)
    em("sync", sy_pre)
    dch["pre"].bump(); dch["pre"].bump()
    pre_done = dch["pre"].n

    def g_rw(g):
        g.dma_start(routw.rearrange("p (c e) -> p c e", e=8),
                    routw_in.ap().rearrange("(c p) e -> p c e", p=128)
                    ).then_inc(dch["rw"].sem, 16)
    em("gpsimd", g_rw)
    dch["rw"].bump()
    routw_done = dch["rw"].n

    def v_consts(v):
        v.memset(tc("eps_ln"), EPS_LN)
        v.memset(tc("eps_ffn"), EPS_RMS_FFN)
        v.memset(tc("eps11"), EPS_LN * EPS_RMS_FFN).then_inc(s_vp.sem, 1)
    em("vector", v_consts); s_vp.bump()
    wait("vector", s_vp, s_vp.n)

    wait("vector", dch["pre"], pre_done)
    wait("vector", s_init, init_done)

    def vchain(v_ops):
        for fn in v_ops:
            em("vector", fn)
            s_vp.bump()
            wait("vector", s_vp, s_vp.n)

    vchain([
        lambda v: v.tensor_mul(lamv[:, 0:DH], lamv[:, 0:DH],
                               lamv[:, DH:2 * DH]).then_inc(s_vp.sem, 1),
        lambda v: v.reduce_sum(tc("dot1"), lamv[:, 0:DH], axis=AX.X).then_inc(s_vp.sem, 1),
        lambda v: v.tensor_mul(lamv[:, 2 * DH:3 * DH], lamv[:, 2 * DH:3 * DH],
                               lamv[:, 3 * DH:4 * DH]).then_inc(s_vp.sem, 1),
        lambda v: v.reduce_sum(tc("dot2"), lamv[:, 2 * DH:3 * DH],
                               axis=AX.X).then_inc(s_vp.sem, 1),
    ])
    wait("scalar", s_vp, s_vp.n)

    def s_lam(s):
        s.activation(tc("e1"), tc("dot1"), AF.Exp).then_inc(s_sp.sem, 1)
        s.activation(tc("e2"), tc("dot2"), AF.Exp).then_inc(s_sp.sem, 1)
    em("scalar", s_lam); s_sp.bump(); s_sp.bump()
    wait("vector", s_sp, s_sp.n)
    vchain([
        lambda v: v.tensor_sub(tc("lam"), tc("e1"), tc("e2")).then_inc(s_vp.sem, 1),
        lambda v: v.tensor_single_scalar(tc("lam"), tc("lam"), LAMBDA_INIT, ALU.add).then_inc(s_vp.sem, 1),
    ])
    lam_done_v = s_vp.n

    tp_n = [0]
    tpe_done = []

    def transpose_tile(src, dst_stage, first_evac_guards=(), dst_fn=None):
        for c0 in range(0, NC, 8):
            nn = min(8, NC - c0)
            k = tp_n[0]
            if k >= 2:
                wait("tensor", s_vp, tpe_done[k - 2])
            for ci in range(nn):
                def t_tr(t, src=src, c=c0 + ci, ci=ci, k=k, last=(ci == nn - 1)):
                    ins = t.transpose(pbf[k % 2][:, ci * 128:(ci + 1) * 128],
                                      src[:, c * 128:(c + 1) * 128], ident[:, :])
                    if last:
                        ins.then_inc(s_tn.sem, 1)
                em("tensor", t_tr)
            s_tn.bump()
            wait("vector", s_tn, s_tn.n)
            if c0 == 0:
                for g in first_evac_guards:
                    wait("vector", *g)
            if dst_fn is not None:
                for ci in range(nn):
                    def v_ev(v, dst=dst_fn(c0 + ci), ci=ci, k=k,
                             last=(ci == nn - 1)):
                        ins = v.tensor_copy(dst, pbf[k % 2][:, ci * 128:(ci + 1) * 128])
                        if last:
                            ins.then_inc(s_vp.sem, 1)
                    em("vector", v_ev)
                s_vp.bump()
            else:
                def v_ev(v, dst=dst_stage[:, c0 * 128:(c0 + nn) * 128], k=k, nn=nn):
                    v.tensor_copy(dst, pbf[k % 2][:, 0:nn * 128]).then_inc(s_vp.sem, 1)
                em("vector", v_ev); s_vp.bump()
            tpe_done.append(s_vp.n)
            tp_n[0] += 1

    wait("tensor", s_init, init_done)
    app_done = {}
    red_done = {}
    lnscr_free = {}
    lns_guard = [0, 0]
    lnTc_ready = [0, 0]
    xl_n = {}
    bsn1 = {}

    def p1_loads(grp):
        for i in grp:
            chx = dch["xl%d" % (i % 4)]
            if i >= 4:
                wait("sync", s_vp, app_done[i - 4])
                wait("sync", s_vp, red_done[i - 4])
            def sy_x(sy, i=i, ch=chx):
                sy.dma_start(xbuf[i % 4][:, :],
                             x_in.ap()[i * 128:(i + 1) * 128, :]).then_inc(ch.sem, 16)
            em("sync", sy_x); chx.bump()
            xl_n[i] = (chx, chx.n)

    def p1_stats(grp):
        sqn = {}; stv = {}
        for i in grp:
            wait("scalar", *xl_n[i])
            if i > 0:
                wait("scalar", s_sp, s_sp.n)
            def s_sq(s, i=i):
                s.activation(sqjunk[:, :], xbuf[i % 4][:, :], AF.Square,
                             accum_out=ln1c("sumsq", i % 4)).then_inc(s_sp.sem, 1)
            em("scalar", s_sq); s_sp.bump()
            sqn[i] = s_sp.n
        for i in grp:
            wait("vector", *xl_n[i])
            def v_red(v, i=i):
                v.reduce_sum(ln1c("sum", i % 4), xbuf[i % 4][:, :],
                             axis=AX.X).then_inc(s_vp.sem, 1)
            em("vector", v_red); s_vp.bump()
            red_done[i] = s_vp.n
        for i in grp:
            wait("vector", s_vp, red_done[i])
            def v_msq(v, i=i):
                v.tensor_scalar(ln1c("varD", i % 4), ln1c("sum", i % 4),
                                ln1c("sum", i % 4), 1.0 / D, ALU.mult,
                                ALU.mult).then_inc(s_vp.sem, 1)
            em("vector", v_msq); s_vp.bump()
            stv[i] = s_vp.n
        for i in grp:
            wait("vector", s_vp, stv[i])
            wait("vector", s_sp, sqn[i])
            def v_var(v, i=i):
                v.tensor_sub(ln1c("varD", i % 4), ln1c("sumsq", i % 4),
                             ln1c("varD", i % 4)).then_inc(s_vp.sem, 1)
            em("vector", v_var); s_vp.bump()
            stv[i] = s_vp.n
        lnn = {}
        for i in grp:
            wait("scalar", s_vp, stv[i])
            def s_ln(s, i=i):
                s.activation(ln1c("rstd", i % 4), ln1c("varD", i % 4), AF.Ln,
                             bias=tc("eps_ln"), scale=1.0 / D).then_inc(s_sp.sem, 1)
            em("scalar", s_ln); s_sp.bump()
            lnn[i] = s_sp.n
        for i in grp:
            wait("scalar", s_sp, lnn[i])
            def s_ex(s, i=i):
                s.activation(ln1c("rstd", i % 4), ln1c("rstd", i % 4), AF.Exp,
                             scale=-0.5).then_inc(s_sp.sem, 1)
            em("scalar", s_ex); s_sp.bump()
            lnn[i] = s_sp.n
        for i in grp:
            wait("vector", s_sp, lnn[i])
            def v_bias(v, i=i):
                v.tensor_scalar(ln1c("bias", i % 4), ln1c("sum", i % 4),
                                ln1c("rstd", i % 4), -1.0 / D, ALU.mult,
                                ALU.mult).then_inc(s_vp.sem, 1)
            em("vector", v_bias); s_vp.bump()
            bsn1[i] = s_vp.n

    def p1_apply_tr(grp):
        for i in grp:
            wait("vector", s_vp, bsn1[i])
            if i >= 2:
                wait("vector", s_tn, lnscr_free[i - 2])
            def v_ap(v, i=i):
                v.tensor_scalar(lnscr[i % 2][:, :], xbuf[i % 4][:, :],
                                ln1c("rstd", i % 4), ln1c("bias", i % 4),
                                ALU.mult, ALU.add).then_inc(s_vp.sem, 1)
            em("vector", v_ap); s_vp.bump()
            app_done[i] = s_vp.n
            wait("tensor", s_vp, app_done[i])
            direct = (i < 8 and JW == 512 and NJ >= 4)
            if direct:
                transpose_tile(
                    lnscr[i % 2], None,
                    dst_fn=lambda c, i=i: lnTc[i // 4][
                        :, c * JW + (i % 4) * 128: c * JW + (i % 4) * 128 + 128])
                lnTc_ready[i // 4] = s_vp.n
            else:
                gmin = 10 if (JW == 512 and NJ >= 4) else 2
                g = [(dch["lns%d" % (i % 2)], lns_guard[i % 2])] if i >= gmin else []
                transpose_tile(lnscr[i % 2], lnT_stage[i % 2], first_evac_guards=g)
            lnscr_free[i] = s_tn.n
            wait("sync", s_vp, s_vp.n)
            chs = dch["lns%d" % (i % 2)]
            def sy_st(sy, i=i, ch=chs, direct=direct):
                dst = lnT_hbm.ap()[:, :, i * 128:(i + 1) * 128].rearrange("c p w -> p c w")
                if direct:
                    src = lnTc[i // 4].rearrange("q (c w) -> q c w", w=JW)[
                        :, :, (i % 4) * 128:(i % 4) * 128 + 128]
                else:
                    src = lnT_stage[i % 2].rearrange("p (c w) -> p c w", w=128)
                sy.dma_start(dst, src).then_inc(ch.sem, 16)
            em("sync", sy_st); chs.bump()
            lns_guard[i % 2] = chs.n

    pend1 = None
    for g0 in range(0, NT, 4):
        grp = list(range(g0, min(g0 + 4, NT)))
        if pend1 is not None:
            p1_apply_tr(pend1)
        p1_loads(grp)
        p1_stats(grp)
        pend1 = grp
    p1_apply_tr(pend1)
    phase1_vp_end = s_vp.n

    wait("gpsimd", s_tn, lnscr_free[NT - 1])
    def g_vinit(g):
        vr = vbuf.rearrange("p (c w) -> p c w", w=VCH)
        for i, st in enumerate([
            lambda: g.memset(vbuf[:, :], 0.0),
            lambda: g.memset(vr[:, :, 128:129], 1.0),
            lambda: g.memset(vr[:, :, 264:265], 1.0),
        ]):
            st().then_inc(s_init.sem, 1)
            g.wait_ge(s_init.sem, init_done + i + 1)
    em("gpsimd", g_vinit)
    for _ in range(3):
        s_init.bump()
    vbuf_ready = s_init.n

    col0 = {"q1": 0, "k1": QH, "q2": 2 * QH, "k2": 3 * QH}
    wq_pair_n = []
    wfree_n = []
    lnc_guard = [0, 0]
    qk_evac_n = []
    v_evac_n = []
    s_ev_n = []
    s_cnt = [0]
    qk_cnt = [0]
    v_cnt = [0]
    pv_hist = [0]
    vd1_n = [0]
    dtr_slot = {i: 0 for i in range(NSL)}
    ln5_guard = {i: 0 for i in range(NSL)}
    st7_guard = {i: 0 for i in range(NSL)}
    dst_guard = [0, 0]
    pv_groups = [0]
    gslot = [0]
    pend = []
    pend_sc = [None]

    def flush_chunk(hbase, pj, dnn, slots):
        ntile = JW // 128
        k = tp_n[0]
        if k >= 2:
            wait("tensor", s_vp, tpe_done[k - 2])
        wait("tensor", s_vp, dnn[-1])
        nsl_c = len(slots)
        for hl in range(2):
            for tl in range(ntile):
                sl = slots[tl * 2 + hl]
                si = hl * ntile + tl
                def t_dtr(t, sl=sl, si=si, k=k, last=(hl == 1 and tl == ntile - 1)):
                    ins = t.transpose(pbf[k % 2][:, si * 128:(si + 1) * 128],
                                      dnscr[sl][:, :], ident[:, :])
                    if last:
                        ins.then_inc(s_tn.sem, 1)
                em("tensor", t_dtr)
                dtr_slot[sl] = s_tn.n + 1
        s_tn.bump()
        wait("vector", s_tn, s_tn.n)
        wait("vector", dch["dst0"], dst_guard[0])
        wait("vector", dch["dst1"], dst_guard[1])
        def v_dst(v, k=k, nsl_c=nsl_c):
            v.tensor_copy(dstage[:, 0:nsl_c * 128],
                          pbf[k % 2][:, 0:nsl_c * 128]).then_inc(s_vp.sem, 1)
        em("vector", v_dst); s_vp.bump()
        tpe_done.append(s_vp.n)
        tp_n[0] += 1
        wait("gpsimd", s_vp, s_vp.n)
        for hl in range(2):
            chd = dch["dst%d" % hl]
            def sy_dst(sy, hbase=hbase, hl=hl, pj=pj, ch=chd, ntile=ntile):
                sy.dma_start(dnT_hbm.ap()[hbase + hl, :, pj * JW:(pj + 1) * JW],
                             dstage[:, hl * ntile * 128:(hl + 1) * ntile * 128]
                             ).then_inc(ch.sem, 16)
            em("gpsimd", sy_dst); chd.bump()
            dst_guard[hl] = chd.n

    def emit_chain_tail(hbase, j, slots, st4v):
        nidx = len(slots)
        ln5 = []
        for ix in range(nidx):
            sl = slots[ix]
            wait("scalar", s_vp, st4v[ix])
            wait("scalar", s_vp, st7_guard[sl])
            def s5(s, sl=sl):
                s.activation(t2("rstd2", sl), t2("ms", sl), AF.Ln,
                             bias=tc("eps_ln"), scale=1.0 / DV).then_inc(s_sp.sem, 1)
            em("scalar", s5); s_sp.bump()
            ln5_guard[sl] = s_sp.n
            ln5.append(s_sp.n)
        ex6 = []
        for ix in range(nidx):
            sl = slots[ix]
            wait("scalar", s_sp, ln5[ix])
            def s6(s, sl=sl):
                s.activation(t2("rstd2", sl), t2("rstd2", sl), AF.Exp,
                             scale=-0.5).then_inc(s_sp.sem, 1)
            em("scalar", s6); s_sp.bump()
            ex6.append(s_sp.n)
        dnn = []
        for ix in range(nidx):
            sl = slots[ix]
            wait("vector", s_sp, ex6[ix])
            wait("vector", s_tn, dtr_slot[sl])
            def v7(v, sl=sl):
                v.tensor_scalar(dnscr[sl][:, :], asb[sl][:, 0:128],
                                t2("rstd2", sl), 1.0 - LAMBDA_INIT,
                                ALU.mult, ALU.mult).then_inc(s_vp.sem, 1)
            em("vector", v7); s_vp.bump()
            st7_guard[sl] = s_vp.n
            dnn.append(s_vp.n)
        pend.append((hbase, j, dnn, slots))

    def emit_wq(p):
        chw = dch["wq%d" % (p % 2)]
        if p >= 2:
            wait("gpsimd", s_tn, wfree_n[p - 2])
        def g_wq(g, p=p, ch=chw):
            for jt, nm in enumerate(("q1", "k1", "q2", "k2")):
                src = wqkv.ap()[:, col0[nm] + p * 128: col0[nm] + (p + 1) * 128]
                src = src.rearrange("(c q) j -> q c j", q=128)
                dst = wqk_st[p % 2].rearrange("q (c j) -> q c j", j=512)
                g.dma_start(dst[:, :, jt * 128:(jt + 1) * 128], src).then_inc(ch.sem, 16)
            srcv = wqkv.ap()[:, 4 * QH + p * 256: 4 * QH + (p + 1) * 256]
            srcv = srcv.rearrange("(c q) j -> q c j", q=128)
            dstv = wv_st[p % 2].rearrange("q (c j) -> q c j", j=256)
            g.dma_start(dstv, srcv).then_inc(ch.sem, 16)
        em("gpsimd", g_wq)
        for _ in range(5):
            chw.bump()
        wq_pair_n.append((chw, chw.n))

    emit_wq(0)
    emit_wq(1)
    for p in range(8):
        wait("tensor", *wq_pair_n[p])
        if p == 0:
            wait("tensor", s_vp, phase1_vp_end)
        else:
            wait("tensor", s_vp, qk_evac_n[-1])
        for tcx in range(NJ):
            glob_tc = p * NJ + tcx
            direct = (glob_tc < 2 and JW == 512 and NJ >= 4)
            chl = dch["lnc%d" % (glob_tc % 2)]
            if not direct:
                if glob_tc >= 2:
                    wait("sync", s_tn, lnc_guard[glob_tc % 2])
                wait("sync", dch["lns0"], dch["lns0"].n)
                wait("sync", dch["lns1"], dch["lns1"].n)
                def sy_ln(sy, tcx=tcx, glob_tc=glob_tc, ch=chl):
                    src = lnT_hbm.ap()[:, :, tcx * JW:(tcx + 1) * JW].rearrange("c p w -> p c w")
                    dst = lnTc[glob_tc % 2].rearrange("q (c w) -> q c w", w=JW)
                    sy.dma_start(dst, src).then_inc(ch.sem, 16)
                em("sync", sy_ln); chl.bump()
                wait("tensor", chl, chl.n)
            else:
                wait("tensor", s_vp, lnTc_ready[glob_tc])
            for jt in range(4):
                k = qk_cnt[0]
                bank = bank512[k % 2]
                if k >= 2:
                    wait("tensor", s_vp, qk_evac_n[k - 2])
                for c in range(NC):
                    def t_qk(t, jt=jt, c=c, gp=glob_tc % 2, bank=bank, p=p):
                        ins = t.matmul(bank[:, 0:JW],
                                       wqk_st[p % 2][:, c * 512 + jt * 128: c * 512 + jt * 128 + 128],
                                       lnTc[gp][:, c * JW:(c + 1) * JW],
                                       start=(c == 0), stop=(c == NC - 1))
                        if c == NC - 1:
                            ins.then_inc(s_tn.sem, 1)
                    em("tensor", t_qk)
                s_tn.bump()
                wait("vector", s_tn, s_tn.n)
                sc = 0.125 if jt in (0, 2) else 1.0
                def v_qke(v, jt=jt, tcx=tcx, bank=bank, sc=sc):
                    dst = qkT[:, jt * T + tcx * JW: jt * T + tcx * JW + JW]
                    if sc == 1.0:
                        v.tensor_copy(dst, bank[:, 0:JW]).then_inc(s_vp.sem, 1)
                    else:
                        v.tensor_single_scalar(dst, bank[:, 0:JW], sc,
                                               ALU.mult).then_inc(s_vp.sem, 1)
                em("vector", v_qke); s_vp.bump()
                qk_evac_n.append(s_vp.n)
                qk_cnt[0] += 1
            for tl in range(JW // 128):
                ti = tcx * (JW // 128) + tl
                k = v_cnt[0]
                bank = bank512[2 + k % 2]
                if k >= 2:
                    wait("tensor", s_vp, v_evac_n[k - 2])
                for c in range(NC):
                    def t_v(t, c=c, tl=tl, gp=glob_tc % 2, bank=bank, p=p):
                        ins = t.matmul(bank[:, 0:256],
                                       lnTc[gp][:, c * JW + tl * 128: c * JW + tl * 128 + 128],
                                       wv_st[p % 2][:, c * 256:(c + 1) * 256],
                                       start=(c == 0), stop=(c == NC - 1))
                        if c == NC - 1:
                            ins.then_inc(s_tn.sem, 1)
                    em("tensor", t_v)
                s_tn.bump()
                wait("vector", s_tn, s_tn.n)
                if p == 0 and v_cnt[0] == 0:
                    wait("vector", s_init, vbuf_ready)
                def v_ev(v, ti=ti, bank=bank):
                    v.tensor_copy(vbuf[:, ti * VCH: ti * VCH + 128],
                                  bank[:, 0:128]).then_inc(s_vp.sem, 1)
                    v.tensor_copy(vbuf[:, ti * VCH + 136: ti * VCH + 264],
                                  bank[:, 128:256]).then_inc(s_vp.sem, 1)
                em("vector", v_ev); s_vp.bump(); s_vp.bump()
                v_evac_n.append(s_vp.n)
                v_cnt[0] += 1
            lnc_guard[glob_tc % 2] = s_tn.n
        wfree_n.append(s_tn.n)
        if p + 2 < 8:
            emit_wq(p + 2)

        wait("tensor", s_vp, qk_evac_n[-1])
        wait("tensor", s_vp, v_evac_n[-1])
        if p == 0:
            wait("tensor", s_init, vbuf_ready)
            wait("scalar", s_vp, red_done[NT - 1])
        for j in range(NJ):
            cmax = min(4 * j + 3, NT - 1)
            gates = [None, None]
            for a in range(2):
                qs = 0 if a == 0 else 2
                ks = 1 if a == 0 else 3
                pv_guard = pv_hist[0]
                for cc in range(cmax + 1):
                    k = s_cnt[0]
                    bankp = spair[k % 2]
                    if k >= 2:
                        wait("tensor", s_sp, s_ev_n[k - 2])
                    for hl in range(2):
                        def t_s(t, cc=cc, j=j, qs=qs, ks=ks, hl=hl,
                                bankp=bankp, last=(hl == 1)):
                            qoff = 64 * hl
                            ins = t.matmul(bankp[:, hl * JW: (hl + 1) * JW],
                                           qkT[qoff:qoff + 64, ks * T + cc * 128: ks * T + cc * 128 + 128],
                                           qkT[qoff:qoff + 64, qs * T + j * JW: qs * T + (j + 1) * JW],
                                           start=True, stop=True,
                                           tile_position=(64 * hl, 0))
                            if last:
                                ins.then_inc(s_tn.sem, 1)
                        em("tensor", t_s)
                    s_tn.bump()
                    wait("scalar", s_tn, s_tn.n)
                    if pv_guard:
                        wait("scalar", s_tn, pv_guard)
                        pv_guard = 0
                    def s_exp(s, a=a, cc=cc, bankp=bankp):
                        s.activation(pT[a][:, cc * 2 * JW: (cc + 1) * 2 * JW],
                                     bankp[:, 0:2 * JW],
                                     AF.Exp).then_inc(s_sp.sem, 1)
                    em("scalar", s_exp); s_sp.bump()
                    s_ev_n.append(s_sp.n)
                    s_cnt[0] += 1
                    if 4 * j <= cc and (cc - 4 * j) * 128 < JW:
                        s0 = cc - 4 * j
                        wait("vector", s_sp, s_sp.n)
                        for hl in range(2):
                            def v_mask(v, a=a, cc=cc, s0=s0, hl=hl):
                                sl_ = pT[a][:, cc * 2 * JW + hl * JW + s0 * 128:
                                            cc * 2 * JW + hl * JW + s0 * 128 + 128]
                                v.tensor_mul(sl_, sl_, mask01[:, :]).then_inc(s_vp.sem, 1)
                            em("vector", v_mask); s_vp.bump()
                gates[a] = (s_sp.n, s_vp.n)
            ntile = JW // 128
            copy_n = []
            slots = []
            for tl in range(ntile):
                gidx = j * ntile + tl
                for hl in range(2):
                    voff = 136 * hl
                    apar = pv_groups[0] % 2
                    base = apar * 512
                    if len(vd1_n) >= 2:
                        wait("tensor", s_vp, vd1_n[-2])
                    for a in range(2):
                        sev_after, pm_after = gates[a]
                        wait("tensor", s_sp, sev_after)
                        wait("tensor", s_vp, pm_after)
                        for cc in range(gidx + 1):
                            def t_pv(t, a=a, cc=cc, tl=tl, hl=hl, voff=voff,
                                     base=base, gidx=gidx,
                                     last=(a == 1 and cc == gidx)):
                                ins = t.matmul(pCD[:, base + a * 132: base + a * 132 + 129],
                                               pT[a][:, cc * 2 * JW + hl * JW + tl * 128:
                                                      cc * 2 * JW + hl * JW + tl * 128 + 128],
                                               vbuf[:, cc * VCH + voff: cc * VCH + voff + 129],
                                               start=(cc == 0), stop=(cc == gidx))
                                if last:
                                    ins.then_inc(s_tn.sem, 1)
                            em("tensor", t_pv)
                    s_tn.bump()
                    pv_groups[0] += 1
                    pv_hist[0] = s_tn.n
                    wait("vector", s_tn, s_tn.n)
                    sl = gslot[0] % NSL
                    gslot[0] += 1
                    slots.append(sl)
                    def v_cp(v, sl=sl, base=base):
                        v.tensor_copy(
                            asb[sl].rearrange("p (t c) -> p t c", c=132)[:, :, 0:129],
                            pCD[:, base:base + 264].rearrange("p (t c) -> p t c", c=132)[:, :, 0:129]
                        ).then_inc(s_vp.sem, 1)
                    em("vector", v_cp); s_vp.bump()
                    vd1_n.append(s_vp.n)
                    copy_n.append(s_vp.n)

            if len(pend) >= 2:
                flush_chunk(*pend.pop(0))

            nidx = len(slots)
            st_n = [list(copy_n)]
            cur = []
            for ix in range(nidx):
                sl = slots[ix]
                wait("vector", s_vp, st_n[0][ix])
                def v0(v, sl=sl):
                    v.reciprocal(tiny[:, 64 + 4 * sl: 64 + 4 * sl + 2],
                                 asb[sl].rearrange("p (t c) -> p t c", c=132)[:, :, 128:129]
                                 ).then_inc(s_vp.sem, 1)
                em("vector", v0); s_vp.bump()
                cur.append(s_vp.n)
            st_n.append(cur)
            cur = []
            for ix in range(nidx):
                sl = slots[ix]
                wait("vector", s_vp, st_n[1][ix])
                def v1(v, sl=sl):
                    v.tensor_scalar(t2("r2", sl), t2("r2", sl), tc("lam"), -1.0,
                                    ALU.mult, ALU.mult).then_inc(s_vp.sem, 1)
                em("vector", v1); s_vp.bump()
                cur.append(s_vp.n)
            st_n.append(cur)
            cur = []
            for ix in range(nidx):
                sl = slots[ix]
                wait("vector", s_vp, st_n[1][ix])
                def v2(v, sl=sl):
                    v.tensor_scalar(asb[sl][:, 0:128], asb[sl][:, 0:128],
                                    t2("r1", sl), None, ALU.mult).then_inc(s_vp.sem, 1)
                em("vector", v2); s_vp.bump()
                cur.append(s_vp.n)
            st_n.append(cur)
            cur = []
            for ix in range(nidx):
                sl = slots[ix]
                wait("vector", s_vp, max(st_n[2][ix], st_n[3][ix]))
                def v3(v, sl=sl):
                    v.scalar_tensor_tensor(asb[sl][:, 0:128], asb[sl][:, 132:260],
                                           t2("r2", sl), asb[sl][:, 0:128],
                                           ALU.mult, ALU.add).then_inc(s_vp.sem, 1)
                em("vector", v3); s_vp.bump()
                cur.append(s_vp.n)
            st_n.append(cur)
            cur = []
            for ix in range(nidx):
                sl = slots[ix]
                wait("vector", s_vp, st_n[4][ix])
                wait("vector", s_sp, ln5_guard[sl])
                def v4(v, sl=sl):
                    v.scalar_tensor_tensor(asb[sl][:, 132:260], asb[sl][:, 0:128],
                                           1.0, asb[sl][:, 0:128], ALU.mult,
                                           ALU.mult,
                                           accum_out=t2("ms", sl)).then_inc(s_vp.sem, 1)
                em("vector", v4); s_vp.bump()
                cur.append(s_vp.n)
            st_n.append(cur)
            if pend_sc[0] is not None:
                emit_chain_tail(*pend_sc[0])
            pend_sc[0] = (2 * p, j, slots, list(st_n[5]))

    while len(pend) >= 2:
        flush_chunk(*pend.pop(0))
    if pend_sc[0] is not None:
        emit_chain_tail(*pend_sc[0])
        pend_sc[0] = None
    while pend:
        flush_chunk(*pend.pop(0))
    phase2_tn_end = s_tn.n
    phase2_vp_end = s_vp.n
    phase2_sp_end = s_sp.n

    wait("gpsimd", s_tn, wfree_n[-1])
    def g_wp(g):
        src = wproj.ap().rearrange("(c q) (dc d) -> q dc c d", q=128, d=512)
        dst = wproj_sb.rearrange("q (dc c d) -> q dc c d", dc=4, d=512)
        g.dma_start(dst, src).then_inc(dch["wp"].sem, 16)
    em("gpsimd", g_wp)
    dch["wp"].bump()
    wp_done = dch["wp"].n

    projev_n = []
    proj_last_tn = 0
    dT_guard = [0, 0]
    dT_n = None
    x1s_guard = [0, 0]
    hts_guard = [0, 0]
    rtv_n = []
    h_sp_done = {}
    h_vp_done = {}
    bias3_n = {}
    htr3_done = {}

    def apply3(pj):
        par = pj % 2
        wait("scalar", s_vp, bias3_n[pj])
        def s3_ap(s, par=par, xpar=pj % 4):
            s.activation(xbuf[xpar][:, :], x1t[par][:, :], AF.Identity,
                         bias=tc("bias", par),
                         scale=tc("rstd", par)).then_inc(s_sp.sem, 1)
        em("scalar", s3_ap); s_sp.bump()
        h_sp_done[pj] = s_sp.n
        wait("vector", s_sp, s_sp.n)
        wait("vector", dch["pre"], pre_done)
        if pj >= 2:
            wait("vector", s_tn, htr3_done[pj - 2])
        def v3_h(v, par=par, xpar=pj % 4):
            v.tensor_mul(hbuf[par][:, :], xbuf[xpar][:, :],
                         rmsw[:, :]).then_inc(s_vp.sem, 1)
        em("vector", v3_h); s_vp.bump()
        h_vp_done[pj] = s_vp.n

    def flush3(pj):
        wait("tensor", s_vp, h_vp_done[pj])
        g = [(dch["hts%d" % (pj % 2)], hts_guard[pj % 2])] if pj >= 2 else []
        transpose_tile(hbuf[pj % 2], hTstage[pj % 2], first_evac_guards=g)
        htr3_done[pj] = s_tn.n
        hT_ready = s_vp.n
        wait("sync", s_vp, hT_ready)
        chh = dch["hts%d" % (pj % 2)]
        def sy_hT(sy, pj=pj, ch=chh):
            dst = hT_hbm.ap()[:, :, pj * 128:(pj + 1) * 128].rearrange("c p w -> p c w")
            src = hTstage[pj % 2].rearrange("p (c w) -> p c w", w=128)
            sy.dma_start(dst, src).then_inc(ch.sem, 16)
        em("sync", sy_hT); chh.bump()
        hts_guard[pj % 2] = chh.n
        wait("tensor", s_vp, hT_ready)
        wait("tensor", dch["rw"], routw_done)
        if pj >= 1:
            wait("tensor", s_vp, rtv_n[pj - 1])
        for c in range(NC):
            def t_rt(t, c=c, pj=pj):
                ins = t.matmul(pCD[:, 0:8], hTstage[pj % 2][:, c * 128:(c + 1) * 128],
                               routw[:, c * 8:(c + 1) * 8],
                               start=(c == 0), stop=(c == NC - 1))
                if c == NC - 1:
                    ins.then_inc(s_tn.sem, 1)
            em("tensor", t_rt)
        s_tn.bump()
        wait("vector", s_tn, s_tn.n)
        vchain([
            lambda v: v.reduce_max(tc("m1"), pCD[:, 0:8], axis=AX.X).then_inc(s_vp.sem, 1),
            lambda v: v.tensor_scalar(i8[:, 0:8], pCD[:, 0:8], tc("m1"), None,
                                      ALU.is_ge).then_inc(s_vp.sem, 1),
            lambda v: v.scalar_tensor_tensor(i8[:, 8:16], i8[:, 0:8], -1e30,
                                             pCD[:, 0:8], ALU.mult,
                                             ALU.add).then_inc(s_vp.sem, 1),
            lambda v: v.reduce_max(tc("m2"), i8[:, 8:16], axis=AX.X).then_inc(s_vp.sem, 1),
            lambda v: v.tensor_sub(tc("dlt"), tc("m1"), tc("m2")).then_inc(s_vp.sem, 1),
        ])
        def v_i2(v):
            v.tensor_scalar(i8[:, 16:24], i8[:, 8:16], tc("m2"), None,
                            ALU.is_ge).then_inc(s_vp.sem, 1)
        em("vector", v_i2); s_vp.bump()
        wait("scalar", s_vp, s_vp.n)
        def s_w1(s):
            s.activation(tc("w1"), tc("dlt"), AF.Exp,
                         scale=-1.0).then_inc(s_sp.sem, 1)
        em("scalar", s_w1); s_sp.bump()
        wait("vector", s_sp, s_sp.n)
        vchain([
            lambda v: v.tensor_single_scalar(tc("w1"), tc("w1"), 1.0,
                                             ALU.add).then_inc(s_vp.sem, 1),
            lambda v: v.reciprocal(tc("w1"), tc("w1")).then_inc(s_vp.sem, 1),
            lambda v: v.tensor_scalar(tc("w2"), tc("w1"), -1.0, 1.0, ALU.mult,
                                      ALU.add).then_inc(s_vp.sem, 1),
            lambda v: v.tensor_scalar(i8[:, 24:32], i8[:, 0:8], tc("w1"), None,
                                      ALU.mult).then_inc(s_vp.sem, 1),
            lambda v, pj=pj: v.scalar_tensor_tensor(s_route[:, pj * 8:(pj + 1) * 8],
                                                    i8[:, 16:24], tc("w2"),
                                                    i8[:, 24:32], ALU.mult,
                                                    ALU.add).then_inc(s_vp.sem, 1),
        ])
        rtv_n.append(s_vp.n)

    for j in range(NT):
        cx = j // (JW // 128)
        tl = j % (JW // 128)
        if tl == 0:
            chD = dch["dT%d" % (cx % 2)]
            wait("sync", dch["dst0"], dch["dst0"].n)
            wait("sync", dch["dst1"], dch["dst1"].n)
            if cx >= 2:
                wait("sync", s_tn, dT_guard[cx % 2])
            def sy_dT(sy, cx=cx, ch=chD):
                src = dnT_hbm.ap()[:, :, cx * JW:(cx + 1) * JW].rearrange("c p w -> p c w")
                dst = dTbuf[cx % 2].rearrange("q (c w) -> q c w", w=JW)
                sy.dma_start(dst, src).then_inc(ch.sem, 16)
            em("sync", sy_dT); chD.bump()
            dT_n = (chD, chD.n)
        chx = dch["x3%d" % (j % 4)]
        if j >= 4:
            wait("sync", s_vp, projev_n[4 * (j - 4) + 3])
            wait("sync", s_vp, h_vp_done[j - 4])
            wait("sync", s_sp, h_sp_done[j - 4])
        def sy_x(sy, j=j, ch=chx):
            sy.dma_start(xbuf[j % 4][:, :],
                         x_in.ap()[j * 128:(j + 1) * 128, :]).then_inc(ch.sem, 16)
        em("sync", sy_x); chx.bump()
        x3j = (chx, chx.n)
        wait("tensor", dT_n[0], dT_n[1])
        if j == 0:
            wait("tensor", s_vp, phase2_vp_end)
            wait("tensor", s_sp, phase2_sp_end)
        for dcx in range(4):
            kk = 4 * j + dcx
            wait("tensor", dch["wp"], wp_done)
            bank = bank512[kk % 2]
            if kk >= 2:
                wait("tensor", s_vp, projev_n[kk - 2])
            for c in range(NC):
                def t_pj(t, c=c, dcx=dcx, tl=tl, cx=cx, bank=bank):
                    ins = t.matmul(bank[:, 0:512],
                                   dTbuf[cx % 2][:, c * JW + tl * 128: c * JW + tl * 128 + 128],
                                   wproj_sb[:, dcx * NC * 512 + c * 512: dcx * NC * 512 + (c + 1) * 512],
                                   start=(c == 0), stop=(c == NC - 1))
                    if c == NC - 1:
                        ins.then_inc(s_tn.sem, 1)
                em("tensor", t_pj)
            s_tn.bump()
            wait("vector", s_tn, s_tn.n)
            wait("vector", *x3j)
            if j >= 2 and dcx == 0:
                wait("vector", dch["x1s%d" % (j % 2)], x1s_guard[j % 2])
                wait("vector", s_sp, h_sp_done[j - 2])
            def v_pe(v, j=j, dcx=dcx, bank=bank):
                v.tensor_add(x1t[j % 2][:, dcx * 512:(dcx + 1) * 512], bank[:, 0:512],
                             xbuf[j % 4][:, dcx * 512:(dcx + 1) * 512]).then_inc(s_vp.sem, 1)
            em("vector", v_pe); s_vp.bump()
            projev_n.append(s_vp.n)
        dT_guard[cx % 2] = s_tn.n
        wait("sync", s_vp, projev_n[-1])
        chs = dch["x1s%d" % (j % 2)]
        def sy_x1(sy, j=j, ch=chs):
            sy.dma_start(x1_hbm.ap()[j * 128:(j + 1) * 128, :],
                         x1t[j % 2][:, :]).then_inc(ch.sem, 16)
        em("sync", sy_x1); chs.bump()
        x1s_guard[j % 2] = chs.n
        if j >= 2:
            flush3(j - 2)
        if j >= 1:
            apply3(j - 1)
        wait("scalar", s_vp, projev_n[-1])
        wait("vector", s_vp, projev_n[-1])
        par = j % 2
        def s3_sq(s, j=j, par=par):
            s.activation(sqjunk[:, :], x1t[par][:, :], AF.Square,
                         accum_out=tc("sumsq", par)).then_inc(s_sp.sem, 1)
        em("scalar", s3_sq); s_sp.bump()
        sq3_n = s_sp.n
        def v3_red(v, j=j, par=par):
            v.reduce_sum(tc("sum", par), x1t[par][:, :], axis=AX.X).then_inc(s_vp.sem, 1)
        em("vector", v3_red); s_vp.bump()
        wait("vector", s_vp, s_vp.n)
        def v3_msq(v, par=par):
            v.tensor_scalar(tc("varD", par), tc("sum", par), tc("sum", par),
                            1.0 / D, ALU.mult, ALU.mult).then_inc(s_vp.sem, 1)
        em("vector", v3_msq); s_vp.bump()
        wait("vector", s_vp, s_vp.n)
        wait("vector", s_sp, sq3_n)
        def v3_var(v, par=par):
            v.tensor_sub(tc("varD", par), tc("sumsq", par),
                         tc("varD", par)).then_inc(s_vp.sem, 1)
        em("vector", v3_var); s_vp.bump()
        wait("scalar", s_vp, s_vp.n)
        def s3_ln(s, par=par):
            s.activation(tc("rstd", par), tc("varD", par), AF.Ln,
                         bias=tc("eps11"),
                         scale=(1.0 + EPS_RMS_FFN) / D).then_inc(s_sp.sem, 1)
        em("scalar", s3_ln); s_sp.bump()
        wait("scalar", s_sp, s_sp.n)
        def s3_ex(s, par=par):
            s.activation(tc("rstd", par), tc("rstd", par), AF.Exp,
                         scale=-0.5).then_inc(s_sp.sem, 1)
        em("scalar", s3_ex); s_sp.bump()
        wait("vector", s_sp, s_sp.n)
        def v3_bias(v, par=par):
            v.tensor_scalar(tc("bias", par), tc("sum", par), tc("rstd", par),
                            -1.0 / D, ALU.mult, ALU.mult).then_inc(s_vp.sem, 1)
        em("vector", v3_bias); s_vp.bump()
        bias3_n[j] = s_vp.n
    apply3(NT - 1)
    flush3(NT - 2)
    flush3(NT - 1)

    phase3_tn_end = s_tn.n
    phase3_vp_end = s_vp.n
    phase3_sp_end = s_sp.n
    if dump_sroute is not None:
        wait("sync", s_vp, s_vp.n)
        def sy_sr(sy):
            sy.dma_start(dump_sroute.ap(), s_route[:, :]).then_inc(dch["pre"].sem, 16)
        em("sync", sy_sr); dch["pre"].bump()

    g_n = []
    u_n = []
    sil_n = []
    amul_n = []
    dn_n = []
    dnev_n = []
    yst_n = [0, 0]
    u_half_end = 0

    def wg_src(e, ft, which):
        if e < NE:
            b = wg_in if which == 0 else wu_in
            return b.ap()[e, ft].rearrange("c p f -> p c f")
        b = swg_in if which == 0 else swu_in
        return b.ap()[ft].rearrange("c p f -> p c f")

    hth2_n = [0]
    for hf in range(2):
        if hf == 0:
            chh = dch["hth"]
            wait("sync", dch["hts0"], dch["hts0"].n)
            wait("sync", dch["hts1"], dch["hts1"].n)
            def sy_hth(sy, ch=chh):
                src = hT_hbm.ap()[:, :, 0:HALF].rearrange("c p w -> p c w")
                dst = hT_half.rearrange("q (c w) -> q c w", w=HALF)
                sy.dma_start(dst, src).then_inc(ch.sem, 16)
            em("sync", sy_hth); chh.bump()
            hth_n = chh.n
        else:
            hth_n = hth2_n[0]
        wait("sync", dch["x1s0"], dch["x1s0"].n)
        wait("sync", dch["x1s1"], dch["x1s1"].n)
        if hf == 0:
            wait("sync", s_tn, phase3_tn_end)
        else:
            wait("sync", dch["yst"], yst_n[0])
        def sy_yi(sy, hf=hf):
            src = x1_hbm.ap()[hf * HALF:(hf + 1) * HALF, :]
            src = src.rearrange("(tt p) d -> p tt d", p=128)
            dst = ybuf.rearrange("p (tt d) -> p tt d", d=D)
            sy.dma_start(dst, src).then_inc(dch["yini"].sem, 16)
        em("sync", sy_yi); dch["yini"].bump()
        yini_n = dch["yini"].n

        for e in range(NE + 1):
            eg = hf * (NE + 1) + e
            if eg == 0:
                wait("gpsimd", s_tn, phase3_tn_end)
                wait("gpsimd", s_sp, phase3_sp_end)
                wait("gpsimd", s_vp, phase3_vp_end)
                wait("gpsimd", dch["x1s0"], dch["x1s0"].n)
                wait("gpsimd", dch["x1s1"], dch["x1s1"].n)
                wait("gpsimd", dch["dT0"], dch["dT0"].n)
                wait("gpsimd", dch["dT1"], dch["dT1"].n)
                wait("gpsimd", dch["hts0"], dch["hts0"].n)
                wait("gpsimd", dch["hts1"], dch["hts1"].n)
            wd_emitted = [False]

            def emit_wd(e=e, eg=eg):
                if eg >= 1:
                    wait("gpsimd", s_tn, dn_n[-1])
                def g_wd(g, e=e):
                    if e < NE:
                        src = wd_in.ap()[e].rearrange("(fc p) d -> p fc d", p=128)
                    else:
                        src = swd_in.ap().rearrange("(fc p) d -> p fc d", p=128)
                    g.dma_start(wd_sb.rearrange("p (fc d) -> p fc d", d=D),
                                src).then_inc(dch["wd"].sem, 16)
                em("gpsimd", g_wd); dch["wd"].bump()
            for ft in range(NF):
                blk = eg * NF + ft
                chb = dch["wg%d" % (blk % 2)]
                if blk >= 2:
                    wait("gpsimd", s_tn, u_n[(blk - 2 + 1) * n_tch - 1])
                def g_blk(g, e=e, ft=ft, blk=blk, ch=chb):
                    g.dma_start(wgu_st[0][blk % 2].rearrange("p (c f) -> p c f", f=128),
                                wg_src(e, ft, 0)).then_inc(ch.sem, 16)
                    g.dma_start(wgu_st[1][blk % 2].rearrange("p (c f) -> p c f", f=128),
                                wg_src(e, ft, 1)).then_inc(ch.sem, 16)
                em("gpsimd", g_blk)
                chb.bump(); chb.bump()
                blk_n = chb.n
                if ft == 1 and not wd_emitted[0]:
                    emit_wd()
                    wd_emitted[0] = True
                wait("tensor", chb, blk_n)
                wait("tensor", dch["hth"], hth_n)
                for tcx in range(n_tch):
                    kk = blk * n_tch + tcx
                    bg = bank512[kk % 2]
                    bu = bank512[2 + kk % 2]
                    if len(sil_n) >= 2:
                        wait("tensor", s_sp, sil_n[-2])
                    if len(amul_n) >= 2:
                        wait("tensor", s_vp, amul_n[-2])
                    if kk == 0:
                        wait("tensor", s_vp, phase3_vp_end)
                        wait("tensor", s_sp, phase3_sp_end)
                    for c in range(NC):
                        def t_g(t, c=c, tcx=tcx, blk=blk, bg=bg):
                            ins = t.matmul(bg[:, 0:TCW],
                                           wgu_st[0][blk % 2][:, c * 128:(c + 1) * 128],
                                           hT_half[:, c * HALF + tcx * TCW: c * HALF + tcx * TCW + TCW],
                                           start=(c == 0), stop=(c == NC - 1))
                            if c == NC - 1:
                                ins.then_inc(s_tn.sem, 1)
                        em("tensor", t_g)
                    s_tn.bump()
                    g_n.append(s_tn.n)
                    for c in range(NC):
                        def t_u(t, c=c, tcx=tcx, blk=blk, bu=bu):
                            ins = t.matmul(bu[:, 0:TCW],
                                           wgu_st[1][blk % 2][:, c * 128:(c + 1) * 128],
                                           hT_half[:, c * HALF + tcx * TCW: c * HALF + tcx * TCW + TCW],
                                           start=(c == 0), stop=(c == NC - 1))
                            if c == NC - 1:
                                ins.then_inc(s_tn.sem, 1)
                        em("tensor", t_u)
                    s_tn.bump()
                    u_n.append(s_tn.n)
                    wait("scalar", s_tn, g_n[-1])
                    spar = len(sil_n) % 2
                    if not sil_n:
                        wait("scalar", dch["hts0"], dch["hts0"].n)
                        wait("scalar", dch["hts1"], dch["hts1"].n)
                    if len(sil_n) >= 2:
                        wait("scalar", s_vp, amul_n[-2])
                    def s_sil(s, spar=spar, bg=bg):
                        s.activation(sgscr[spar][:, :], bg[:, 0:TCW],
                                     AF.Silu).then_inc(s_sp.sem, 1)
                    em("scalar", s_sil); s_sp.bump()
                    sil_n.append(s_sp.n)
                    wait("vector", s_sp, sil_n[-1])
                    wait("vector", s_tn, u_n[-1])
                    if eg >= 1 and ft == 0 and tcx == 0:
                        wait("vector", s_tn, dn_n[-1])
                    if eg == 0 and ft == 0 and tcx == 0:
                        wait("vector", dch["hts0"], dch["hts0"].n)
                        wait("vector", dch["hts1"], dch["hts1"].n)
                        wait("vector", s_tn, phase3_tn_end)
                        wait("vector", dch["dT0"], dch["dT0"].n)
                        wait("vector", dch["dT1"], dch["dT1"].n)
                    def v_am(v, ft=ft, tcx=tcx, spar=spar, bu=bu):
                        v.tensor_mul(abuf[:, ft * HALF + tcx * TCW: ft * HALF + tcx * TCW + TCW],
                                     sgscr[spar][:, :], bu[:, 0:TCW]).then_inc(s_vp.sem, 1)
                    em("vector", v_am); s_vp.bump()
                    amul_n.append(s_vp.n)
            a_ready = s_vp.n
            wait("tensor", dch["wd"], dch["wd"].n)
            wait("tensor", s_vp, a_ready)
            for tl in range(NTH):
                for dcx in range(4):
                    kk = len(dn_n)
                    bank = bank512[4 + kk % 2]
                    if kk >= 2:
                        wait("tensor", s_vp, dnev_n[kk - 2])
                    if kk == 0:
                        wait("tensor", s_vp, phase3_vp_end)
                    for fc in range(NF):
                        def t_dn(t, fc=fc, tl=tl, dcx=dcx, eg=eg, bank=bank):
                            ins = t.matmul(bank[:, 0:512],
                                           abuf[:, fc * HALF + tl * 128: fc * HALF + tl * 128 + 128],
                                           wd_sb[:, fc * D + dcx * 512: fc * D + (dcx + 1) * 512],
                                           start=(fc == 0), stop=(fc == NF - 1))
                            if fc == NF - 1:
                                ins.then_inc(s_tn.sem, 1)
                        em("tensor", t_dn)
                    s_tn.bump()
                    dn_n.append(s_tn.n)
                    wait("vector", s_tn, dn_n[-1])
                    if e == 0 and tl == 0 and dcx == 0:
                        wait("vector", dch["yini"], yini_n)
                    def v_dne(v, e=e, tl=tl, dcx=dcx, hf=hf, bank=bank):
                        ysl = ybuf[:, tl * D + dcx * 512: tl * D + (dcx + 1) * 512]
                        if e < NE:
                            cix = (hf * NTH + tl) * 8 + e
                            v.scalar_tensor_tensor(ysl, bank[:, 0:512],
                                                   s_route[:, cix:cix + 1], ysl,
                                                   ALU.mult, ALU.add).then_inc(s_vp.sem, 1)
                        else:
                            v.tensor_add(ysl, ysl, bank[:, 0:512]).then_inc(s_vp.sem, 1)
                    em("vector", v_dne); s_vp.bump()
                    dnev_n.append(s_vp.n)
            if hf == 0 and e == NE:
                u_half_end = u_n[-1]
        if hf == 0:
            chh2 = dch["hth"]
            wait("sync", s_tn, u_n[-1])
            def sy_hth2(sy, ch=chh2):
                src_ = hT_hbm.ap()[:, :, HALF:2 * HALF].rearrange("c p w -> p c w")
                dst_ = hT_half.rearrange("q (c w) -> q c w", w=HALF)
                sy.dma_start(dst_, src_).then_inc(ch.sem, 16)
            em("sync", sy_hth2); chh2.bump()
            hth2_n[0] = chh2.n
        wait("sync", s_vp, dnev_n[-1])
        def sy_yst(sy, hf=hf):
            src = ybuf.rearrange("p (tt d) -> p tt d", d=D)
            dst = out_t.ap()[hf * HALF:(hf + 1) * HALF, :]
            dst = dst.rearrange("(tt p) d -> p tt d", p=128)
            sy.dma_start(dst, src).then_inc(dch["yst"].sem, 16)
        em("sync", sy_yst); dch["yst"].bump()
        yst_n[hf] = dch["yst"].n

    for nm, ch in dch.items():
        if nm != "yst" and ch.n > 0:
            wait("sync", ch, ch.n)
    wait("sync", dch["yst"], dch["yst"].n)

    with nc.Block() as block:
        @block.tensor
        def _(e):
            for f in ops["tensor"]:
                f(e)

        @block.vector
        def _(e):
            for f in ops["vector"]:
                f(e)

        @block.scalar
        def _(e):
            for f in ops["scalar"]:
                f(e)

        @block.gpsimd
        def _(e):
            for f in ops["gpsimd"]:
                f(e)

        @block.sync
        def _(e):
            for f in ops["sync"]:
                f(e)

    return nc


def _prep_host(inputs):
    f32 = lambda a: np.ascontiguousarray(np.asarray(a, dtype=np.float32))
    x = f32(inputs["L0_attn_x"])
    rmsw = np.ascontiguousarray(
        np.broadcast_to(f32(inputs["L0_ffn_rms_weight"])[None, :], (128, D)))
    lam = np.ascontiguousarray(np.concatenate([
        np.broadcast_to(f32(inputs[k])[None, :], (128, DH))
        for k in ("L0_attn_lambda_q1", "L0_attn_lambda_k1",
                  "L0_attn_lambda_q2", "L0_attn_lambda_k2")], axis=1))
    tile_gu = lambda w: np.ascontiguousarray(
        w.reshape(-1, NC, 128, NF, 128).transpose(0, 3, 1, 2, 4))
    wg_t = tile_gu(f32(inputs["L0_ffn_w_gate"]))
    wu_t = tile_gu(f32(inputs["L0_ffn_w_up"]))
    swg_t = np.ascontiguousarray(
        f32(inputs["L0_ffn_sw_gate"]).reshape(NC, 128, NF, 128).transpose(2, 0, 1, 3))
    swu_t = np.ascontiguousarray(
        f32(inputs["L0_ffn_sw_up"]).reshape(NC, 128, NF, 128).transpose(2, 0, 1, 3))
    common = dict(wqkv=f32(inputs["L0_attn_Wqkv"]),
                  wproj=f32(inputs["L0_attn_Wproj"]),
                  rmsw_in=rmsw, lam_in=lam,
                  routw_in=f32(inputs["L0_ffn_router_w"]),
                  wg_in=wg_t, wu_in=wu_t,
                  wd_in=f32(inputs["L0_ffn_w_down"]),
                  swg_in=swg_t, swu_in=swu_t,
                  swd_in=f32(inputs["L0_ffn_sw_down"]))
    return x, common


def kernel(**inputs):
    x, common = _prep_host(inputs)
    B = x.shape[0]
    nc = build(T=x.shape[1])
    in_maps = []
    for b in range(B):
        m = dict(common)
        m["x_in"] = np.ascontiguousarray(x[b])
        in_maps.append(m)
    res = run_bass_kernel_spmd(nc, in_maps, core_ids=list(range(B)))
    out = np.stack([np.asarray(res.results[b]["out"]) for b in range(B)], axis=0)
    return out.astype(np.float32)


if __name__ == "__main__":
    nc = build()
    print("instructions:", nc.n_instructions())


# version 16
# speedup vs baseline: 1.1079x; 1.0878x over previous
import math
import numpy as np
import concourse.bass as bass
import concourse.mybir as mybir
from concourse.bass_utils import run_bass_kernel_spmd

F32 = mybir.dt.float32
BF16 = mybir.dt.bfloat16
AF = mybir.ActivationFunctionType
ALU = mybir.AluOpType
AX = mybir.AxisListType

D = 2048
NH = 16
DH = 64
DV = 128
FF = 1024
NE = 8
NF = FF // 128
NC = D // 128
LAMBDA_INIT = 0.8 - 0.6 * math.exp(-0.3)
EPS_LN = 1e-5
EPS_RMS_ATTN = 1e-5
EPS_RMS_FFN = 1e-6
VCH = 272


class Ch:
    def __init__(self, nc, name, step=1):
        self.sem = nc.alloc_semaphore(name)
        self.n = 0
        self.step = step

    def bump(self):
        self.n += self.step
        return self.n


def build(T=2048, dumps=()):
    nc = bass.Bass("TRN2", target_bir_lowering=False)
    NT = T // 128
    JW = min(512, T)
    NJ = T // JW
    HALF = T // 2
    NTH = HALF // 128
    TCW = min(512, HALF)
    n_tch = HALF // TCW
    QH = D // 2

    x_in = nc.dram_tensor("x_in", [T, D], F32, kind="ExternalInput")
    wqkv = nc.dram_tensor("wqkv", [D, 3 * D], F32, kind="ExternalInput")
    wproj = nc.dram_tensor("wproj", [D, D], F32, kind="ExternalInput")
    rmsw_in = nc.dram_tensor("rmsw_in", [128, D], F32, kind="ExternalInput")
    lam_in = nc.dram_tensor("lam_in", [128, 4 * DH], F32, kind="ExternalInput")
    routw_in = nc.dram_tensor("routw_in", [D, NE], F32, kind="ExternalInput")
    wg_in = nc.dram_tensor("wg_in", [NE, NF, NC, 128, 128], F32, kind="ExternalInput")
    wu_in = nc.dram_tensor("wu_in", [NE, NF, NC, 128, 128], F32, kind="ExternalInput")
    wd_in = nc.dram_tensor("wd_in", [NE, FF, D], F32, kind="ExternalInput")
    swg_in = nc.dram_tensor("swg_in", [NF, NC, 128, 128], F32, kind="ExternalInput")
    swu_in = nc.dram_tensor("swu_in", [NF, NC, 128, 128], F32, kind="ExternalInput")
    swd_in = nc.dram_tensor("swd_in", [FF, D], F32, kind="ExternalInput")
    out_t = nc.dram_tensor("out", [T, D], F32, kind="ExternalOutput")

    sk = lambda nm: "ExternalOutput" if nm in dumps else "Internal"
    lnT_hbm = nc.dram_tensor("lnT_hbm", [NC, 128, T], BF16, kind=sk("lnT_hbm"))
    dnT_hbm = nc.dram_tensor("dnT_hbm", [NC, 128, T], BF16, kind=sk("dnT_hbm"))
    hT_hbm = nc.dram_tensor("hT_hbm", [NC, 128, T], BF16, kind=sk("hT_hbm"))
    x1_hbm = nc.dram_tensor("x1_hbm", [T, D], F32, kind=sk("x1_hbm"))
    dump_sroute = None
    if "s_route" in dumps:
        dump_sroute = nc.dram_tensor("s_route_d", [128, NT * 8], F32,
                                     kind="ExternalOutput")

    ident = nc.alloc_sbuf_tensor("ident", [128, 128], BF16).ap()
    mask01 = nc.alloc_sbuf_tensor("mask01", [128, 128], BF16).ap()
    rmsw = nc.alloc_sbuf_tensor("rmsw", [128, D], F32).ap()
    lamv = nc.alloc_sbuf_tensor("lamv", [128, 4 * DH], F32).ap()
    tiny = nc.alloc_sbuf_tensor("tiny", [128, 128], F32).ap()
    i8 = nc.alloc_sbuf_tensor("i8", [128, 48], F32).ap()
    routw = nc.alloc_sbuf_tensor("routw", [128, NC * 8], BF16).ap()
    s_route = nc.alloc_sbuf_tensor("s_route", [128, NT * 8], F32).ap()
    NSL = 16
    dnscr = [nc.alloc_sbuf_tensor(f"dnscr{i}", [128, 128], BF16).ap() for i in range(NSL)]
    asb = [nc.alloc_sbuf_tensor(f"asb{i}", [128, 264], F32).ap() for i in range(NSL)]
    sqjunk = nc.alloc_sbuf_tensor("sqjunk", [128, D], BF16).ap()

    GCOL = dict(lam=0, dot1=1, dot2=2, e1=3, e2=4, m1=5, m2=6, dlt=7, w1=8,
                w2=9, eps_ln=10, eps_ffn=11, eps11=12)
    LNCOL = dict(sum=0, sumsq=1, varD=2, rstd=3, bias=4, vv=5, ta=6, q=7, r1b=8)

    def tc(name, par=0):
        if name in GCOL:
            c = GCOL[name]
        else:
            c = 32 + 16 * par + LNCOL[name]
        return tiny[:, c:c + 1]

    def ln1c(name, p):
        c = 64 + 16 * p + LNCOL[name]
        return tiny[:, c:c + 1]

    def t2(name, sl):
        c = 64 + 4 * sl + {"r1": 0, "r2": 1, "ms": 2, "rstd2": 3}[name]
        return tiny[:, c:c + 1]

    base = nc.sbuf_base
    avail = nc.sbuf_top - base

    def at(name, shape, dtype, off):
        sz = shape[1] * (4 if dtype == F32 else 2)
        assert off >= 0 and off + sz <= avail, (name, off, sz, avail)
        return nc.alloc_sbuf_tensor_at(name, shape, dtype,
                                       offset=base + off).ap()

    lnTsz = NC * JW * 2
    o = 0
    lnTc = [at(f"lnTc{i}", [128, NC * JW], BF16, o + i * lnTsz) for i in range(2)]
    o += 2 * lnTsz
    o_wqk = o
    wqk_st = [at(f"wqk{i}", [128, NC * 512], BF16, o + i * NC * 1024) for i in range(2)]
    o += 2 * NC * 1024
    o_wv = o
    wv_st = [at(f"wv{i}", [128, NC * 256], BF16, o + i * NC * 512) for i in range(2)]
    o += 2 * NC * 512
    o_qkT = o
    qkT = at("qkT", [128, 4 * T], BF16, o)
    o += 4 * T * 2
    o_pT = o
    pT = []
    for a in range(2):
        pT.append(at(f"pT{a}", [128, NT * 2 * JW], BF16, o))
        o += NT * 2 * JW * 2
    o_vbuf = o
    vbuf = at("vbuf", [128, NT * VCH], BF16, o)
    o += NT * VCH * 2
    dstage = at("dstage", [128, 2 * JW], BF16, o)
    o += 2 * JW * 2
    p12_top = o
    o_dstage_end = o
    if 2 * D * 2 <= o_dstage_end - o_vbuf:
        o_lnscr = o_vbuf
    else:
        o_lnscr = p12_top
        p12_top += 2 * D * 2
    lnscr = [at(f"lnscr{i}", [128, D], BF16, o_lnscr + i * D * 2) for i in range(2)]
    if 2 * NC * 256 <= 2 * NT * JW * 2:
        o_lnts = o_pT + 2 * NT * JW * 2
    else:
        o_lnts = p12_top
        p12_top += 2 * NC * 256
    lnT_stage = [at(f"lnTs{i}", [128, NC * 128], BF16,
                    o_lnts + i * NC * 256) for i in range(2)]

    wproj_sb = at("wproj_sb", [128, 4 * NC * 512], BF16, 0)
    o3 = 4 * NC * 512 * 2
    o_dT = o3
    dTbuf = [at(f"dT{i}", [128, NC * JW], BF16, o3 + i * lnTsz) for i in range(2)]
    o3 += 2 * lnTsz
    o_hth = o3
    hT_half = at("hT_half", [128, NC * HALF], BF16, o3)
    o3 += NC * HALF * 2
    o_x1t = o3
    x1t = [at(f"x1t{i}", [128, D], F32, o3 + i * D * 4) for i in range(2)]
    o3 += 2 * D * 4
    hbuf = [at(f"hbuf{i}", [128, D], BF16, o3 + i * D * 2) for i in range(2)]
    o3 += 2 * D * 2
    o_hTs = o3
    hTstage = [at(f"hTs{i}", [128, NC * 128], BF16, o3 + i * NC * 256) for i in range(2)]
    o3 += 2 * NC * 256
    p3_top = o3

    ysz = NTH * D * 4
    assert ysz <= o_dT, (ysz, o_dT)
    ybuf = at("ybuf", [128, NTH * D], F32, 0)
    asz = NF * HALF * 2
    sgsz = 2 * TCW * 2
    if asz + 4 * NC * 256 <= o_hth - o_dT:
        o_a = o_dT
        o_wgu = o_dT + asz
        o_wd = o_x1t
        if o_wgu + 4 * NC * 256 + sgsz <= o_hth:
            o_sg = o_wgu + 4 * NC * 256
        else:
            o_sg = o_wd + NF * D * 2
        assert o_wgu + 4 * NC * 256 <= o_hth, (o_wgu, o_hth)
        assert o_wd >= o_hth + NC * HALF * 2
    else:
        o_wd = o_x1t
        o_sg = o_wd + NF * D * 2
        o_a = max(p12_top, p3_top, o_sg + sgsz)
        o_wgu = o_a + asz
    abuf = at("abuf", [128, NF * HALF], BF16, o_a)
    wgu_st = []
    for nmi, nm in enumerate(("wgst", "wust")):
        wgu_st.append([at(f"{nm}{i}", [128, NC * 128], BF16,
                          o_wgu + (2 * nmi + i) * NC * 256) for i in range(2)])
    wd_sb = at("wd_sb", [128, NF * D], BF16, o_wd)
    sgscr = [at(f"sg{i}", [128, TCW], BF16, o_sg + i * TCW * 2) for i in range(2)]
    p4_top = max(o_sg + sgsz, o_wgu + 4 * NC * 256, o_wd + NF * D * 2)
    if 4 * D * 4 <= 2 * NT * 2 * JW * 2 and o_pT == o_hth and 4 * D * 4 <= NC * HALF * 2:
        o_xb = o_pT
    else:
        o_xb = avail - 4 * D * 4
        assert o_xb >= max(p12_top, p3_top, p4_top), \
            (o_xb, p12_top, p3_top, p4_top)
    xbuf = [at(f"xbuf{i}", [128, D], F32, o_xb + i * D * 4) for i in range(4)]
    assert max(p12_top, p3_top, p4_top) <= avail, (p12_top, p3_top, p4_top, avail)

    pA = nc.alloc_psum_tensor("pA", [128, 1024], F32).ap()
    pB = nc.alloc_psum_tensor("pB", [128, 1024], F32).ap()
    pCD = nc.alloc_psum_tensor("pCD", [128, 1024], F32).ap()
    pbf = [nc.alloc_psum_tensor(f"pbf{i}", [128, 1024], BF16).ap() for i in range(2)]
    bank512 = [pA[:, 0:512], pA[:, 512:1024], pB[:, 0:512], pB[:, 512:1024],
               pCD[:, 0:512], pCD[:, 512:1024]]
    spair = [pA, pB]

    s_init = Ch(nc, "s_init")
    s_tn = Ch(nc, "s_tn")
    s_vp = Ch(nc, "s_vp")
    s_sp = Ch(nc, "s_sp")
    dch = {}
    for nm in ("xl0", "xl1", "xl2", "xl3", "lns0", "lns1", "lnc0", "lnc1", "wq0", "wq1",
               "dst0", "dst1", "wp", "rw", "dT0", "dT1", "x30", "x31", "x32", "x33", "x1s0",
               "x1s1", "hts0", "hts1", "hth", "wg0", "wg1", "wd", "yini",
               "yst", "pre"):
        dch[nm] = Ch(nc, "d_" + nm, 16)

    ops = {k: [] for k in ("tensor", "vector", "scalar", "gpsimd", "sync")}

    def em(eng, fn):
        ops[eng].append(fn)

    last_wait = {}

    def wait(eng, ch, val):
        assert val <= ch.n, (eng, val, ch.n)
        key = (eng, ch.sem.num)
        if val <= last_wait.get(key, 0):
            return
        last_wait[key] = val
        em(eng, lambda e, s=ch.sem, v=val: e.wait_ge(s, v))

    def g_init(g):
        steps = [
            lambda: g.memset(ident[:, :], 0.0),
            lambda: g.affine_select(out=ident[:, :], in_=ident[:, :],
                                    compare_op=ALU.not_equal, fill=1.0, base=0,
                                    pattern=[[-1, 128]], channel_multiplier=1),
            lambda: g.memset(mask01[:, :], 0.0),
            lambda: g.affine_select(out=mask01[:, :], in_=mask01[:, :],
                                    compare_op=ALU.is_gt, fill=1.0, base=0,
                                    pattern=[[-1, 128]], channel_multiplier=1),
        ]
        n = 0
        for st in steps:
            st().then_inc(s_init.sem, 1)
            n += 1
            g.wait_ge(s_init.sem, n)
    em("gpsimd", g_init)
    for _ in range(4):
        s_init.bump()
    init_done = s_init.n

    def sy_pre(sy):
        sy.dma_start(rmsw[:, :], rmsw_in.ap()).then_inc(dch["pre"].sem, 16)
        sy.dma_start(lamv[:, :], lam_in.ap()).then_inc(dch["pre"].sem, 16)
    em("sync", sy_pre)
    dch["pre"].bump(); dch["pre"].bump()
    pre_done = dch["pre"].n

    def g_rw(g):
        g.dma_start(routw.rearrange("p (c e) -> p c e", e=8),
                    routw_in.ap().rearrange("(c p) e -> p c e", p=128)
                    ).then_inc(dch["rw"].sem, 16)
    em("gpsimd", g_rw)
    dch["rw"].bump()
    routw_done = dch["rw"].n

    def v_consts(v):
        v.memset(tc("eps_ln"), EPS_LN)
        v.memset(tc("eps_ffn"), EPS_RMS_FFN)
        v.memset(tc("eps11"), EPS_LN * EPS_RMS_FFN).then_inc(s_vp.sem, 1)
    em("vector", v_consts); s_vp.bump()
    wait("vector", s_vp, s_vp.n)

    wait("vector", dch["pre"], pre_done)
    wait("vector", s_init, init_done)

    def vchain(v_ops):
        for fn in v_ops:
            em("vector", fn)
            s_vp.bump()
            wait("vector", s_vp, s_vp.n)

    vchain([
        lambda v: v.tensor_mul(lamv[:, 0:DH], lamv[:, 0:DH],
                               lamv[:, DH:2 * DH]).then_inc(s_vp.sem, 1),
        lambda v: v.reduce_sum(tc("dot1"), lamv[:, 0:DH], axis=AX.X).then_inc(s_vp.sem, 1),
        lambda v: v.tensor_mul(lamv[:, 2 * DH:3 * DH], lamv[:, 2 * DH:3 * DH],
                               lamv[:, 3 * DH:4 * DH]).then_inc(s_vp.sem, 1),
        lambda v: v.reduce_sum(tc("dot2"), lamv[:, 2 * DH:3 * DH],
                               axis=AX.X).then_inc(s_vp.sem, 1),
    ])
    wait("scalar", s_vp, s_vp.n)

    def s_lam(s):
        s.activation(tc("e1"), tc("dot1"), AF.Exp).then_inc(s_sp.sem, 1)
        s.activation(tc("e2"), tc("dot2"), AF.Exp).then_inc(s_sp.sem, 1)
    em("scalar", s_lam); s_sp.bump(); s_sp.bump()
    wait("vector", s_sp, s_sp.n)
    vchain([
        lambda v: v.tensor_sub(tc("lam"), tc("e1"), tc("e2")).then_inc(s_vp.sem, 1),
        lambda v: v.tensor_single_scalar(tc("lam"), tc("lam"), LAMBDA_INIT, ALU.add).then_inc(s_vp.sem, 1),
    ])
    lam_done_v = s_vp.n

    tp_n = [0]
    tpe_done = []

    def transpose_tile(src, dst_stage, first_evac_guards=(), dst_fn=None):
        for c0 in range(0, NC, 8):
            nn = min(8, NC - c0)
            k = tp_n[0]
            if k >= 2:
                wait("tensor", s_vp, tpe_done[k - 2])
            for ci in range(nn):
                def t_tr(t, src=src, c=c0 + ci, ci=ci, k=k, last=(ci == nn - 1)):
                    ins = t.transpose(pbf[k % 2][:, ci * 128:(ci + 1) * 128],
                                      src[:, c * 128:(c + 1) * 128], ident[:, :])
                    if last:
                        ins.then_inc(s_tn.sem, 1)
                em("tensor", t_tr)
            s_tn.bump()
            wait("vector", s_tn, s_tn.n)
            if c0 == 0:
                for g in first_evac_guards:
                    wait("vector", *g)
            if dst_fn is not None:
                for ci in range(nn):
                    def v_ev(v, dst=dst_fn(c0 + ci), ci=ci, k=k,
                             last=(ci == nn - 1)):
                        ins = v.tensor_copy(dst, pbf[k % 2][:, ci * 128:(ci + 1) * 128])
                        if last:
                            ins.then_inc(s_vp.sem, 1)
                    em("vector", v_ev)
                s_vp.bump()
            else:
                def v_ev(v, dst=dst_stage[:, c0 * 128:(c0 + nn) * 128], k=k, nn=nn):
                    v.tensor_copy(dst, pbf[k % 2][:, 0:nn * 128]).then_inc(s_vp.sem, 1)
                em("vector", v_ev); s_vp.bump()
            tpe_done.append(s_vp.n)
            tp_n[0] += 1

    wait("tensor", s_init, init_done)
    app_done = {}
    red_done = {}
    lnscr_free = {}
    lns_guard = [0, 0]
    lnTc_ready = [0, 0]
    xl_n = {}
    bsn1 = {}

    def p1_loads(grp):
        for i in grp:
            chx = dch["xl%d" % (i % 4)]
            if i >= 4:
                wait("sync", s_vp, app_done[i - 4])
                wait("sync", s_vp, red_done[i - 4])
            def sy_x(sy, i=i, ch=chx):
                sy.dma_start(xbuf[i % 4][:, :],
                             x_in.ap()[i * 128:(i + 1) * 128, :]).then_inc(ch.sem, 16)
            em("sync", sy_x); chx.bump()
            xl_n[i] = (chx, chx.n)

    def p1_stats(grp):
        sqn = {}; stv = {}
        for i in grp:
            wait("scalar", *xl_n[i])
            if i > 0:
                wait("scalar", s_sp, s_sp.n)
            def s_sq(s, i=i):
                s.activation(sqjunk[:, :], xbuf[i % 4][:, :], AF.Square,
                             accum_out=ln1c("sumsq", i % 4)).then_inc(s_sp.sem, 1)
            em("scalar", s_sq); s_sp.bump()
            sqn[i] = s_sp.n
        for i in grp:
            wait("vector", *xl_n[i])
            def v_red(v, i=i):
                v.reduce_sum(ln1c("sum", i % 4), xbuf[i % 4][:, :],
                             axis=AX.X).then_inc(s_vp.sem, 1)
            em("vector", v_red); s_vp.bump()
            red_done[i] = s_vp.n
        for i in grp:
            wait("vector", s_vp, red_done[i])
            def v_msq(v, i=i):
                v.tensor_scalar(ln1c("varD", i % 4), ln1c("sum", i % 4),
                                ln1c("sum", i % 4), 1.0 / D, ALU.mult,
                                ALU.mult).then_inc(s_vp.sem, 1)
            em("vector", v_msq); s_vp.bump()
            stv[i] = s_vp.n
        for i in grp:
            wait("vector", s_vp, stv[i])
            wait("vector", s_sp, sqn[i])
            def v_var(v, i=i):
                v.tensor_sub(ln1c("varD", i % 4), ln1c("sumsq", i % 4),
                             ln1c("varD", i % 4)).then_inc(s_vp.sem, 1)
            em("vector", v_var); s_vp.bump()
            stv[i] = s_vp.n
        lnn = {}
        for i in grp:
            wait("scalar", s_vp, stv[i])
            def s_ln(s, i=i):
                s.activation(ln1c("rstd", i % 4), ln1c("varD", i % 4), AF.Ln,
                             bias=tc("eps_ln"), scale=1.0 / D).then_inc(s_sp.sem, 1)
            em("scalar", s_ln); s_sp.bump()
            lnn[i] = s_sp.n
        for i in grp:
            wait("scalar", s_sp, lnn[i])
            def s_ex(s, i=i):
                s.activation(ln1c("rstd", i % 4), ln1c("rstd", i % 4), AF.Exp,
                             scale=-0.5).then_inc(s_sp.sem, 1)
            em("scalar", s_ex); s_sp.bump()
            lnn[i] = s_sp.n
        for i in grp:
            wait("vector", s_sp, lnn[i])
            def v_bias(v, i=i):
                v.tensor_scalar(ln1c("bias", i % 4), ln1c("sum", i % 4),
                                ln1c("rstd", i % 4), -1.0 / D, ALU.mult,
                                ALU.mult).then_inc(s_vp.sem, 1)
            em("vector", v_bias); s_vp.bump()
            bsn1[i] = s_vp.n

    def p1_apply_tr(grp):
        for i in grp:
            wait("vector", s_vp, bsn1[i])
            if i >= 2:
                wait("vector", s_tn, lnscr_free[i - 2])
            def v_ap(v, i=i):
                v.tensor_scalar(lnscr[i % 2][:, :], xbuf[i % 4][:, :],
                                ln1c("rstd", i % 4), ln1c("bias", i % 4),
                                ALU.mult, ALU.add).then_inc(s_vp.sem, 1)
            em("vector", v_ap); s_vp.bump()
            app_done[i] = s_vp.n
            wait("tensor", s_vp, app_done[i])
            direct = (i < 8 and JW == 512 and NJ >= 4)
            if direct:
                transpose_tile(
                    lnscr[i % 2], None,
                    dst_fn=lambda c, i=i: lnTc[i // 4][
                        :, c * JW + (i % 4) * 128: c * JW + (i % 4) * 128 + 128])
                lnTc_ready[i // 4] = s_vp.n
            else:
                gmin = 10 if (JW == 512 and NJ >= 4) else 2
                g = [(dch["lns%d" % (i % 2)], lns_guard[i % 2])] if i >= gmin else []
                transpose_tile(lnscr[i % 2], lnT_stage[i % 2], first_evac_guards=g)
            lnscr_free[i] = s_tn.n
            wait("sync", s_vp, s_vp.n)
            chs = dch["lns%d" % (i % 2)]
            def sy_st(sy, i=i, ch=chs, direct=direct):
                dst = lnT_hbm.ap()[:, :, i * 128:(i + 1) * 128].rearrange("c p w -> p c w")
                if direct:
                    src = lnTc[i // 4].rearrange("q (c w) -> q c w", w=JW)[
                        :, :, (i % 4) * 128:(i % 4) * 128 + 128]
                else:
                    src = lnT_stage[i % 2].rearrange("p (c w) -> p c w", w=128)
                sy.dma_start(dst, src).then_inc(ch.sem, 16)
            em("sync", sy_st); chs.bump()
            lns_guard[i % 2] = chs.n

    pend1 = None
    for g0 in range(0, NT, 4):
        grp = list(range(g0, min(g0 + 4, NT)))
        if pend1 is not None:
            p1_apply_tr(pend1)
        p1_loads(grp)
        p1_stats(grp)
        pend1 = grp
    p1_apply_tr(pend1)
    phase1_vp_end = s_vp.n

    wait("gpsimd", s_tn, lnscr_free[NT - 1])
    def g_vinit(g):
        vr = vbuf.rearrange("p (c w) -> p c w", w=VCH)
        for i, st in enumerate([
            lambda: g.memset(vbuf[:, :], 0.0),
            lambda: g.memset(vr[:, :, 128:129], 1.0),
            lambda: g.memset(vr[:, :, 264:265], 1.0),
        ]):
            st().then_inc(s_init.sem, 1)
            g.wait_ge(s_init.sem, init_done + i + 1)
    em("gpsimd", g_vinit)
    for _ in range(3):
        s_init.bump()
    vbuf_ready = s_init.n

    col0 = {"q1": 0, "k1": QH, "q2": 2 * QH, "k2": 3 * QH}
    wq_pair_n = []
    wfree_n = []
    lnc_guard = [0, 0]
    qk_evac_n = []
    v_evac_n = []
    s_ev_n = []
    s_cnt = [0]
    qk_cnt = [0]
    v_cnt = [0]
    pv_hist = [0]
    vd1_n = [0]
    dtr_slot = {i: 0 for i in range(NSL)}
    ln5_guard = {i: 0 for i in range(NSL)}
    st7_guard = {i: 0 for i in range(NSL)}
    dst_guard = [0, 0]
    pv_groups = [0]
    gslot = [0]
    pend = []
    pend_sc = [None]

    def flush_chunk(hbase, pj, dnn, slots):
        ntile = JW // 128
        k = tp_n[0]
        if k >= 2:
            wait("tensor", s_vp, tpe_done[k - 2])
        wait("tensor", s_vp, dnn[-1])
        nsl_c = len(slots)
        for hl in range(2):
            for tl in range(ntile):
                sl = slots[tl * 2 + hl]
                si = hl * ntile + tl
                def t_dtr(t, sl=sl, si=si, k=k, last=(hl == 1 and tl == ntile - 1)):
                    ins = t.transpose(pbf[k % 2][:, si * 128:(si + 1) * 128],
                                      dnscr[sl][:, :], ident[:, :])
                    if last:
                        ins.then_inc(s_tn.sem, 1)
                em("tensor", t_dtr)
                dtr_slot[sl] = s_tn.n + 1
        s_tn.bump()
        wait("vector", s_tn, s_tn.n)
        wait("vector", dch["dst0"], dst_guard[0])
        wait("vector", dch["dst1"], dst_guard[1])
        def v_dst(v, k=k, nsl_c=nsl_c):
            v.tensor_copy(dstage[:, 0:nsl_c * 128],
                          pbf[k % 2][:, 0:nsl_c * 128]).then_inc(s_vp.sem, 1)
        em("vector", v_dst); s_vp.bump()
        tpe_done.append(s_vp.n)
        tp_n[0] += 1
        wait("gpsimd", s_vp, s_vp.n)
        for hl in range(2):
            chd = dch["dst%d" % hl]
            def sy_dst(sy, hbase=hbase, hl=hl, pj=pj, ch=chd, ntile=ntile):
                sy.dma_start(dnT_hbm.ap()[hbase + hl, :, pj * JW:(pj + 1) * JW],
                             dstage[:, hl * ntile * 128:(hl + 1) * ntile * 128]
                             ).then_inc(ch.sem, 16)
            em("gpsimd", sy_dst); chd.bump()
            dst_guard[hl] = chd.n

    def emit_chain_tail(hbase, j, slots, st4v):
        nidx = len(slots)
        ln5 = []
        for ix in range(nidx):
            sl = slots[ix]
            wait("scalar", s_vp, st4v[ix])
            wait("scalar", s_vp, st7_guard[sl])
            def s5(s, sl=sl):
                s.activation(t2("rstd2", sl), t2("ms", sl), AF.Ln,
                             bias=tc("eps_ln"), scale=1.0 / DV).then_inc(s_sp.sem, 1)
            em("scalar", s5); s_sp.bump()
            ln5_guard[sl] = s_sp.n
            ln5.append(s_sp.n)
        ex6 = []
        for ix in range(nidx):
            sl = slots[ix]
            wait("scalar", s_sp, ln5[ix])
            def s6(s, sl=sl):
                s.activation(t2("rstd2", sl), t2("rstd2", sl), AF.Exp,
                             scale=-0.5).then_inc(s_sp.sem, 1)
            em("scalar", s6); s_sp.bump()
            ex6.append(s_sp.n)
        dnn = []
        for ix in range(nidx):
            sl = slots[ix]
            wait("vector", s_sp, ex6[ix])
            wait("vector", s_tn, dtr_slot[sl])
            def v7(v, sl=sl):
                v.tensor_scalar(dnscr[sl][:, :], asb[sl][:, 0:128],
                                t2("rstd2", sl), 1.0 - LAMBDA_INIT,
                                ALU.mult, ALU.mult).then_inc(s_vp.sem, 1)
            em("vector", v7); s_vp.bump()
            st7_guard[sl] = s_vp.n
            dnn.append(s_vp.n)
        pend.append((hbase, j, dnn, slots))

    def emit_wq(p):
        chw = dch["wq%d" % (p % 2)]
        if p >= 2:
            wait("gpsimd", s_tn, wfree_n[p - 2])
        def g_wq(g, p=p, ch=chw):
            for jt, nm in enumerate(("q1", "k1", "q2", "k2")):
                src = wqkv.ap()[:, col0[nm] + p * 128: col0[nm] + (p + 1) * 128]
                src = src.rearrange("(c q) j -> q c j", q=128)
                dst = wqk_st[p % 2].rearrange("q (c j) -> q c j", j=512)
                g.dma_start(dst[:, :, jt * 128:(jt + 1) * 128], src).then_inc(ch.sem, 16)
            srcv = wqkv.ap()[:, 4 * QH + p * 256: 4 * QH + (p + 1) * 256]
            srcv = srcv.rearrange("(c q) j -> q c j", q=128)
            dstv = wv_st[p % 2].rearrange("q (c j) -> q c j", j=256)
            g.dma_start(dstv, srcv).then_inc(ch.sem, 16)
        em("gpsimd", g_wq)
        for _ in range(5):
            chw.bump()
        wq_pair_n.append((chw, chw.n))

    emit_wq(0)
    emit_wq(1)
    for p in range(8):
        wait("tensor", *wq_pair_n[p])
        if p == 0:
            wait("tensor", s_vp, phase1_vp_end)
        else:
            wait("tensor", s_vp, qk_evac_n[-1])
        for tcx in range(NJ):
            glob_tc = p * NJ + tcx
            direct = (glob_tc < 2 and JW == 512 and NJ >= 4)
            chl = dch["lnc%d" % (glob_tc % 2)]
            if not direct:
                if glob_tc >= 2:
                    wait("sync", s_tn, lnc_guard[glob_tc % 2])
                wait("sync", dch["lns0"], dch["lns0"].n)
                wait("sync", dch["lns1"], dch["lns1"].n)
                def sy_ln(sy, tcx=tcx, glob_tc=glob_tc, ch=chl):
                    src = lnT_hbm.ap()[:, :, tcx * JW:(tcx + 1) * JW].rearrange("c p w -> p c w")
                    dst = lnTc[glob_tc % 2].rearrange("q (c w) -> q c w", w=JW)
                    sy.dma_start(dst, src).then_inc(ch.sem, 16)
                em("sync", sy_ln); chl.bump()
                wait("tensor", chl, chl.n)
            else:
                wait("tensor", s_vp, lnTc_ready[glob_tc])
            for jt in range(4):
                k = qk_cnt[0]
                bank = bank512[k % 2]
                if k >= 2:
                    wait("tensor", s_vp, qk_evac_n[k - 2])
                for c in range(NC):
                    def t_qk(t, jt=jt, c=c, gp=glob_tc % 2, bank=bank, p=p):
                        ins = t.matmul(bank[:, 0:JW],
                                       wqk_st[p % 2][:, c * 512 + jt * 128: c * 512 + jt * 128 + 128],
                                       lnTc[gp][:, c * JW:(c + 1) * JW],
                                       start=(c == 0), stop=(c == NC - 1))
                        if c == NC - 1:
                            ins.then_inc(s_tn.sem, 1)
                    em("tensor", t_qk)
                s_tn.bump()
                wait("vector", s_tn, s_tn.n)
                sc = 0.125 if jt in (0, 2) else 1.0
                def v_qke(v, jt=jt, tcx=tcx, bank=bank, sc=sc):
                    dst = qkT[:, jt * T + tcx * JW: jt * T + tcx * JW + JW]
                    if sc == 1.0:
                        v.tensor_copy(dst, bank[:, 0:JW]).then_inc(s_vp.sem, 1)
                    else:
                        v.tensor_single_scalar(dst, bank[:, 0:JW], sc,
                                               ALU.mult).then_inc(s_vp.sem, 1)
                em("vector", v_qke); s_vp.bump()
                qk_evac_n.append(s_vp.n)
                qk_cnt[0] += 1
            for tl in range(JW // 128):
                ti = tcx * (JW // 128) + tl
                k = v_cnt[0]
                bank = bank512[2 + k % 2]
                if k >= 2:
                    wait("tensor", s_vp, v_evac_n[k - 2])
                for c in range(NC):
                    def t_v(t, c=c, tl=tl, gp=glob_tc % 2, bank=bank, p=p):
                        ins = t.matmul(bank[:, 0:256],
                                       lnTc[gp][:, c * JW + tl * 128: c * JW + tl * 128 + 128],
                                       wv_st[p % 2][:, c * 256:(c + 1) * 256],
                                       start=(c == 0), stop=(c == NC - 1))
                        if c == NC - 1:
                            ins.then_inc(s_tn.sem, 1)
                    em("tensor", t_v)
                s_tn.bump()
                wait("vector", s_tn, s_tn.n)
                if p == 0 and v_cnt[0] == 0:
                    wait("vector", s_init, vbuf_ready)
                def v_ev(v, ti=ti, bank=bank):
                    v.tensor_copy(vbuf[:, ti * VCH: ti * VCH + 128],
                                  bank[:, 0:128]).then_inc(s_vp.sem, 1)
                    v.tensor_copy(vbuf[:, ti * VCH + 136: ti * VCH + 264],
                                  bank[:, 128:256]).then_inc(s_vp.sem, 1)
                em("vector", v_ev); s_vp.bump(); s_vp.bump()
                v_evac_n.append(s_vp.n)
                v_cnt[0] += 1
            lnc_guard[glob_tc % 2] = s_tn.n
        wfree_n.append(s_tn.n)
        if p + 2 < 8:
            emit_wq(p + 2)

        wait("tensor", s_vp, qk_evac_n[-1])
        wait("tensor", s_vp, v_evac_n[-1])
        if p == 0:
            wait("tensor", s_init, vbuf_ready)
            wait("scalar", s_vp, red_done[NT - 1])
        for j in range(NJ):
            cmax = min(4 * j + 3, NT - 1)
            gates = [None, None]
            for a in range(2):
                qs = 0 if a == 0 else 2
                ks = 1 if a == 0 else 3
                pv_guard = pv_hist[0]
                for cc in range(cmax + 1):
                    k = s_cnt[0]
                    bankp = spair[k % 2]
                    if k >= 2:
                        wait("tensor", s_sp, s_ev_n[k - 2])
                    sc0 = max(0, cc - 4 * j) * 128
                    for hl in range(2):
                        def t_s(t, cc=cc, j=j, qs=qs, ks=ks, hl=hl, sc0=sc0,
                                bankp=bankp, last=(hl == 1)):
                            qoff = 64 * hl
                            ins = t.matmul(bankp[:, hl * JW + sc0: (hl + 1) * JW],
                                           qkT[qoff:qoff + 64, ks * T + cc * 128: ks * T + cc * 128 + 128],
                                           qkT[qoff:qoff + 64, qs * T + j * JW + sc0: qs * T + (j + 1) * JW],
                                           start=True, stop=True,
                                           tile_position=(64 * hl, 0))
                            if last:
                                ins.then_inc(s_tn.sem, 1)
                        em("tensor", t_s)
                    s_tn.bump()
                    wait("scalar", s_tn, s_tn.n)
                    if pv_guard:
                        wait("scalar", s_tn, pv_guard)
                        pv_guard = 0
                    def s_exp(s, a=a, cc=cc, sc0=sc0, bankp=bankp):
                        if sc0 == 0:
                            s.activation(pT[a][:, cc * 2 * JW: (cc + 1) * 2 * JW],
                                         bankp[:, 0:2 * JW],
                                         AF.Exp).then_inc(s_sp.sem, 1)
                        else:
                            src_v = bankp.rearrange("p (t w) -> p t w", w=JW)[:, :, sc0:]
                            dst_v = pT[a][:, cc * 2 * JW: (cc + 1) * 2 * JW].rearrange(
                                "p (t w) -> p t w", w=JW)[:, :, sc0:]
                            s.activation(dst_v, src_v, AF.Exp).then_inc(s_sp.sem, 1)
                    em("scalar", s_exp); s_sp.bump()
                    s_ev_n.append(s_sp.n)
                    s_cnt[0] += 1
                    if 4 * j <= cc and (cc - 4 * j) * 128 < JW:
                        s0 = cc - 4 * j
                        wait("vector", s_sp, s_sp.n)
                        for hl in range(2):
                            def v_mask(v, a=a, cc=cc, s0=s0, hl=hl):
                                sl_ = pT[a][:, cc * 2 * JW + hl * JW + s0 * 128:
                                            cc * 2 * JW + hl * JW + s0 * 128 + 128]
                                v.tensor_mul(sl_, sl_, mask01[:, :]).then_inc(s_vp.sem, 1)
                            em("vector", v_mask); s_vp.bump()
                gates[a] = (s_sp.n, s_vp.n)
            ntile = JW // 128
            copy_n = []
            slots = []
            for tl in range(ntile):
                gidx = j * ntile + tl
                for hl in range(2):
                    voff = 136 * hl
                    apar = pv_groups[0] % 2
                    base = apar * 512
                    if len(vd1_n) >= 2:
                        wait("tensor", s_vp, vd1_n[-2])
                    for a in range(2):
                        sev_after, pm_after = gates[a]
                        wait("tensor", s_sp, sev_after)
                        wait("tensor", s_vp, pm_after)
                        for cc in range(gidx + 1):
                            def t_pv(t, a=a, cc=cc, tl=tl, hl=hl, voff=voff,
                                     base=base, gidx=gidx,
                                     last=(a == 1 and cc == gidx)):
                                ins = t.matmul(pCD[:, base + a * 132: base + a * 132 + 129],
                                               pT[a][:, cc * 2 * JW + hl * JW + tl * 128:
                                                      cc * 2 * JW + hl * JW + tl * 128 + 128],
                                               vbuf[:, cc * VCH + voff: cc * VCH + voff + 129],
                                               start=(cc == 0), stop=(cc == gidx))
                                if last:
                                    ins.then_inc(s_tn.sem, 1)
                            em("tensor", t_pv)
                    s_tn.bump()
                    pv_groups[0] += 1
                    pv_hist[0] = s_tn.n
                    wait("vector", s_tn, s_tn.n)
                    sl = gslot[0] % NSL
                    gslot[0] += 1
                    slots.append(sl)
                    def v_cp(v, sl=sl, base=base):
                        v.tensor_copy(
                            asb[sl].rearrange("p (t c) -> p t c", c=132)[:, :, 0:129],
                            pCD[:, base:base + 264].rearrange("p (t c) -> p t c", c=132)[:, :, 0:129]
                        ).then_inc(s_vp.sem, 1)
                    em("vector", v_cp); s_vp.bump()
                    vd1_n.append(s_vp.n)
                    copy_n.append(s_vp.n)

            if len(pend) >= 2:
                flush_chunk(*pend.pop(0))

            nidx = len(slots)
            st_n = [list(copy_n)]
            cur = []
            for ix in range(nidx):
                sl = slots[ix]
                wait("vector", s_vp, st_n[0][ix])
                def v0(v, sl=sl):
                    v.reciprocal(tiny[:, 64 + 4 * sl: 64 + 4 * sl + 2],
                                 asb[sl].rearrange("p (t c) -> p t c", c=132)[:, :, 128:129]
                                 ).then_inc(s_vp.sem, 1)
                em("vector", v0); s_vp.bump()
                cur.append(s_vp.n)
            st_n.append(cur)
            cur = []
            for ix in range(nidx):
                sl = slots[ix]
                wait("vector", s_vp, st_n[1][ix])
                def v1(v, sl=sl):
                    v.tensor_scalar(t2("r2", sl), t2("r2", sl), tc("lam"), -1.0,
                                    ALU.mult, ALU.mult).then_inc(s_vp.sem, 1)
                em("vector", v1); s_vp.bump()
                cur.append(s_vp.n)
            st_n.append(cur)
            cur = []
            for ix in range(nidx):
                sl = slots[ix]
                wait("vector", s_vp, st_n[1][ix])
                def v2(v, sl=sl):
                    v.tensor_scalar(asb[sl][:, 0:128], asb[sl][:, 0:128],
                                    t2("r1", sl), None, ALU.mult).then_inc(s_vp.sem, 1)
                em("vector", v2); s_vp.bump()
                cur.append(s_vp.n)
            st_n.append(cur)
            cur = []
            for ix in range(nidx):
                sl = slots[ix]
                wait("vector", s_vp, max(st_n[2][ix], st_n[3][ix]))
                def v3(v, sl=sl):
                    v.scalar_tensor_tensor(asb[sl][:, 0:128], asb[sl][:, 132:260],
                                           t2("r2", sl), asb[sl][:, 0:128],
                                           ALU.mult, ALU.add).then_inc(s_vp.sem, 1)
                em("vector", v3); s_vp.bump()
                cur.append(s_vp.n)
            st_n.append(cur)
            cur = []
            for ix in range(nidx):
                sl = slots[ix]
                wait("vector", s_vp, st_n[4][ix])
                wait("vector", s_sp, ln5_guard[sl])
                def v4(v, sl=sl):
                    v.scalar_tensor_tensor(asb[sl][:, 132:260], asb[sl][:, 0:128],
                                           1.0, asb[sl][:, 0:128], ALU.mult,
                                           ALU.mult,
                                           accum_out=t2("ms", sl)).then_inc(s_vp.sem, 1)
                em("vector", v4); s_vp.bump()
                cur.append(s_vp.n)
            st_n.append(cur)
            if pend_sc[0] is not None:
                emit_chain_tail(*pend_sc[0])
            pend_sc[0] = (2 * p, j, slots, list(st_n[5]))

    while len(pend) >= 2:
        flush_chunk(*pend.pop(0))
    if pend_sc[0] is not None:
        emit_chain_tail(*pend_sc[0])
        pend_sc[0] = None
    while pend:
        flush_chunk(*pend.pop(0))
    phase2_tn_end = s_tn.n
    phase2_vp_end = s_vp.n
    phase2_sp_end = s_sp.n

    wait("gpsimd", s_tn, wfree_n[-1])
    def g_wp(g):
        src = wproj.ap().rearrange("(c q) (dc d) -> q dc c d", q=128, d=512)
        dst = wproj_sb.rearrange("q (dc c d) -> q dc c d", dc=4, d=512)
        g.dma_start(dst, src).then_inc(dch["wp"].sem, 16)
    em("gpsimd", g_wp)
    dch["wp"].bump()
    wp_done = dch["wp"].n

    projev_n = []
    proj_last_tn = 0
    dT_guard = [0, 0]
    dT_n = None
    x1s_guard = [0, 0]
    hts_guard = [0, 0]
    rtv_n = []
    h_sp_done = {}
    h_vp_done = {}
    bias3_n = {}
    htr3_done = {}

    def apply3(pj):
        par = pj % 2
        wait("scalar", s_vp, bias3_n[pj])
        def s3_ap(s, par=par, xpar=pj % 4):
            s.activation(xbuf[xpar][:, :], x1t[par][:, :], AF.Identity,
                         bias=tc("bias", par),
                         scale=tc("rstd", par)).then_inc(s_sp.sem, 1)
        em("scalar", s3_ap); s_sp.bump()
        h_sp_done[pj] = s_sp.n
        wait("vector", s_sp, s_sp.n)
        wait("vector", dch["pre"], pre_done)
        if pj >= 2:
            wait("vector", s_tn, htr3_done[pj - 2])
        def v3_h(v, par=par, xpar=pj % 4):
            v.tensor_mul(hbuf[par][:, :], xbuf[xpar][:, :],
                         rmsw[:, :]).then_inc(s_vp.sem, 1)
        em("vector", v3_h); s_vp.bump()
        h_vp_done[pj] = s_vp.n

    def flush3(pj):
        wait("tensor", s_vp, h_vp_done[pj])
        g = [(dch["hts%d" % (pj % 2)], hts_guard[pj % 2])] if pj >= 2 else []
        transpose_tile(hbuf[pj % 2], hTstage[pj % 2], first_evac_guards=g)
        htr3_done[pj] = s_tn.n
        hT_ready = s_vp.n
        wait("sync", s_vp, hT_ready)
        chh = dch["hts%d" % (pj % 2)]
        def sy_hT(sy, pj=pj, ch=chh):
            dst = hT_hbm.ap()[:, :, pj * 128:(pj + 1) * 128].rearrange("c p w -> p c w")
            src = hTstage[pj % 2].rearrange("p (c w) -> p c w", w=128)
            sy.dma_start(dst, src).then_inc(ch.sem, 16)
        em("sync", sy_hT); chh.bump()
        hts_guard[pj % 2] = chh.n
        wait("tensor", s_vp, hT_ready)
        wait("tensor", dch["rw"], routw_done)
        if pj >= 1:
            wait("tensor", s_vp, rtv_n[pj - 1])
        for c in range(NC):
            def t_rt(t, c=c, pj=pj):
                ins = t.matmul(pCD[:, 0:8], hTstage[pj % 2][:, c * 128:(c + 1) * 128],
                               routw[:, c * 8:(c + 1) * 8],
                               start=(c == 0), stop=(c == NC - 1))
                if c == NC - 1:
                    ins.then_inc(s_tn.sem, 1)
            em("tensor", t_rt)
        s_tn.bump()
        wait("vector", s_tn, s_tn.n)
        vchain([
            lambda v: v.reduce_max(tc("m1"), pCD[:, 0:8], axis=AX.X).then_inc(s_vp.sem, 1),
            lambda v: v.tensor_scalar(i8[:, 0:8], pCD[:, 0:8], tc("m1"), None,
                                      ALU.is_ge).then_inc(s_vp.sem, 1),
            lambda v: v.scalar_tensor_tensor(i8[:, 8:16], i8[:, 0:8], -1e30,
                                             pCD[:, 0:8], ALU.mult,
                                             ALU.add).then_inc(s_vp.sem, 1),
            lambda v: v.reduce_max(tc("m2"), i8[:, 8:16], axis=AX.X).then_inc(s_vp.sem, 1),
            lambda v: v.tensor_sub(tc("dlt"), tc("m1"), tc("m2")).then_inc(s_vp.sem, 1),
        ])
        def v_i2(v):
            v.tensor_scalar(i8[:, 16:24], i8[:, 8:16], tc("m2"), None,
                            ALU.is_ge).then_inc(s_vp.sem, 1)
        em("vector", v_i2); s_vp.bump()
        wait("scalar", s_vp, s_vp.n)
        def s_w1(s):
            s.activation(tc("w1"), tc("dlt"), AF.Exp,
                         scale=-1.0).then_inc(s_sp.sem, 1)
        em("scalar", s_w1); s_sp.bump()
        wait("vector", s_sp, s_sp.n)
        vchain([
            lambda v: v.tensor_single_scalar(tc("w1"), tc("w1"), 1.0,
                                             ALU.add).then_inc(s_vp.sem, 1),
            lambda v: v.reciprocal(tc("w1"), tc("w1")).then_inc(s_vp.sem, 1),
            lambda v: v.tensor_scalar(tc("w2"), tc("w1"), -1.0, 1.0, ALU.mult,
                                      ALU.add).then_inc(s_vp.sem, 1),
            lambda v: v.tensor_scalar(i8[:, 24:32], i8[:, 0:8], tc("w1"), None,
                                      ALU.mult).then_inc(s_vp.sem, 1),
            lambda v, pj=pj: v.scalar_tensor_tensor(s_route[:, pj * 8:(pj + 1) * 8],
                                                    i8[:, 16:24], tc("w2"),
                                                    i8[:, 24:32], ALU.mult,
                                                    ALU.add).then_inc(s_vp.sem, 1),
        ])
        rtv_n.append(s_vp.n)

    for j in range(NT):
        cx = j // (JW // 128)
        tl = j % (JW // 128)
        if tl == 0:
            chD = dch["dT%d" % (cx % 2)]
            wait("sync", dch["dst0"], dch["dst0"].n)
            wait("sync", dch["dst1"], dch["dst1"].n)
            if cx >= 2:
                wait("sync", s_tn, dT_guard[cx % 2])
            def sy_dT(sy, cx=cx, ch=chD):
                src = dnT_hbm.ap()[:, :, cx * JW:(cx + 1) * JW].rearrange("c p w -> p c w")
                dst = dTbuf[cx % 2].rearrange("q (c w) -> q c w", w=JW)
                sy.dma_start(dst, src).then_inc(ch.sem, 16)
            em("sync", sy_dT); chD.bump()
            dT_n = (chD, chD.n)
        chx = dch["x3%d" % (j % 4)]
        if j >= 4:
            wait("sync", s_vp, projev_n[4 * (j - 4) + 3])
            wait("sync", s_vp, h_vp_done[j - 4])
            wait("sync", s_sp, h_sp_done[j - 4])
        def sy_x(sy, j=j, ch=chx):
            sy.dma_start(xbuf[j % 4][:, :],
                         x_in.ap()[j * 128:(j + 1) * 128, :]).then_inc(ch.sem, 16)
        em("sync", sy_x); chx.bump()
        x3j = (chx, chx.n)
        wait("tensor", dT_n[0], dT_n[1])
        if j == 0:
            wait("tensor", s_vp, phase2_vp_end)
            wait("tensor", s_sp, phase2_sp_end)
        for dcx in range(4):
            kk = 4 * j + dcx
            wait("tensor", dch["wp"], wp_done)
            bank = bank512[kk % 2]
            if kk >= 2:
                wait("tensor", s_vp, projev_n[kk - 2])
            for c in range(NC):
                def t_pj(t, c=c, dcx=dcx, tl=tl, cx=cx, bank=bank):
                    ins = t.matmul(bank[:, 0:512],
                                   dTbuf[cx % 2][:, c * JW + tl * 128: c * JW + tl * 128 + 128],
                                   wproj_sb[:, dcx * NC * 512 + c * 512: dcx * NC * 512 + (c + 1) * 512],
                                   start=(c == 0), stop=(c == NC - 1))
                    if c == NC - 1:
                        ins.then_inc(s_tn.sem, 1)
                em("tensor", t_pj)
            s_tn.bump()
            wait("vector", s_tn, s_tn.n)
            wait("vector", *x3j)
            if j >= 2 and dcx == 0:
                wait("vector", dch["x1s%d" % (j % 2)], x1s_guard[j % 2])
                wait("vector", s_sp, h_sp_done[j - 2])
            def v_pe(v, j=j, dcx=dcx, bank=bank):
                v.tensor_add(x1t[j % 2][:, dcx * 512:(dcx + 1) * 512], bank[:, 0:512],
                             xbuf[j % 4][:, dcx * 512:(dcx + 1) * 512]).then_inc(s_vp.sem, 1)
            em("vector", v_pe); s_vp.bump()
            projev_n.append(s_vp.n)
        dT_guard[cx % 2] = s_tn.n
        wait("sync", s_vp, projev_n[-1])
        chs = dch["x1s%d" % (j % 2)]
        def sy_x1(sy, j=j, ch=chs):
            sy.dma_start(x1_hbm.ap()[j * 128:(j + 1) * 128, :],
                         x1t[j % 2][:, :]).then_inc(ch.sem, 16)
        em("sync", sy_x1); chs.bump()
        x1s_guard[j % 2] = chs.n
        if j >= 2:
            flush3(j - 2)
        if j >= 1:
            apply3(j - 1)
        wait("scalar", s_vp, projev_n[-1])
        wait("vector", s_vp, projev_n[-1])
        par = j % 2
        def s3_sq(s, j=j, par=par):
            s.activation(sqjunk[:, :], x1t[par][:, :], AF.Square,
                         accum_out=tc("sumsq", par)).then_inc(s_sp.sem, 1)
        em("scalar", s3_sq); s_sp.bump()
        sq3_n = s_sp.n
        def v3_red(v, j=j, par=par):
            v.reduce_sum(tc("sum", par), x1t[par][:, :], axis=AX.X).then_inc(s_vp.sem, 1)
        em("vector", v3_red); s_vp.bump()
        wait("vector", s_vp, s_vp.n)
        def v3_msq(v, par=par):
            v.tensor_scalar(tc("varD", par), tc("sum", par), tc("sum", par),
                            1.0 / D, ALU.mult, ALU.mult).then_inc(s_vp.sem, 1)
        em("vector", v3_msq); s_vp.bump()
        wait("vector", s_vp, s_vp.n)
        wait("vector", s_sp, sq3_n)
        def v3_var(v, par=par):
            v.tensor_sub(tc("varD", par), tc("sumsq", par),
                         tc("varD", par)).then_inc(s_vp.sem, 1)
        em("vector", v3_var); s_vp.bump()
        wait("scalar", s_vp, s_vp.n)
        def s3_ln(s, par=par):
            s.activation(tc("rstd", par), tc("varD", par), AF.Ln,
                         bias=tc("eps11"),
                         scale=(1.0 + EPS_RMS_FFN) / D).then_inc(s_sp.sem, 1)
        em("scalar", s3_ln); s_sp.bump()
        wait("scalar", s_sp, s_sp.n)
        def s3_ex(s, par=par):
            s.activation(tc("rstd", par), tc("rstd", par), AF.Exp,
                         scale=-0.5).then_inc(s_sp.sem, 1)
        em("scalar", s3_ex); s_sp.bump()
        wait("vector", s_sp, s_sp.n)
        def v3_bias(v, par=par):
            v.tensor_scalar(tc("bias", par), tc("sum", par), tc("rstd", par),
                            -1.0 / D, ALU.mult, ALU.mult).then_inc(s_vp.sem, 1)
        em("vector", v3_bias); s_vp.bump()
        bias3_n[j] = s_vp.n
    apply3(NT - 1)
    flush3(NT - 2)
    flush3(NT - 1)

    phase3_tn_end = s_tn.n
    phase3_vp_end = s_vp.n
    phase3_sp_end = s_sp.n
    if dump_sroute is not None:
        wait("sync", s_vp, s_vp.n)
        def sy_sr(sy):
            sy.dma_start(dump_sroute.ap(), s_route[:, :]).then_inc(dch["pre"].sem, 16)
        em("sync", sy_sr); dch["pre"].bump()

    g_n = []
    u_n = []
    sil_n = []
    amul_n = []
    dn_n = []
    dnev_n = []
    yst_n = [0, 0]
    u_half_end = 0

    def wg_src(e, ft, which):
        if e < NE:
            b = wg_in if which == 0 else wu_in
            return b.ap()[e, ft].rearrange("c p f -> p c f")
        b = swg_in if which == 0 else swu_in
        return b.ap()[ft].rearrange("c p f -> p c f")

    hth2_n = [0]
    for hf in range(2):
        if hf == 0:
            chh = dch["hth"]
            wait("sync", dch["hts0"], dch["hts0"].n)
            wait("sync", dch["hts1"], dch["hts1"].n)
            def sy_hth(sy, ch=chh):
                src = hT_hbm.ap()[:, :, 0:HALF].rearrange("c p w -> p c w")
                dst = hT_half.rearrange("q (c w) -> q c w", w=HALF)
                sy.dma_start(dst, src).then_inc(ch.sem, 16)
            em("sync", sy_hth); chh.bump()
            hth_n = chh.n
        else:
            hth_n = hth2_n[0]
        wait("sync", dch["x1s0"], dch["x1s0"].n)
        wait("sync", dch["x1s1"], dch["x1s1"].n)
        if hf == 0:
            wait("sync", s_tn, phase3_tn_end)
        else:
            wait("sync", dch["yst"], yst_n[0])
        def sy_yi(sy, hf=hf):
            src = x1_hbm.ap()[hf * HALF:(hf + 1) * HALF, :]
            src = src.rearrange("(tt p) d -> p tt d", p=128)
            dst = ybuf.rearrange("p (tt d) -> p tt d", d=D)
            sy.dma_start(dst, src).then_inc(dch["yini"].sem, 16)
        em("sync", sy_yi); dch["yini"].bump()
        yini_n = dch["yini"].n

        for e in range(NE + 1):
            eg = hf * (NE + 1) + e
            if eg == 0:
                wait("gpsimd", s_tn, phase3_tn_end)
                wait("gpsimd", s_sp, phase3_sp_end)
                wait("gpsimd", s_vp, phase3_vp_end)
                wait("gpsimd", dch["x1s0"], dch["x1s0"].n)
                wait("gpsimd", dch["x1s1"], dch["x1s1"].n)
                wait("gpsimd", dch["dT0"], dch["dT0"].n)
                wait("gpsimd", dch["dT1"], dch["dT1"].n)
                wait("gpsimd", dch["hts0"], dch["hts0"].n)
                wait("gpsimd", dch["hts1"], dch["hts1"].n)
            wd_emitted = [False]

            def emit_wd(e=e, eg=eg):
                if eg >= 1:
                    wait("gpsimd", s_tn, dn_n[-1])
                def g_wd(g, e=e):
                    if e < NE:
                        src = wd_in.ap()[e].rearrange("(fc p) d -> p fc d", p=128)
                    else:
                        src = swd_in.ap().rearrange("(fc p) d -> p fc d", p=128)
                    g.dma_start(wd_sb.rearrange("p (fc d) -> p fc d", d=D),
                                src).then_inc(dch["wd"].sem, 16)
                em("gpsimd", g_wd); dch["wd"].bump()
            for ft in range(NF):
                blk = eg * NF + ft
                chb = dch["wg%d" % (blk % 2)]
                if blk >= 2:
                    wait("gpsimd", s_tn, u_n[(blk - 2 + 1) * n_tch - 1])
                def g_blk(g, e=e, ft=ft, blk=blk, ch=chb):
                    g.dma_start(wgu_st[0][blk % 2].rearrange("p (c f) -> p c f", f=128),
                                wg_src(e, ft, 0)).then_inc(ch.sem, 16)
                    g.dma_start(wgu_st[1][blk % 2].rearrange("p (c f) -> p c f", f=128),
                                wg_src(e, ft, 1)).then_inc(ch.sem, 16)
                em("gpsimd", g_blk)
                chb.bump(); chb.bump()
                blk_n = chb.n
                if ft == 1 and not wd_emitted[0]:
                    emit_wd()
                    wd_emitted[0] = True
                wait("tensor", chb, blk_n)
                wait("tensor", dch["hth"], hth_n)
                for tcx in range(n_tch):
                    kk = blk * n_tch + tcx
                    bg = bank512[kk % 2]
                    bu = bank512[2 + kk % 2]
                    if len(sil_n) >= 2:
                        wait("tensor", s_sp, sil_n[-2])
                    if len(amul_n) >= 2:
                        wait("tensor", s_vp, amul_n[-2])
                    if kk == 0:
                        wait("tensor", s_vp, phase3_vp_end)
                        wait("tensor", s_sp, phase3_sp_end)
                    for c in range(NC):
                        def t_g(t, c=c, tcx=tcx, blk=blk, bg=bg):
                            ins = t.matmul(bg[:, 0:TCW],
                                           wgu_st[0][blk % 2][:, c * 128:(c + 1) * 128],
                                           hT_half[:, c * HALF + tcx * TCW: c * HALF + tcx * TCW + TCW],
                                           start=(c == 0), stop=(c == NC - 1))
                            if c == NC - 1:
                                ins.then_inc(s_tn.sem, 1)
                        em("tensor", t_g)
                    s_tn.bump()
                    g_n.append(s_tn.n)
                    for c in range(NC):
                        def t_u(t, c=c, tcx=tcx, blk=blk, bu=bu):
                            ins = t.matmul(bu[:, 0:TCW],
                                           wgu_st[1][blk % 2][:, c * 128:(c + 1) * 128],
                                           hT_half[:, c * HALF + tcx * TCW: c * HALF + tcx * TCW + TCW],
                                           start=(c == 0), stop=(c == NC - 1))
                            if c == NC - 1:
                                ins.then_inc(s_tn.sem, 1)
                        em("tensor", t_u)
                    s_tn.bump()
                    u_n.append(s_tn.n)
                    wait("scalar", s_tn, g_n[-1])
                    spar = len(sil_n) % 2
                    if not sil_n:
                        wait("scalar", dch["hts0"], dch["hts0"].n)
                        wait("scalar", dch["hts1"], dch["hts1"].n)
                    if len(sil_n) >= 2:
                        wait("scalar", s_vp, amul_n[-2])
                    def s_sil(s, spar=spar, bg=bg):
                        s.activation(sgscr[spar][:, :], bg[:, 0:TCW],
                                     AF.Silu).then_inc(s_sp.sem, 1)
                    em("scalar", s_sil); s_sp.bump()
                    sil_n.append(s_sp.n)
                    wait("vector", s_sp, sil_n[-1])
                    wait("vector", s_tn, u_n[-1])
                    if eg >= 1 and ft == 0 and tcx == 0:
                        wait("vector", s_tn, dn_n[-1])
                    if eg == 0 and ft == 0 and tcx == 0:
                        wait("vector", dch["hts0"], dch["hts0"].n)
                        wait("vector", dch["hts1"], dch["hts1"].n)
                        wait("vector", s_tn, phase3_tn_end)
                        wait("vector", dch["dT0"], dch["dT0"].n)
                        wait("vector", dch["dT1"], dch["dT1"].n)
                    def v_am(v, ft=ft, tcx=tcx, spar=spar, bu=bu):
                        v.tensor_mul(abuf[:, ft * HALF + tcx * TCW: ft * HALF + tcx * TCW + TCW],
                                     sgscr[spar][:, :], bu[:, 0:TCW]).then_inc(s_vp.sem, 1)
                    em("vector", v_am); s_vp.bump()
                    amul_n.append(s_vp.n)
            a_ready = s_vp.n
            wait("tensor", dch["wd"], dch["wd"].n)
            wait("tensor", s_vp, a_ready)
            for tl in range(NTH):
                for dcx in range(4):
                    kk = len(dn_n)
                    bank = bank512[4 + kk % 2]
                    if kk >= 2:
                        wait("tensor", s_vp, dnev_n[kk - 2])
                    if kk == 0:
                        wait("tensor", s_vp, phase3_vp_end)
                    for fc in range(NF):
                        def t_dn(t, fc=fc, tl=tl, dcx=dcx, eg=eg, bank=bank):
                            ins = t.matmul(bank[:, 0:512],
                                           abuf[:, fc * HALF + tl * 128: fc * HALF + tl * 128 + 128],
                                           wd_sb[:, fc * D + dcx * 512: fc * D + (dcx + 1) * 512],
                                           start=(fc == 0), stop=(fc == NF - 1))
                            if fc == NF - 1:
                                ins.then_inc(s_tn.sem, 1)
                        em("tensor", t_dn)
                    s_tn.bump()
                    dn_n.append(s_tn.n)
                    wait("vector", s_tn, dn_n[-1])
                    if e == 0 and tl == 0 and dcx == 0:
                        wait("vector", dch["yini"], yini_n)
                    def v_dne(v, e=e, tl=tl, dcx=dcx, hf=hf, bank=bank):
                        ysl = ybuf[:, tl * D + dcx * 512: tl * D + (dcx + 1) * 512]
                        if e < NE:
                            cix = (hf * NTH + tl) * 8 + e
                            v.scalar_tensor_tensor(ysl, bank[:, 0:512],
                                                   s_route[:, cix:cix + 1], ysl,
                                                   ALU.mult, ALU.add).then_inc(s_vp.sem, 1)
                        else:
                            v.tensor_add(ysl, ysl, bank[:, 0:512]).then_inc(s_vp.sem, 1)
                    em("vector", v_dne); s_vp.bump()
                    dnev_n.append(s_vp.n)
            if hf == 0 and e == NE:
                u_half_end = u_n[-1]
        if hf == 0:
            chh2 = dch["hth"]
            wait("sync", s_tn, u_n[-1])
            def sy_hth2(sy, ch=chh2):
                src_ = hT_hbm.ap()[:, :, HALF:2 * HALF].rearrange("c p w -> p c w")
                dst_ = hT_half.rearrange("q (c w) -> q c w", w=HALF)
                sy.dma_start(dst_, src_).then_inc(ch.sem, 16)
            em("sync", sy_hth2); chh2.bump()
            hth2_n[0] = chh2.n
        wait("sync", s_vp, dnev_n[-1])
        def sy_yst(sy, hf=hf):
            src = ybuf.rearrange("p (tt d) -> p tt d", d=D)
            dst = out_t.ap()[hf * HALF:(hf + 1) * HALF, :]
            dst = dst.rearrange("(tt p) d -> p tt d", p=128)
            sy.dma_start(dst, src).then_inc(dch["yst"].sem, 16)
        em("sync", sy_yst); dch["yst"].bump()
        yst_n[hf] = dch["yst"].n

    for nm, ch in dch.items():
        if nm != "yst" and ch.n > 0:
            wait("sync", ch, ch.n)
    wait("sync", dch["yst"], dch["yst"].n)

    with nc.Block() as block:
        @block.tensor
        def _(e):
            for f in ops["tensor"]:
                f(e)

        @block.vector
        def _(e):
            for f in ops["vector"]:
                f(e)

        @block.scalar
        def _(e):
            for f in ops["scalar"]:
                f(e)

        @block.gpsimd
        def _(e):
            for f in ops["gpsimd"]:
                f(e)

        @block.sync
        def _(e):
            for f in ops["sync"]:
                f(e)

    return nc


def _prep_host(inputs):
    f32 = lambda a: np.ascontiguousarray(np.asarray(a, dtype=np.float32))
    x = f32(inputs["L0_attn_x"])
    rmsw = np.ascontiguousarray(
        np.broadcast_to(f32(inputs["L0_ffn_rms_weight"])[None, :], (128, D)))
    lam = np.ascontiguousarray(np.concatenate([
        np.broadcast_to(f32(inputs[k])[None, :], (128, DH))
        for k in ("L0_attn_lambda_q1", "L0_attn_lambda_k1",
                  "L0_attn_lambda_q2", "L0_attn_lambda_k2")], axis=1))
    tile_gu = lambda w: np.ascontiguousarray(
        w.reshape(-1, NC, 128, NF, 128).transpose(0, 3, 1, 2, 4))
    wg_t = tile_gu(f32(inputs["L0_ffn_w_gate"]))
    wu_t = tile_gu(f32(inputs["L0_ffn_w_up"]))
    swg_t = np.ascontiguousarray(
        f32(inputs["L0_ffn_sw_gate"]).reshape(NC, 128, NF, 128).transpose(2, 0, 1, 3))
    swu_t = np.ascontiguousarray(
        f32(inputs["L0_ffn_sw_up"]).reshape(NC, 128, NF, 128).transpose(2, 0, 1, 3))
    common = dict(wqkv=f32(inputs["L0_attn_Wqkv"]),
                  wproj=f32(inputs["L0_attn_Wproj"]),
                  rmsw_in=rmsw, lam_in=lam,
                  routw_in=f32(inputs["L0_ffn_router_w"]),
                  wg_in=wg_t, wu_in=wu_t,
                  wd_in=f32(inputs["L0_ffn_w_down"]),
                  swg_in=swg_t, swu_in=swu_t,
                  swd_in=f32(inputs["L0_ffn_sw_down"]))
    return x, common


def kernel(**inputs):
    x, common = _prep_host(inputs)
    B = x.shape[0]
    nc = build(T=x.shape[1])
    in_maps = []
    for b in range(B):
        m = dict(common)
        m["x_in"] = np.ascontiguousarray(x[b])
        in_maps.append(m)
    res = run_bass_kernel_spmd(nc, in_maps, core_ids=list(range(B)))
    out = np.stack([np.asarray(res.results[b]["out"]) for b in range(B)], axis=0)
    return out.astype(np.float32)


if __name__ == "__main__":
    nc = build()
    print("instructions:", nc.n_instructions())


# version 18
# speedup vs baseline: 1.1111x; 1.0029x over previous
import math
import numpy as np
import concourse.bass as bass
import concourse.mybir as mybir
from concourse.bass_utils import run_bass_kernel_spmd

F32 = mybir.dt.float32
BF16 = mybir.dt.bfloat16
AF = mybir.ActivationFunctionType
ALU = mybir.AluOpType
AX = mybir.AxisListType

D = 2048
NH = 16
DH = 64
DV = 128
FF = 1024
NE = 8
NF = FF // 128
NC = D // 128
LAMBDA_INIT = 0.8 - 0.6 * math.exp(-0.3)
EPS_LN = 1e-5
EPS_RMS_ATTN = 1e-5
EPS_RMS_FFN = 1e-6
VCH = 272


class Ch:
    def __init__(self, nc, name, step=1):
        self.sem = nc.alloc_semaphore(name)
        self.n = 0
        self.step = step

    def bump(self):
        self.n += self.step
        return self.n


def build(T=2048, dumps=()):
    nc = bass.Bass("TRN2", target_bir_lowering=False)
    NT = T // 128
    JW = min(512, T)
    NJ = T // JW
    HALF = T // 2
    NTH = HALF // 128
    TCW = min(512, HALF)
    n_tch = HALF // TCW
    QH = D // 2

    x_in = nc.dram_tensor("x_in", [T, D], F32, kind="ExternalInput")
    wqkv = nc.dram_tensor("wqkv", [D, 3 * D], F32, kind="ExternalInput")
    wproj = nc.dram_tensor("wproj", [D, D], F32, kind="ExternalInput")
    rmsw_in = nc.dram_tensor("rmsw_in", [128, D], F32, kind="ExternalInput")
    lam_in = nc.dram_tensor("lam_in", [128, 4 * DH], F32, kind="ExternalInput")
    routw_in = nc.dram_tensor("routw_in", [D, NE], F32, kind="ExternalInput")
    wg_in = nc.dram_tensor("wg_in", [NE, NF, NC, 128, 128], F32, kind="ExternalInput")
    wu_in = nc.dram_tensor("wu_in", [NE, NF, NC, 128, 128], F32, kind="ExternalInput")
    wd_in = nc.dram_tensor("wd_in", [NE, FF, D], F32, kind="ExternalInput")
    swg_in = nc.dram_tensor("swg_in", [NF, NC, 128, 128], F32, kind="ExternalInput")
    swu_in = nc.dram_tensor("swu_in", [NF, NC, 128, 128], F32, kind="ExternalInput")
    swd_in = nc.dram_tensor("swd_in", [FF, D], F32, kind="ExternalInput")
    out_t = nc.dram_tensor("out", [T, D], F32, kind="ExternalOutput")

    sk = lambda nm: "ExternalOutput" if nm in dumps else "Internal"
    lnT_hbm = nc.dram_tensor("lnT_hbm", [NC, 128, T], BF16, kind=sk("lnT_hbm"))
    dnT_hbm = nc.dram_tensor("dnT_hbm", [NC, 128, T], BF16, kind=sk("dnT_hbm"))
    hT_hbm = nc.dram_tensor("hT_hbm", [NC, 128, T], BF16, kind=sk("hT_hbm"))
    x1_hbm = nc.dram_tensor("x1_hbm", [T, D], F32, kind=sk("x1_hbm"))
    dump_sroute = None
    if "s_route" in dumps:
        dump_sroute = nc.dram_tensor("s_route_d", [128, NT * 8], F32,
                                     kind="ExternalOutput")

    ident = nc.alloc_sbuf_tensor("ident", [128, 128], BF16).ap()
    mask01 = nc.alloc_sbuf_tensor("mask01", [128, 128], BF16).ap()
    rmsw = nc.alloc_sbuf_tensor("rmsw", [128, D], F32).ap()
    lamv = nc.alloc_sbuf_tensor("lamv", [128, 4 * DH], F32).ap()
    tiny = nc.alloc_sbuf_tensor("tiny", [128, 128], F32).ap()
    i8 = nc.alloc_sbuf_tensor("i8", [128, 48], F32).ap()
    routw = nc.alloc_sbuf_tensor("routw", [128, NC * 8], BF16).ap()
    s_route = nc.alloc_sbuf_tensor("s_route", [128, NT * 8], F32).ap()
    NSL = 16
    dnscr = [nc.alloc_sbuf_tensor(f"dnscr{i}", [128, 128], BF16).ap() for i in range(NSL)]
    asb = [nc.alloc_sbuf_tensor(f"asb{i}", [128, 264], F32).ap() for i in range(NSL)]
    sqjunk = nc.alloc_sbuf_tensor("sqjunk", [128, D], BF16).ap()

    GCOL = dict(lam=0, dot1=1, dot2=2, e1=3, e2=4, m1=5, m2=6, dlt=7, w1=8,
                w2=9, eps_ln=10, eps_ffn=11, eps11=12)
    LNCOL = dict(sum=0, sumsq=1, varD=2, rstd=3, bias=4, vv=5, ta=6, q=7, r1b=8)

    def tc(name, par=0):
        if name in GCOL:
            c = GCOL[name]
        else:
            c = 32 + 16 * par + LNCOL[name]
        return tiny[:, c:c + 1]

    def ln1c(name, p):
        c = 64 + 16 * p + LNCOL[name]
        return tiny[:, c:c + 1]

    def t2(name, sl):
        c = 64 + 4 * sl + {"r1": 0, "r2": 1, "ms": 2, "rstd2": 3}[name]
        return tiny[:, c:c + 1]

    base = nc.sbuf_base
    avail = nc.sbuf_top - base

    def at(name, shape, dtype, off):
        sz = shape[1] * (4 if dtype == F32 else 2)
        assert off >= 0 and off + sz <= avail, (name, off, sz, avail)
        return nc.alloc_sbuf_tensor_at(name, shape, dtype,
                                       offset=base + off).ap()

    lnTsz = NC * JW * 2
    o = 0
    lnTc = [at(f"lnTc{i}", [128, NC * JW], BF16, o + i * lnTsz) for i in range(2)]
    o += 2 * lnTsz
    o_wqk = o
    wqk_st = [at(f"wqk{i}", [128, NC * 512], BF16, o + i * NC * 1024) for i in range(2)]
    o += 2 * NC * 1024
    o_wv = o
    wv_st = [at(f"wv{i}", [128, NC * 256], BF16, o + i * NC * 512) for i in range(2)]
    o += 2 * NC * 512
    o_qkT = o
    qkT = at("qkT", [128, 4 * T], BF16, o)
    o += 4 * T * 2
    o_pT = o
    pT = []
    for a in range(2):
        pT.append(at(f"pT{a}", [128, NT * 2 * JW], BF16, o))
        o += NT * 2 * JW * 2
    o_vbuf = o
    vbuf = at("vbuf", [128, NT * VCH], BF16, o)
    o += NT * VCH * 2
    dstage = at("dstage", [128, 2 * JW], BF16, o)
    o += 2 * JW * 2
    p12_top = o
    if 4 * D * 2 <= 2 * NT * JW * 2:
        o_lnscr = o_pT + 2 * NT * JW * 2
    else:
        o_lnscr = p12_top
        p12_top += 2 * D * 2
    lnscr = [at(f"lnscr{i}", [128, D], BF16, o_lnscr + i * D * 2) for i in range(2)]

    wproj_sb = at("wproj_sb", [128, 4 * NC * 512], BF16, 0)
    o3 = 4 * NC * 512 * 2
    o_dT = o3
    dTbuf = [at(f"dT{i}", [128, NC * JW], BF16, o3 + i * lnTsz) for i in range(2)]
    o3 += 2 * lnTsz
    o_hth = o3
    hT_half = at("hT_half", [128, NC * HALF], BF16, o3)
    o3 += NC * HALF * 2
    o_x1t = o3
    x1t = [at(f"x1t{i}", [128, D], F32, o3 + i * D * 4) for i in range(2)]
    o3 += 2 * D * 4
    hbuf = [at(f"hbuf{i}", [128, D], BF16, o3 + i * D * 2) for i in range(2)]
    o3 += 2 * D * 2
    o_hTs = o3
    hTstage = [at(f"hTs{i}", [128, NC * 128], BF16, o3 + i * NC * 256) for i in range(2)]
    o3 += 2 * NC * 256
    p3_top = o3

    ysz = NTH * D * 4
    assert ysz <= o_dT, (ysz, o_dT)
    ybuf = at("ybuf", [128, NTH * D], F32, 0)
    asz = NF * HALF * 2
    sgsz = 2 * TCW * 2
    if asz + 4 * NC * 256 <= o_hth - o_dT:
        o_a = o_dT
        o_wgu = o_dT + asz
        o_wd = o_x1t
        if o_wgu + 4 * NC * 256 + sgsz <= o_hth:
            o_sg = o_wgu + 4 * NC * 256
        else:
            o_sg = o_wd + NF * D * 2
        assert o_wgu + 4 * NC * 256 <= o_hth, (o_wgu, o_hth)
        assert o_wd >= o_hth + NC * HALF * 2
    else:
        o_wd = o_x1t
        o_sg = o_wd + NF * D * 2
        o_a = max(p12_top, p3_top, o_sg + sgsz)
        o_wgu = o_a + asz
    abuf = at("abuf", [128, NF * HALF], BF16, o_a)
    wgu_st = []
    for nmi, nm in enumerate(("wgst", "wust")):
        wgu_st.append([at(f"{nm}{i}", [128, NC * 128], BF16,
                          o_wgu + (2 * nmi + i) * NC * 256) for i in range(2)])
    wd_sb = at("wd_sb", [128, NF * D], BF16, o_wd)
    sgscr = [at(f"sg{i}", [128, TCW], BF16, o_sg + i * TCW * 2) for i in range(2)]
    p4_top = max(o_sg + sgsz, o_wgu + 4 * NC * 256, o_wd + NF * D * 2)
    if 4 * D * 4 <= 2 * NT * 2 * JW * 2 and o_pT == o_hth and 4 * D * 4 <= NC * HALF * 2:
        o_xb = o_pT
    else:
        o_xb = avail - 4 * D * 4
        assert o_xb >= max(p12_top, p3_top, p4_top), \
            (o_xb, p12_top, p3_top, p4_top)
    xbuf = [at(f"xbuf{i}", [128, D], F32, o_xb + i * D * 4) for i in range(4)]
    assert max(p12_top, p3_top, p4_top) <= avail, (p12_top, p3_top, p4_top, avail)

    pA = nc.alloc_psum_tensor("pA", [128, 1024], F32).ap()
    pB = nc.alloc_psum_tensor("pB", [128, 1024], F32).ap()
    pCD = nc.alloc_psum_tensor("pCD", [128, 1024], F32).ap()
    pbf = [nc.alloc_psum_tensor(f"pbf{i}", [128, 1024], BF16).ap() for i in range(2)]
    bank512 = [pA[:, 0:512], pA[:, 512:1024], pB[:, 0:512], pB[:, 512:1024],
               pCD[:, 0:512], pCD[:, 512:1024]]
    spair = [pA, pB]

    s_init = Ch(nc, "s_init")
    s_tn = Ch(nc, "s_tn")
    s_vp = Ch(nc, "s_vp")
    s_sp = Ch(nc, "s_sp")
    dch = {}
    for nm in ("xl0", "xl1", "xl2", "xl3", "lns0", "lns1", "lnc0", "lnc1", "wq0", "wq1",
               "dst0", "dst1", "wp", "rw", "dT0", "dT1", "x30", "x31", "x32", "x33", "x1s0",
               "x1s1", "hts0", "hts1", "hth", "wg0", "wg1", "wd", "yini",
               "yst", "pre"):
        dch[nm] = Ch(nc, "d_" + nm, 16)

    ops = {k: [] for k in ("tensor", "vector", "scalar", "gpsimd", "sync")}

    def em(eng, fn):
        ops[eng].append(fn)

    last_wait = {}

    def wait(eng, ch, val):
        assert val <= ch.n, (eng, val, ch.n)
        key = (eng, ch.sem.num)
        if val <= last_wait.get(key, 0):
            return
        last_wait[key] = val
        em(eng, lambda e, s=ch.sem, v=val: e.wait_ge(s, v))

    def g_init(g):
        steps = [
            lambda: g.memset(ident[:, :], 0.0),
            lambda: g.affine_select(out=ident[:, :], in_=ident[:, :],
                                    compare_op=ALU.not_equal, fill=1.0, base=0,
                                    pattern=[[-1, 128]], channel_multiplier=1),
            lambda: g.memset(mask01[:, :], 0.0),
            lambda: g.affine_select(out=mask01[:, :], in_=mask01[:, :],
                                    compare_op=ALU.is_gt, fill=1.0, base=0,
                                    pattern=[[-1, 128]], channel_multiplier=1),
        ]
        n = 0
        for st in steps:
            st().then_inc(s_init.sem, 1)
            n += 1
            g.wait_ge(s_init.sem, n)
    em("gpsimd", g_init)
    for _ in range(4):
        s_init.bump()
    init_done = s_init.n

    def sy_pre(sy):
        sy.dma_start(rmsw[:, :], rmsw_in.ap()).then_inc(dch["pre"].sem, 16)
        sy.dma_start(lamv[:, :], lam_in.ap()).then_inc(dch["pre"].sem, 16)
    em("sync", sy_pre)
    dch["pre"].bump(); dch["pre"].bump()
    pre_done = dch["pre"].n

    def g_rw(g):
        g.dma_start(routw.rearrange("p (c e) -> p c e", e=8),
                    routw_in.ap().rearrange("(c p) e -> p c e", p=128)
                    ).then_inc(dch["rw"].sem, 16)
    em("gpsimd", g_rw)
    dch["rw"].bump()
    routw_done = dch["rw"].n

    def v_consts(v):
        v.memset(tc("eps_ln"), EPS_LN)
        v.memset(tc("eps_ffn"), EPS_RMS_FFN)
        v.memset(tc("eps11"), EPS_LN * EPS_RMS_FFN).then_inc(s_vp.sem, 1)
    em("vector", v_consts); s_vp.bump()
    wait("vector", s_vp, s_vp.n)

    wait("vector", dch["pre"], pre_done)
    wait("vector", s_init, init_done)

    def vchain(v_ops):
        for fn in v_ops:
            em("vector", fn)
            s_vp.bump()
            wait("vector", s_vp, s_vp.n)

    vchain([
        lambda v: v.tensor_mul(lamv[:, 0:DH], lamv[:, 0:DH],
                               lamv[:, DH:2 * DH]).then_inc(s_vp.sem, 1),
        lambda v: v.reduce_sum(tc("dot1"), lamv[:, 0:DH], axis=AX.X).then_inc(s_vp.sem, 1),
        lambda v: v.tensor_mul(lamv[:, 2 * DH:3 * DH], lamv[:, 2 * DH:3 * DH],
                               lamv[:, 3 * DH:4 * DH]).then_inc(s_vp.sem, 1),
        lambda v: v.reduce_sum(tc("dot2"), lamv[:, 2 * DH:3 * DH],
                               axis=AX.X).then_inc(s_vp.sem, 1),
    ])
    wait("scalar", s_vp, s_vp.n)

    def s_lam(s):
        s.activation(tc("e1"), tc("dot1"), AF.Exp).then_inc(s_sp.sem, 1)
        s.activation(tc("e2"), tc("dot2"), AF.Exp).then_inc(s_sp.sem, 1)
    em("scalar", s_lam); s_sp.bump(); s_sp.bump()
    wait("vector", s_sp, s_sp.n)
    vchain([
        lambda v: v.tensor_sub(tc("lam"), tc("e1"), tc("e2")).then_inc(s_vp.sem, 1),
        lambda v: v.tensor_single_scalar(tc("lam"), tc("lam"), LAMBDA_INIT, ALU.add).then_inc(s_vp.sem, 1),
    ])
    lam_done_v = s_vp.n

    tp_n = [0]
    tpe_done = []

    def transpose_tile(src, dst_stage, first_evac_guards=(), dst_fn=None):
        for c0 in range(0, NC, 8):
            nn = min(8, NC - c0)
            k = tp_n[0]
            if k >= 2:
                wait("tensor", s_vp, tpe_done[k - 2])
            for ci in range(nn):
                def t_tr(t, src=src, c=c0 + ci, ci=ci, k=k, last=(ci == nn - 1)):
                    ins = t.transpose(pbf[k % 2][:, ci * 128:(ci + 1) * 128],
                                      src[:, c * 128:(c + 1) * 128], ident[:, :])
                    if last:
                        ins.then_inc(s_tn.sem, 1)
                em("tensor", t_tr)
            s_tn.bump()
            wait("vector", s_tn, s_tn.n)
            if c0 == 0:
                for g in first_evac_guards:
                    wait("vector", *g)
            if dst_fn is not None:
                for ci in range(nn):
                    def v_ev(v, dst=dst_fn(c0 + ci), ci=ci, k=k,
                             last=(ci == nn - 1)):
                        ins = v.tensor_copy(dst, pbf[k % 2][:, ci * 128:(ci + 1) * 128])
                        if last:
                            ins.then_inc(s_vp.sem, 1)
                    em("vector", v_ev)
                s_vp.bump()
            else:
                def v_ev(v, dst=dst_stage[:, c0 * 128:(c0 + nn) * 128], k=k, nn=nn):
                    v.tensor_copy(dst, pbf[k % 2][:, 0:nn * 128]).then_inc(s_vp.sem, 1)
                em("vector", v_ev); s_vp.bump()
            tpe_done.append(s_vp.n)
            tp_n[0] += 1

    wait("tensor", s_init, init_done)
    app_done = {}
    red_done = {}
    lnscr_free = {}
    lns_guard = [0, 0]
    lnTc_ready = [0, 0, 0, 0]
    xl_n = {}
    bsn1 = {}

    def p1_loads(grp):
        for i in grp:
            chx = dch["xl%d" % (i % 4)]
            if i >= 4:
                wait("sync", s_vp, app_done[i - 4])
                wait("sync", s_vp, red_done[i - 4])
            def sy_x(sy, i=i, ch=chx):
                sy.dma_start(xbuf[i % 4][:, :],
                             x_in.ap()[i * 128:(i + 1) * 128, :]).then_inc(ch.sem, 16)
            em("sync", sy_x); chx.bump()
            xl_n[i] = (chx, chx.n)

    def p1_stats(grp):
        sqn = {}; stv = {}
        for i in grp:
            wait("scalar", *xl_n[i])
            if i > 0:
                wait("scalar", s_sp, s_sp.n)
            def s_sq(s, i=i):
                s.activation(sqjunk[:, :], xbuf[i % 4][:, :], AF.Square,
                             accum_out=ln1c("sumsq", i % 4)).then_inc(s_sp.sem, 1)
            em("scalar", s_sq); s_sp.bump()
            sqn[i] = s_sp.n
        for i in grp:
            wait("vector", *xl_n[i])
            def v_red(v, i=i):
                v.reduce_sum(ln1c("sum", i % 4), xbuf[i % 4][:, :],
                             axis=AX.X).then_inc(s_vp.sem, 1)
            em("vector", v_red); s_vp.bump()
            red_done[i] = s_vp.n
        for i in grp:
            wait("vector", s_vp, red_done[i])
            def v_msq(v, i=i):
                v.tensor_scalar(ln1c("varD", i % 4), ln1c("sum", i % 4),
                                ln1c("sum", i % 4), 1.0 / D, ALU.mult,
                                ALU.mult).then_inc(s_vp.sem, 1)
            em("vector", v_msq); s_vp.bump()
            stv[i] = s_vp.n
        for i in grp:
            wait("vector", s_vp, stv[i])
            wait("vector", s_sp, sqn[i])
            def v_var(v, i=i):
                v.tensor_sub(ln1c("varD", i % 4), ln1c("sumsq", i % 4),
                             ln1c("varD", i % 4)).then_inc(s_vp.sem, 1)
            em("vector", v_var); s_vp.bump()
            stv[i] = s_vp.n
        lnn = {}
        for i in grp:
            wait("scalar", s_vp, stv[i])
            def s_ln(s, i=i):
                s.activation(ln1c("rstd", i % 4), ln1c("varD", i % 4), AF.Ln,
                             bias=tc("eps_ln"), scale=1.0 / D).then_inc(s_sp.sem, 1)
            em("scalar", s_ln); s_sp.bump()
            lnn[i] = s_sp.n
        for i in grp:
            wait("scalar", s_sp, lnn[i])
            def s_ex(s, i=i):
                s.activation(ln1c("rstd", i % 4), ln1c("rstd", i % 4), AF.Exp,
                             scale=-0.5).then_inc(s_sp.sem, 1)
            em("scalar", s_ex); s_sp.bump()
            lnn[i] = s_sp.n
        for i in grp:
            wait("vector", s_sp, lnn[i])
            def v_bias(v, i=i):
                v.tensor_scalar(ln1c("bias", i % 4), ln1c("sum", i % 4),
                                ln1c("rstd", i % 4), -1.0 / D, ALU.mult,
                                ALU.mult).then_inc(s_vp.sem, 1)
            em("vector", v_bias); s_vp.bump()
            bsn1[i] = s_vp.n

    lnsc = {}

    def p1_apply_tr(grp):
        for i in grp:
            gr = i // 4
            wait("vector", s_vp, bsn1[i])
            if i >= 2:
                wait("vector", s_tn, lnscr_free[i - 2])
            def v_ap(v, i=i):
                v.tensor_scalar(lnscr[i % 2][:, :], xbuf[i % 4][:, :],
                                ln1c("rstd", i % 4), ln1c("bias", i % 4),
                                ALU.mult, ALU.add).then_inc(s_vp.sem, 1)
            em("vector", v_ap); s_vp.bump()
            app_done[i] = s_vp.n
            wait("tensor", s_vp, app_done[i])
            g = []
            if gr >= 2:
                g = [(s_tn, lnc_guard[gr % 2]),
                     (dch["lns0"], lnsc[gr - 2][0]),
                     (dch["lns1"], lnsc[gr - 2][1])]
            transpose_tile(
                lnscr[i % 2], None, first_evac_guards=g,
                dst_fn=lambda c, i=i: lnTc[(i // 4) % 2][
                    :, c * JW + (i % 4) * 128: c * JW + (i % 4) * 128 + 128])
            lnTc_ready[i // 4] = s_vp.n
            lnscr_free[i] = s_tn.n
            wait("sync", s_vp, s_vp.n)
            chs = dch["lns%d" % (i % 2)]
            def sy_st(sy, i=i, ch=chs):
                dst = lnT_hbm.ap()[:, :, i * 128:(i + 1) * 128].rearrange("c p w -> p c w")
                src = lnTc[(i // 4) % 2].rearrange("q (c w) -> q c w", w=JW)[
                    :, :, (i % 4) * 128:(i % 4) * 128 + 128]
                sy.dma_start(dst, src).then_inc(ch.sem, 16)
            em("sync", sy_st); chs.bump()
            lns_guard[i % 2] = chs.n
        lnsc[grp[0] // 4] = (dch["lns0"].n, dch["lns1"].n)

    def g_vinit(g):
        vr = vbuf.rearrange("p (c w) -> p c w", w=VCH)
        for i, st in enumerate([
            lambda: g.memset(vbuf[:, :], 0.0),
            lambda: g.memset(vr[:, :, 128:129], 1.0),
            lambda: g.memset(vr[:, :, 264:265], 1.0),
        ]):
            st().then_inc(s_init.sem, 1)
            g.wait_ge(s_init.sem, init_done + i + 1)
    em("gpsimd", g_vinit)
    for _ in range(3):
        s_init.bump()
    vbuf_ready = s_init.n

    col0 = {"q1": 0, "k1": QH, "q2": 2 * QH, "k2": 3 * QH}
    wq_pair_n = []
    wfree_n = []
    lnc_guard = [0, 0]
    qk_evac_n = []
    v_evac_n = []
    s_ev_n = []
    s_cnt = [0]
    qk_cnt = [0]
    v_cnt = [0]
    pv_hist = [0]
    vd1_n = [0]
    dtr_slot = {i: 0 for i in range(NSL)}
    ln5_guard = {i: 0 for i in range(NSL)}
    st7_guard = {i: 0 for i in range(NSL)}
    dst_guard = [0, 0]
    dstc = {}
    pv_groups = [0]
    gslot = [0]
    pend = []
    pend_sc = [None]

    def flush_chunk(hbase, pj, dnn, slots):
        ntile = JW // 128
        k = tp_n[0]
        if k >= 2:
            wait("tensor", s_vp, tpe_done[k - 2])
        wait("tensor", s_vp, dnn[-1])
        nsl_c = len(slots)
        for hl in range(2):
            for tl in range(ntile):
                sl = slots[tl * 2 + hl]
                si = hl * ntile + tl
                def t_dtr(t, sl=sl, si=si, k=k, last=(hl == 1 and tl == ntile - 1)):
                    ins = t.transpose(pbf[k % 2][:, si * 128:(si + 1) * 128],
                                      dnscr[sl][:, :], ident[:, :])
                    if last:
                        ins.then_inc(s_tn.sem, 1)
                em("tensor", t_dtr)
                dtr_slot[sl] = s_tn.n + 1
        s_tn.bump()
        wait("vector", s_tn, s_tn.n)
        wait("vector", dch["dst0"], dst_guard[0])
        wait("vector", dch["dst1"], dst_guard[1])
        def v_dst(v, k=k, nsl_c=nsl_c):
            v.tensor_copy(dstage[:, 0:nsl_c * 128],
                          pbf[k % 2][:, 0:nsl_c * 128]).then_inc(s_vp.sem, 1)
        em("vector", v_dst); s_vp.bump()
        tpe_done.append(s_vp.n)
        tp_n[0] += 1
        wait("gpsimd", s_vp, s_vp.n)
        for hl in range(2):
            chd = dch["dst%d" % hl]
            def sy_dst(sy, hbase=hbase, hl=hl, pj=pj, ch=chd, ntile=ntile):
                sy.dma_start(dnT_hbm.ap()[hbase + hl, :, pj * JW:(pj + 1) * JW],
                             dstage[:, hl * ntile * 128:(hl + 1) * ntile * 128]
                             ).then_inc(ch.sem, 16)
            em("gpsimd", sy_dst); chd.bump()
            dst_guard[hl] = chd.n
        if hbase == 14:
            dstc[pj] = (dch["dst0"].n, dch["dst1"].n)

    def emit_chain_tail(hbase, j, slots, st4v):
        nidx = len(slots)
        ln5 = []
        for ix in range(nidx):
            sl = slots[ix]
            wait("scalar", s_vp, st4v[ix])
            wait("scalar", s_vp, st7_guard[sl])
            def s5(s, sl=sl):
                s.activation(t2("rstd2", sl), t2("ms", sl), AF.Ln,
                             bias=tc("eps_ln"), scale=1.0 / DV).then_inc(s_sp.sem, 1)
            em("scalar", s5); s_sp.bump()
            ln5_guard[sl] = s_sp.n
            ln5.append(s_sp.n)
        ex6 = []
        for ix in range(nidx):
            sl = slots[ix]
            wait("scalar", s_sp, ln5[ix])
            def s6(s, sl=sl):
                s.activation(t2("rstd2", sl), t2("rstd2", sl), AF.Exp,
                             scale=-0.5).then_inc(s_sp.sem, 1)
            em("scalar", s6); s_sp.bump()
            ex6.append(s_sp.n)
        dnn = []
        for ix in range(nidx):
            sl = slots[ix]
            wait("vector", s_sp, ex6[ix])
            wait("vector", s_tn, dtr_slot[sl])
            def v7(v, sl=sl):
                v.tensor_scalar(dnscr[sl][:, :], asb[sl][:, 0:128],
                                t2("rstd2", sl), 1.0 - LAMBDA_INIT,
                                ALU.mult, ALU.mult).then_inc(s_vp.sem, 1)
            em("vector", v7); s_vp.bump()
            st7_guard[sl] = s_vp.n
            dnn.append(s_vp.n)
        pend.append((hbase, j, dnn, slots))

    def emit_wq(p):
        chw = dch["wq%d" % (p % 2)]
        if p >= 2:
            wait("gpsimd", s_tn, wfree_n[p - 2])
        def g_wq(g, p=p, ch=chw):
            for jt, nm in enumerate(("q1", "k1", "q2", "k2")):
                src = wqkv.ap()[:, col0[nm] + p * 128: col0[nm] + (p + 1) * 128]
                src = src.rearrange("(c q) j -> q c j", q=128)
                dst = wqk_st[p % 2].rearrange("q (c j) -> q c j", j=512)
                g.dma_start(dst[:, :, jt * 128:(jt + 1) * 128], src).then_inc(ch.sem, 16)
            srcv = wqkv.ap()[:, 4 * QH + p * 256: 4 * QH + (p + 1) * 256]
            srcv = srcv.rearrange("(c q) j -> q c j", q=128)
            dstv = wv_st[p % 2].rearrange("q (c j) -> q c j", j=256)
            g.dma_start(dstv, srcv).then_inc(ch.sem, 16)
        em("gpsimd", g_wq)
        for _ in range(5):
            chw.bump()
        wq_pair_n.append((chw, chw.n))

    emit_wq(0)
    emit_wq(1)
    lncn = {}

    def emit_lnc_load(g):
        if g >= 8 * NJ or g < 4:
            return
        tcx_g = g % NJ
        chl = dch["lnc%d" % (g % 2)]
        wait("sync", s_tn, lnc_guard[g % 2])
        sgrp = 2 + (g % 2) if g < 6 else 3
        wait("sync", dch["lns0"], lnsc[sgrp][0])
        wait("sync", dch["lns1"], lnsc[sgrp][1])
        def sy_ln(sy, tcx=tcx_g, g=g, ch=chl):
            src = lnT_hbm.ap()[:, :, tcx * JW:(tcx + 1) * JW].rearrange("c p w -> p c w")
            dst = lnTc[g % 2].rearrange("q (c w) -> q c w", w=JW)
            sy.dma_start(dst, src).then_inc(ch.sem, 16)
        em("sync", sy_ln); chl.bump()
        lncn[g] = chl.n

    def emit_qkv_chunk(p, tcx):
        if True:
            glob_tc = p * NJ + tcx
            direct = glob_tc < 4
            chl = dch["lnc%d" % (glob_tc % 2)]
            if tcx == 0:
                wait("tensor", *wq_pair_n[p])
            if not direct:
                wait("tensor", chl, lncn[glob_tc])
            else:
                wait("tensor", s_vp, lnTc_ready[glob_tc])
            for jt in range(4):
                k = qk_cnt[0]
                bank = bank512[k % 2]
                if k >= 2:
                    wait("tensor", s_vp, qk_evac_n[k - 2])
                for c in range(NC):
                    def t_qk(t, jt=jt, c=c, gp=glob_tc % 2, bank=bank, p=p):
                        ins = t.matmul(bank[:, 0:JW],
                                       wqk_st[p % 2][:, c * 512 + jt * 128: c * 512 + jt * 128 + 128],
                                       lnTc[gp][:, c * JW:(c + 1) * JW],
                                       start=(c == 0), stop=(c == NC - 1))
                        if c == NC - 1:
                            ins.then_inc(s_tn.sem, 1)
                    em("tensor", t_qk)
                s_tn.bump()
                wait("vector", s_tn, s_tn.n)
                sc = 0.125 if jt in (0, 2) else 1.0
                def v_qke(v, jt=jt, tcx=tcx, bank=bank, sc=sc):
                    dst = qkT[:, jt * T + tcx * JW: jt * T + tcx * JW + JW]
                    if sc == 1.0:
                        v.tensor_copy(dst, bank[:, 0:JW]).then_inc(s_vp.sem, 1)
                    else:
                        v.tensor_single_scalar(dst, bank[:, 0:JW], sc,
                                               ALU.mult).then_inc(s_vp.sem, 1)
                em("vector", v_qke); s_vp.bump()
                qk_evac_n.append(s_vp.n)
                qk_cnt[0] += 1
            for tl in range(JW // 128):
                ti = tcx * (JW // 128) + tl
                k = v_cnt[0]
                bank = bank512[2 + k % 2]
                if k >= 2:
                    wait("tensor", s_vp, v_evac_n[k - 2])
                for c in range(NC):
                    def t_v(t, c=c, tl=tl, gp=glob_tc % 2, bank=bank, p=p):
                        ins = t.matmul(bank[:, 0:256],
                                       lnTc[gp][:, c * JW + tl * 128: c * JW + tl * 128 + 128],
                                       wv_st[p % 2][:, c * 256:(c + 1) * 256],
                                       start=(c == 0), stop=(c == NC - 1))
                        if c == NC - 1:
                            ins.then_inc(s_tn.sem, 1)
                    em("tensor", t_v)
                s_tn.bump()
                wait("vector", s_tn, s_tn.n)
                if p == 0 and v_cnt[0] == 0:
                    wait("vector", s_init, vbuf_ready)
                def v_ev(v, ti=ti, bank=bank):
                    v.tensor_copy(vbuf[:, ti * VCH: ti * VCH + 128],
                                  bank[:, 0:128]).then_inc(s_vp.sem, 1)
                    v.tensor_copy(vbuf[:, ti * VCH + 136: ti * VCH + 264],
                                  bank[:, 128:256]).then_inc(s_vp.sem, 1)
                em("vector", v_ev); s_vp.bump(); s_vp.bump()
                v_evac_n.append(s_vp.n)
                v_cnt[0] += 1
            lnc_guard[glob_tc % 2] = s_tn.n
            emit_lnc_load(glob_tc + 2)

    assert NJ == 4 and NT == 16
    grps = [list(range(g0, g0 + 4)) for g0 in range(0, NT, 4)]
    p1_loads(grps[0]); p1_stats(grps[0])
    p1_apply_tr(grps[0])
    p1_loads(grps[1]); p1_stats(grps[1])
    emit_qkv_chunk(0, 0)
    p1_apply_tr(grps[1])
    p1_loads(grps[2]); p1_stats(grps[2])
    emit_qkv_chunk(0, 1)
    p1_apply_tr(grps[2])
    p1_loads(grps[3]); p1_stats(grps[3])
    emit_qkv_chunk(0, 2)
    p1_apply_tr(grps[3])
    emit_qkv_chunk(0, 3)
    phase1_vp_end = s_vp.n
    wfree_n.append(s_tn.n)
    emit_wq(2)

    for p in range(8):
        if p > 0:
            wait("tensor", *wq_pair_n[p])
            wait("tensor", s_vp, qk_evac_n[-1])
            for tcx in range(NJ):
                emit_qkv_chunk(p, tcx)
            wfree_n.append(s_tn.n)
            if p + 2 < 8:
                emit_wq(p + 2)

        wait("tensor", s_vp, qk_evac_n[-1])
        wait("tensor", s_vp, v_evac_n[-1])
        if p == 0:
            wait("tensor", s_init, vbuf_ready)
            wait("scalar", s_vp, red_done[NT - 1])
        for j in range(NJ):
            cmax = min(4 * j + 3, NT - 1)
            exp_n = {}
            mask_n = {}
            pv_guard = pv_hist[0]
            for cc in range(cmax + 1):
                for a in range(2):
                    qs = 0 if a == 0 else 2
                    ks = 1 if a == 0 else 3
                    k = s_cnt[0]
                    bankp = spair[k % 2]
                    if k >= 2:
                        wait("tensor", s_sp, s_ev_n[k - 2])
                    sc0 = max(0, cc - 4 * j) * 128
                    for hl in range(2):
                        def t_s(t, cc=cc, j=j, qs=qs, ks=ks, hl=hl, sc0=sc0,
                                bankp=bankp, last=(hl == 1)):
                            qoff = 64 * hl
                            ins = t.matmul(bankp[:, hl * JW + sc0: (hl + 1) * JW],
                                           qkT[qoff:qoff + 64, ks * T + cc * 128: ks * T + cc * 128 + 128],
                                           qkT[qoff:qoff + 64, qs * T + j * JW + sc0: qs * T + (j + 1) * JW],
                                           start=True, stop=True,
                                           tile_position=(64 * hl, 0))
                            if last:
                                ins.then_inc(s_tn.sem, 1)
                        em("tensor", t_s)
                    s_tn.bump()
                    wait("scalar", s_tn, s_tn.n)
                    if pv_guard:
                        wait("scalar", s_tn, pv_guard)
                        pv_guard = 0
                    def s_exp(s, a=a, cc=cc, sc0=sc0, bankp=bankp):
                        if sc0 == 0:
                            s.activation(pT[a][:, cc * 2 * JW: (cc + 1) * 2 * JW],
                                         bankp[:, 0:2 * JW],
                                         AF.Exp).then_inc(s_sp.sem, 1)
                        else:
                            src_v = bankp.rearrange("p (t w) -> p t w", w=JW)[:, :, sc0:]
                            dst_v = pT[a][:, cc * 2 * JW: (cc + 1) * 2 * JW].rearrange(
                                "p (t w) -> p t w", w=JW)[:, :, sc0:]
                            s.activation(dst_v, src_v, AF.Exp).then_inc(s_sp.sem, 1)
                    em("scalar", s_exp); s_sp.bump()
                    s_ev_n.append(s_sp.n)
                    s_cnt[0] += 1
                    exp_n[(a, cc)] = s_sp.n
                    if 4 * j <= cc and (cc - 4 * j) * 128 < JW:
                        s0 = cc - 4 * j
                        wait("vector", s_sp, s_sp.n)
                        for hl in range(2):
                            def v_mask(v, a=a, cc=cc, s0=s0, hl=hl):
                                sl_ = pT[a][:, cc * 2 * JW + hl * JW + s0 * 128:
                                            cc * 2 * JW + hl * JW + s0 * 128 + 128]
                                v.tensor_mul(sl_, sl_, mask01[:, :]).then_inc(s_vp.sem, 1)
                            em("vector", v_mask); s_vp.bump()
                        mask_n[(a, cc)] = s_vp.n
            ntile = JW // 128
            copy_n = []
            slots = []
            for tl in range(ntile):
                gidx = j * ntile + tl
                for hl in range(2):
                    voff = 136 * hl
                    apar = pv_groups[0] % 2
                    base = apar * 512
                    if len(vd1_n) >= 2:
                        wait("tensor", s_vp, vd1_n[-2])
                    for a in range(2):
                        wait("tensor", s_sp, exp_n[(a, gidx)])
                        wait("tensor", s_vp, mask_n[(a, gidx)])
                        for cc in range(gidx + 1):
                            def t_pv(t, a=a, cc=cc, tl=tl, hl=hl, voff=voff,
                                     base=base, gidx=gidx,
                                     last=(a == 1 and cc == gidx)):
                                ins = t.matmul(pCD[:, base + a * 132: base + a * 132 + 129],
                                               pT[a][:, cc * 2 * JW + hl * JW + tl * 128:
                                                      cc * 2 * JW + hl * JW + tl * 128 + 128],
                                               vbuf[:, cc * VCH + voff: cc * VCH + voff + 129],
                                               start=(cc == 0), stop=(cc == gidx))
                                if last:
                                    ins.then_inc(s_tn.sem, 1)
                            em("tensor", t_pv)
                    s_tn.bump()
                    pv_groups[0] += 1
                    pv_hist[0] = s_tn.n
                    wait("vector", s_tn, s_tn.n)
                    sl = gslot[0] % NSL
                    gslot[0] += 1
                    slots.append(sl)
                    def v_cp(v, sl=sl, base=base):
                        v.tensor_copy(
                            asb[sl].rearrange("p (t c) -> p t c", c=132)[:, :, 0:129],
                            pCD[:, base:base + 264].rearrange("p (t c) -> p t c", c=132)[:, :, 0:129]
                        ).then_inc(s_vp.sem, 1)
                    em("vector", v_cp); s_vp.bump()
                    vd1_n.append(s_vp.n)
                    copy_n.append(s_vp.n)

            if len(pend) >= 2:
                flush_chunk(*pend.pop(0))

            nidx = len(slots)
            st_n = [list(copy_n)]
            cur = []
            for ix in range(nidx):
                sl = slots[ix]
                wait("vector", s_vp, st_n[0][ix])
                def v0(v, sl=sl):
                    v.reciprocal(tiny[:, 64 + 4 * sl: 64 + 4 * sl + 2],
                                 asb[sl].rearrange("p (t c) -> p t c", c=132)[:, :, 128:129]
                                 ).then_inc(s_vp.sem, 1)
                em("vector", v0); s_vp.bump()
                cur.append(s_vp.n)
            st_n.append(cur)
            cur = []
            for ix in range(nidx):
                sl = slots[ix]
                wait("vector", s_vp, st_n[1][ix])
                def v1(v, sl=sl):
                    v.tensor_scalar(t2("r2", sl), t2("r2", sl), tc("lam"), -1.0,
                                    ALU.mult, ALU.mult).then_inc(s_vp.sem, 1)
                em("vector", v1); s_vp.bump()
                cur.append(s_vp.n)
            st_n.append(cur)
            cur = []
            for ix in range(nidx):
                sl = slots[ix]
                wait("vector", s_vp, st_n[1][ix])
                def v2(v, sl=sl):
                    v.tensor_scalar(asb[sl][:, 0:128], asb[sl][:, 0:128],
                                    t2("r1", sl), None, ALU.mult).then_inc(s_vp.sem, 1)
                em("vector", v2); s_vp.bump()
                cur.append(s_vp.n)
            st_n.append(cur)
            cur = []
            for ix in range(nidx):
                sl = slots[ix]
                wait("vector", s_vp, max(st_n[2][ix], st_n[3][ix]))
                def v3(v, sl=sl):
                    v.scalar_tensor_tensor(asb[sl][:, 0:128], asb[sl][:, 132:260],
                                           t2("r2", sl), asb[sl][:, 0:128],
                                           ALU.mult, ALU.add).then_inc(s_vp.sem, 1)
                em("vector", v3); s_vp.bump()
                cur.append(s_vp.n)
            st_n.append(cur)
            cur = []
            for ix in range(nidx):
                sl = slots[ix]
                wait("vector", s_vp, st_n[4][ix])
                wait("vector", s_sp, ln5_guard[sl])
                def v4(v, sl=sl):
                    v.scalar_tensor_tensor(asb[sl][:, 132:260], asb[sl][:, 0:128],
                                           1.0, asb[sl][:, 0:128], ALU.mult,
                                           ALU.mult,
                                           accum_out=t2("ms", sl)).then_inc(s_vp.sem, 1)
                em("vector", v4); s_vp.bump()
                cur.append(s_vp.n)
            st_n.append(cur)
            if pend_sc[0] is not None:
                emit_chain_tail(*pend_sc[0])
            pend_sc[0] = (2 * p, j, slots, list(st_n[5]))

    while len(pend) >= 2:
        flush_chunk(*pend.pop(0))
    if pend_sc[0] is not None:
        emit_chain_tail(*pend_sc[0])
        pend_sc[0] = None
    while pend:
        flush_chunk(*pend.pop(0))
    phase2_tn_end = s_tn.n
    phase2_vp_end = s_vp.n
    phase2_sp_end = s_sp.n

    wait("gpsimd", s_tn, wfree_n[-1])
    def g_wp(g):
        src = wproj.ap().rearrange("(c q) (dc d) -> q dc c d", q=128, d=512)
        dst = wproj_sb.rearrange("q (dc c d) -> q dc c d", dc=4, d=512)
        g.dma_start(dst, src).then_inc(dch["wp"].sem, 16)
    em("gpsimd", g_wp)
    dch["wp"].bump()
    wp_done = dch["wp"].n

    projev_n = []
    proj_last_tn = 0
    dT_guard = [0, 0]
    dT_n = None
    x1s_guard = [0, 0]
    hts_guard = [0, 0]
    rtv_n = []
    h_sp_done = {}
    h_vp_done = {}
    bias3_n = {}
    htr3_done = {}

    def apply3(pj):
        par = pj % 2
        wait("scalar", s_vp, bias3_n[pj])
        def s3_ap(s, par=par, xpar=pj % 4):
            s.activation(xbuf[xpar][:, :], x1t[par][:, :], AF.Identity,
                         bias=tc("bias", par),
                         scale=tc("rstd", par)).then_inc(s_sp.sem, 1)
        em("scalar", s3_ap); s_sp.bump()
        h_sp_done[pj] = s_sp.n
        wait("vector", s_sp, s_sp.n)
        wait("vector", dch["pre"], pre_done)
        if pj >= 2:
            wait("vector", s_tn, htr3_done[pj - 2])
        def v3_h(v, par=par, xpar=pj % 4):
            v.tensor_mul(hbuf[par][:, :], xbuf[xpar][:, :],
                         rmsw[:, :]).then_inc(s_vp.sem, 1)
        em("vector", v3_h); s_vp.bump()
        h_vp_done[pj] = s_vp.n

    def flush3(pj):
        wait("tensor", s_vp, h_vp_done[pj])
        g = [(dch["hts%d" % (pj % 2)], hts_guard[pj % 2])] if pj >= 2 else []
        transpose_tile(hbuf[pj % 2], hTstage[pj % 2], first_evac_guards=g)
        htr3_done[pj] = s_tn.n
        hT_ready = s_vp.n
        wait("sync", s_vp, hT_ready)
        chh = dch["hts%d" % (pj % 2)]
        def sy_hT(sy, pj=pj, ch=chh):
            dst = hT_hbm.ap()[:, :, pj * 128:(pj + 1) * 128].rearrange("c p w -> p c w")
            src = hTstage[pj % 2].rearrange("p (c w) -> p c w", w=128)
            sy.dma_start(dst, src).then_inc(ch.sem, 16)
        em("sync", sy_hT); chh.bump()
        hts_guard[pj % 2] = chh.n
        wait("tensor", s_vp, hT_ready)
        wait("tensor", dch["rw"], routw_done)
        if pj >= 1:
            wait("tensor", s_vp, rtv_n[pj - 1])
        for c in range(NC):
            def t_rt(t, c=c, pj=pj):
                ins = t.matmul(pCD[:, 0:8], hTstage[pj % 2][:, c * 128:(c + 1) * 128],
                               routw[:, c * 8:(c + 1) * 8],
                               start=(c == 0), stop=(c == NC - 1))
                if c == NC - 1:
                    ins.then_inc(s_tn.sem, 1)
            em("tensor", t_rt)
        s_tn.bump()
        wait("vector", s_tn, s_tn.n)
        vchain([
            lambda v: v.reduce_max(tc("m1"), pCD[:, 0:8], axis=AX.X).then_inc(s_vp.sem, 1),
            lambda v: v.tensor_scalar(i8[:, 0:8], pCD[:, 0:8], tc("m1"), None,
                                      ALU.is_ge).then_inc(s_vp.sem, 1),
            lambda v: v.scalar_tensor_tensor(i8[:, 8:16], i8[:, 0:8], -1e30,
                                             pCD[:, 0:8], ALU.mult,
                                             ALU.add).then_inc(s_vp.sem, 1),
            lambda v: v.reduce_max(tc("m2"), i8[:, 8:16], axis=AX.X).then_inc(s_vp.sem, 1),
            lambda v: v.tensor_sub(tc("dlt"), tc("m1"), tc("m2")).then_inc(s_vp.sem, 1),
        ])
        def v_i2(v):
            v.tensor_scalar(i8[:, 16:24], i8[:, 8:16], tc("m2"), None,
                            ALU.is_ge).then_inc(s_vp.sem, 1)
        em("vector", v_i2); s_vp.bump()
        wait("scalar", s_vp, s_vp.n)
        def s_w1(s):
            s.activation(tc("w1"), tc("dlt"), AF.Exp,
                         scale=-1.0).then_inc(s_sp.sem, 1)
        em("scalar", s_w1); s_sp.bump()
        wait("vector", s_sp, s_sp.n)
        vchain([
            lambda v: v.tensor_single_scalar(tc("w1"), tc("w1"), 1.0,
                                             ALU.add).then_inc(s_vp.sem, 1),
            lambda v: v.reciprocal(tc("w1"), tc("w1")).then_inc(s_vp.sem, 1),
            lambda v: v.tensor_scalar(tc("w2"), tc("w1"), -1.0, 1.0, ALU.mult,
                                      ALU.add).then_inc(s_vp.sem, 1),
            lambda v: v.tensor_scalar(i8[:, 24:32], i8[:, 0:8], tc("w1"), None,
                                      ALU.mult).then_inc(s_vp.sem, 1),
            lambda v, pj=pj: v.scalar_tensor_tensor(s_route[:, pj * 8:(pj + 1) * 8],
                                                    i8[:, 16:24], tc("w2"),
                                                    i8[:, 24:32], ALU.mult,
                                                    ALU.add).then_inc(s_vp.sem, 1),
        ])
        rtv_n.append(s_vp.n)

    n_dch = NT // (JW // 128)
    dT_loads = {}

    def emit_dT_load(cx, extra_tn):
        chD = dch["dT%d" % (cx % 2)]
        wait("sync", dch["dst0"], dstc[cx][0])
        wait("sync", dch["dst1"], dstc[cx][1])
        wait("sync", s_tn, extra_tn)
        def sy_dT(sy, cx=cx, ch=chD):
            src = dnT_hbm.ap()[:, :, cx * JW:(cx + 1) * JW].rearrange("c p w -> p c w")
            dst = dTbuf[cx % 2].rearrange("q (c w) -> q c w", w=JW)
            sy.dma_start(dst, src).then_inc(ch.sem, 16)
        em("sync", sy_dT); chD.bump()
        dT_loads[cx] = (chD, chD.n)

    emit_dT_load(0, wfree_n[-1])
    if n_dch > 1:
        emit_dT_load(1, phase2_tn_end)
    for j in range(NT):
        cx = j // (JW // 128)
        tl = j % (JW // 128)
        if tl == 0:
            if cx + 1 < n_dch and (cx + 1) not in dT_loads:
                emit_dT_load(cx + 1, dT_guard[(cx + 1) % 2])
            dT_n = dT_loads[cx]
        chx = dch["x3%d" % (j % 4)]
        if j >= 4:
            wait("sync", s_vp, projev_n[4 * (j - 4) + 3])
            wait("sync", s_vp, h_vp_done[j - 4])
            wait("sync", s_sp, h_sp_done[j - 4])
        def sy_x(sy, j=j, ch=chx):
            sy.dma_start(xbuf[j % 4][:, :],
                         x_in.ap()[j * 128:(j + 1) * 128, :]).then_inc(ch.sem, 16)
        em("sync", sy_x); chx.bump()
        x3j = (chx, chx.n)
        wait("tensor", dT_n[0], dT_n[1])
        if j == 0:
            wait("tensor", s_vp, phase2_vp_end)
            wait("tensor", s_sp, phase2_sp_end)
        for dcx in range(4):
            kk = 4 * j + dcx
            wait("tensor", dch["wp"], wp_done)
            bank = bank512[kk % 2]
            if kk >= 2:
                wait("tensor", s_vp, projev_n[kk - 2])
            for c in range(NC):
                def t_pj(t, c=c, dcx=dcx, tl=tl, cx=cx, bank=bank):
                    ins = t.matmul(bank[:, 0:512],
                                   dTbuf[cx % 2][:, c * JW + tl * 128: c * JW + tl * 128 + 128],
                                   wproj_sb[:, dcx * NC * 512 + c * 512: dcx * NC * 512 + (c + 1) * 512],
                                   start=(c == 0), stop=(c == NC - 1))
                    if c == NC - 1:
                        ins.then_inc(s_tn.sem, 1)
                em("tensor", t_pj)
            s_tn.bump()
            wait("vector", s_tn, s_tn.n)
            wait("vector", *x3j)
            if j >= 2 and dcx == 0:
                wait("vector", dch["x1s%d" % (j % 2)], x1s_guard[j % 2])
                wait("vector", s_sp, h_sp_done[j - 2])
            def v_pe(v, j=j, dcx=dcx, bank=bank):
                v.tensor_add(x1t[j % 2][:, dcx * 512:(dcx + 1) * 512], bank[:, 0:512],
                             xbuf[j % 4][:, dcx * 512:(dcx + 1) * 512]).then_inc(s_vp.sem, 1)
            em("vector", v_pe); s_vp.bump()
            projev_n.append(s_vp.n)
        dT_guard[cx % 2] = s_tn.n
        if j == NT - 1:
            proj_end_tn = s_tn.n
        wait("sync", s_vp, projev_n[-1])
        chs = dch["x1s%d" % (j % 2)]
        def sy_x1(sy, j=j, ch=chs):
            sy.dma_start(x1_hbm.ap()[j * 128:(j + 1) * 128, :],
                         x1t[j % 2][:, :]).then_inc(ch.sem, 16)
        em("sync", sy_x1); chs.bump()
        x1s_guard[j % 2] = chs.n
        if j == NT // 2 - 1:
            x1s_half = (dch["x1s0"].n, dch["x1s1"].n)
        if j >= 2:
            flush3(j - 2)
        if j >= 1:
            apply3(j - 1)
        wait("scalar", s_vp, projev_n[-1])
        wait("vector", s_vp, projev_n[-1])
        par = j % 2
        def s3_sq(s, j=j, par=par):
            s.activation(sqjunk[:, :], x1t[par][:, :], AF.Square,
                         accum_out=tc("sumsq", par)).then_inc(s_sp.sem, 1)
        em("scalar", s3_sq); s_sp.bump()
        sq3_n = s_sp.n
        def v3_red(v, j=j, par=par):
            v.reduce_sum(tc("sum", par), x1t[par][:, :], axis=AX.X).then_inc(s_vp.sem, 1)
        em("vector", v3_red); s_vp.bump()
        wait("vector", s_vp, s_vp.n)
        def v3_msq(v, par=par):
            v.tensor_scalar(tc("varD", par), tc("sum", par), tc("sum", par),
                            1.0 / D, ALU.mult, ALU.mult).then_inc(s_vp.sem, 1)
        em("vector", v3_msq); s_vp.bump()
        wait("vector", s_vp, s_vp.n)
        wait("vector", s_sp, sq3_n)
        def v3_var(v, par=par):
            v.tensor_sub(tc("varD", par), tc("sumsq", par),
                         tc("varD", par)).then_inc(s_vp.sem, 1)
        em("vector", v3_var); s_vp.bump()
        wait("scalar", s_vp, s_vp.n)
        def s3_ln(s, par=par):
            s.activation(tc("rstd", par), tc("varD", par), AF.Ln,
                         bias=tc("eps11"),
                         scale=(1.0 + EPS_RMS_FFN) / D).then_inc(s_sp.sem, 1)
        em("scalar", s3_ln); s_sp.bump()
        wait("scalar", s_sp, s_sp.n)
        def s3_ex(s, par=par):
            s.activation(tc("rstd", par), tc("rstd", par), AF.Exp,
                         scale=-0.5).then_inc(s_sp.sem, 1)
        em("scalar", s3_ex); s_sp.bump()
        wait("vector", s_sp, s_sp.n)
        def v3_bias(v, par=par):
            v.tensor_scalar(tc("bias", par), tc("sum", par), tc("rstd", par),
                            -1.0 / D, ALU.mult, ALU.mult).then_inc(s_vp.sem, 1)
        em("vector", v3_bias); s_vp.bump()
        bias3_n[j] = s_vp.n
    apply3(NT - 1)
    flush3(NT - 2)
    flush3(NT - 1)

    phase3_tn_end = s_tn.n
    phase3_vp_end = s_vp.n
    phase3_sp_end = s_sp.n
    if dump_sroute is not None:
        wait("sync", s_vp, s_vp.n)
        def sy_sr(sy):
            sy.dma_start(dump_sroute.ap(), s_route[:, :]).then_inc(dch["pre"].sem, 16)
        em("sync", sy_sr); dch["pre"].bump()

    g_n = []
    u_n = []
    sil_n = []
    amul_n = []
    dn_n = []
    dnev_n = []
    yst_n = [0, 0]
    u_half_end = 0

    def wg_src(e, ft, which):
        if e < NE:
            b = wg_in if which == 0 else wu_in
            return b.ap()[e, ft].rearrange("c p f -> p c f")
        b = swg_in if which == 0 else swu_in
        return b.ap()[ft].rearrange("c p f -> p c f")

    hth2_n = [0]
    yini0_n = [0]
    for hf in range(2):
        if hf == 0:
            chh = dch["hth"]
            wait("sync", dch["hts0"], dch["hts0"].n)
            wait("sync", dch["hts1"], dch["hts1"].n)
            def sy_hth(sy, ch=chh):
                src = hT_hbm.ap()[:, :, 0:HALF].rearrange("c p w -> p c w")
                dst = hT_half.rearrange("q (c w) -> q c w", w=HALF)
                sy.dma_start(dst, src).then_inc(ch.sem, 16)
            em("sync", sy_hth); chh.bump()
            hth_n = chh.n
        else:
            hth_n = hth2_n[0]
        if hf == 0:
            yini_n = None
        else:
            wait("sync", dch["x1s0"], dch["x1s0"].n)
            wait("sync", dch["x1s1"], dch["x1s1"].n)
            wait("sync", dch["yst"], yst_n[0])
            def sy_yi(sy, hf=hf):
                src = x1_hbm.ap()[hf * HALF:(hf + 1) * HALF, :]
                src = src.rearrange("(tt p) d -> p tt d", p=128)
                dst = ybuf.rearrange("p (tt d) -> p tt d", d=D)
                sy.dma_start(dst, src).then_inc(dch["yini"].sem, 16)
            em("sync", sy_yi); dch["yini"].bump()
            yini_n = dch["yini"].n

        for e in range(NE + 1):
            eg = hf * (NE + 1) + e
            if eg == 0:
                wait("gpsimd", s_tn, proj_end_tn)
                wait("gpsimd", dch["dT0"], dch["dT0"].n)
                wait("gpsimd", dch["dT1"], dch["dT1"].n)
            NWC = 4

            def emit_wd_chunk(ci, e=e, eg=eg):
                if ci == 0:
                    if eg == 0:
                        wait("gpsimd", s_tn, phase3_tn_end)
                        wait("gpsimd", s_sp, phase3_sp_end)
                        wait("gpsimd", s_vp, phase3_vp_end)
                        wait("gpsimd", dch["x1s0"], dch["x1s0"].n)
                        wait("gpsimd", dch["x1s1"], dch["x1s1"].n)
                        wait("gpsimd", dch["hts0"], dch["hts0"].n)
                        wait("gpsimd", dch["hts1"], dch["hts1"].n)
                    else:
                        wait("gpsimd", s_tn, dn_n[-1])
                fc0 = ci * (NF // NWC)
                fc1 = (ci + 1) * (NF // NWC)
                def g_wd(g, e=e, fc0=fc0, fc1=fc1):
                    if e < NE:
                        src = wd_in.ap()[e].rearrange("(fc p) d -> p fc d", p=128)
                    else:
                        src = swd_in.ap().rearrange("(fc p) d -> p fc d", p=128)
                    dst = wd_sb.rearrange("p (fc d) -> p fc d", d=D)
                    g.dma_start(dst[:, fc0:fc1, :],
                                src[:, fc0:fc1, :]).then_inc(dch["wd"].sem, 16)
                em("gpsimd", g_wd); dch["wd"].bump()
            for ft in range(NF):
                blk = eg * NF + ft
                chb = dch["wg%d" % (blk % 2)]
                if blk >= 2:
                    wait("gpsimd", s_tn, u_n[(blk - 2 + 1) * n_tch - 1])
                def g_blk(g, e=e, ft=ft, blk=blk, ch=chb):
                    g.dma_start(wgu_st[0][blk % 2].rearrange("p (c f) -> p c f", f=128),
                                wg_src(e, ft, 0)).then_inc(ch.sem, 16)
                    g.dma_start(wgu_st[1][blk % 2].rearrange("p (c f) -> p c f", f=128),
                                wg_src(e, ft, 1)).then_inc(ch.sem, 16)
                em("gpsimd", g_blk)
                chb.bump(); chb.bump()
                blk_n = chb.n
                if eg == 0 and ft == 0:
                    wait("gpsimd", dch["x1s0"], x1s_half[0])
                    wait("gpsimd", dch["x1s1"], x1s_half[1])
                    def g_yi0(g):
                        src = x1_hbm.ap()[0:HALF, :]
                        src = src.rearrange("(tt p) d -> p tt d", p=128)
                        dst = ybuf.rearrange("p (tt d) -> p tt d", d=D)
                        g.dma_start(dst, src).then_inc(dch["yini"].sem, 16)
                    em("gpsimd", g_yi0); dch["yini"].bump()
                    yini0_n[0] = dch["yini"].n
                if 1 <= ft <= NWC:
                    emit_wd_chunk(ft - 1)
                wait("tensor", chb, blk_n)
                wait("tensor", dch["hth"], hth_n)
                for tcx in range(n_tch):
                    kk = blk * n_tch + tcx
                    bg = bank512[kk % 2]
                    bu = bank512[2 + kk % 2]
                    if len(sil_n) >= 2:
                        wait("tensor", s_sp, sil_n[-2])
                    if len(amul_n) >= 2:
                        wait("tensor", s_vp, amul_n[-2])
                    if kk == 0:
                        wait("tensor", s_vp, phase3_vp_end)
                        wait("tensor", s_sp, phase3_sp_end)
                    for c in range(NC):
                        def t_g(t, c=c, tcx=tcx, blk=blk, bg=bg):
                            ins = t.matmul(bg[:, 0:TCW],
                                           wgu_st[0][blk % 2][:, c * 128:(c + 1) * 128],
                                           hT_half[:, c * HALF + tcx * TCW: c * HALF + tcx * TCW + TCW],
                                           start=(c == 0), stop=(c == NC - 1))
                            if c == NC - 1:
                                ins.then_inc(s_tn.sem, 1)
                        em("tensor", t_g)
                    s_tn.bump()
                    g_n.append(s_tn.n)
                    for c in range(NC):
                        def t_u(t, c=c, tcx=tcx, blk=blk, bu=bu):
                            ins = t.matmul(bu[:, 0:TCW],
                                           wgu_st[1][blk % 2][:, c * 128:(c + 1) * 128],
                                           hT_half[:, c * HALF + tcx * TCW: c * HALF + tcx * TCW + TCW],
                                           start=(c == 0), stop=(c == NC - 1))
                            if c == NC - 1:
                                ins.then_inc(s_tn.sem, 1)
                        em("tensor", t_u)
                    s_tn.bump()
                    u_n.append(s_tn.n)
                    wait("scalar", s_tn, g_n[-1])
                    spar = len(sil_n) % 2
                    if not sil_n:
                        wait("scalar", dch["hts0"], dch["hts0"].n)
                        wait("scalar", dch["hts1"], dch["hts1"].n)
                    if len(sil_n) >= 2:
                        wait("scalar", s_vp, amul_n[-2])
                    def s_sil(s, spar=spar, bg=bg):
                        s.activation(sgscr[spar][:, :], bg[:, 0:TCW],
                                     AF.Silu).then_inc(s_sp.sem, 1)
                    em("scalar", s_sil); s_sp.bump()
                    sil_n.append(s_sp.n)
                    wait("vector", s_sp, sil_n[-1])
                    wait("vector", s_tn, u_n[-1])
                    if eg >= 1 and ft == 0 and tcx == 0:
                        wait("vector", s_tn, dn_n[-1])
                    if eg == 0 and ft == 0 and tcx == 0:
                        wait("vector", dch["hts0"], dch["hts0"].n)
                        wait("vector", dch["hts1"], dch["hts1"].n)
                        wait("vector", s_tn, phase3_tn_end)
                        wait("vector", dch["dT0"], dch["dT0"].n)
                        wait("vector", dch["dT1"], dch["dT1"].n)
                    def v_am(v, ft=ft, tcx=tcx, spar=spar, bu=bu):
                        v.tensor_mul(abuf[:, ft * HALF + tcx * TCW: ft * HALF + tcx * TCW + TCW],
                                     sgscr[spar][:, :], bu[:, 0:TCW]).then_inc(s_vp.sem, 1)
                    em("vector", v_am); s_vp.bump()
                    amul_n.append(s_vp.n)
            a_ready = s_vp.n
            wait("tensor", dch["wd"], dch["wd"].n)
            wait("tensor", s_vp, a_ready)
            for tl in range(NTH):
                for dcx in range(4):
                    kk = len(dn_n)
                    bank = bank512[4 + kk % 2]
                    if kk >= 2:
                        wait("tensor", s_vp, dnev_n[kk - 2])
                    if kk == 0:
                        wait("tensor", s_vp, phase3_vp_end)
                    for fc in range(NF):
                        def t_dn(t, fc=fc, tl=tl, dcx=dcx, eg=eg, bank=bank):
                            ins = t.matmul(bank[:, 0:512],
                                           abuf[:, fc * HALF + tl * 128: fc * HALF + tl * 128 + 128],
                                           wd_sb[:, fc * D + dcx * 512: fc * D + (dcx + 1) * 512],
                                           start=(fc == 0), stop=(fc == NF - 1))
                            if fc == NF - 1:
                                ins.then_inc(s_tn.sem, 1)
                        em("tensor", t_dn)
                    s_tn.bump()
                    dn_n.append(s_tn.n)
                    wait("vector", s_tn, dn_n[-1])
                    if e == 0 and tl == 0 and dcx == 0:
                        wait("vector", dch["yini"],
                             yini0_n[0] if yini_n is None else yini_n)
                    def v_dne(v, e=e, tl=tl, dcx=dcx, hf=hf, bank=bank):
                        ysl = ybuf[:, tl * D + dcx * 512: tl * D + (dcx + 1) * 512]
                        if e < NE:
                            cix = (hf * NTH + tl) * 8 + e
                            v.scalar_tensor_tensor(ysl, bank[:, 0:512],
                                                   s_route[:, cix:cix + 1], ysl,
                                                   ALU.mult, ALU.add).then_inc(s_vp.sem, 1)
                        else:
                            v.tensor_add(ysl, ysl, bank[:, 0:512]).then_inc(s_vp.sem, 1)
                    em("vector", v_dne); s_vp.bump()
                    dnev_n.append(s_vp.n)
            if hf == 0 and e == NE:
                u_half_end = u_n[-1]
        if hf == 0:
            chh2 = dch["hth"]
            wait("sync", s_tn, u_n[-1])
            def sy_hth2(sy, ch=chh2):
                src_ = hT_hbm.ap()[:, :, HALF:2 * HALF].rearrange("c p w -> p c w")
                dst_ = hT_half.rearrange("q (c w) -> q c w", w=HALF)
                sy.dma_start(dst_, src_).then_inc(ch.sem, 16)
            em("sync", sy_hth2); chh2.bump()
            hth2_n[0] = chh2.n
        wait("sync", s_vp, dnev_n[-1])
        def sy_yst(sy, hf=hf):
            src = ybuf.rearrange("p (tt d) -> p tt d", d=D)
            dst = out_t.ap()[hf * HALF:(hf + 1) * HALF, :]
            dst = dst.rearrange("(tt p) d -> p tt d", p=128)
            sy.dma_start(dst, src).then_inc(dch["yst"].sem, 16)
        em("sync", sy_yst); dch["yst"].bump()
        yst_n[hf] = dch["yst"].n

    for nm, ch in dch.items():
        if nm != "yst" and ch.n > 0:
            wait("sync", ch, ch.n)
    wait("sync", dch["yst"], dch["yst"].n)

    with nc.Block() as block:
        @block.tensor
        def _(e):
            for f in ops["tensor"]:
                f(e)

        @block.vector
        def _(e):
            for f in ops["vector"]:
                f(e)

        @block.scalar
        def _(e):
            for f in ops["scalar"]:
                f(e)

        @block.gpsimd
        def _(e):
            for f in ops["gpsimd"]:
                f(e)

        @block.sync
        def _(e):
            for f in ops["sync"]:
                f(e)

    return nc


def _prep_host(inputs):
    f32 = lambda a: np.ascontiguousarray(np.asarray(a, dtype=np.float32))
    x = f32(inputs["L0_attn_x"])
    rmsw = np.ascontiguousarray(
        np.broadcast_to(f32(inputs["L0_ffn_rms_weight"])[None, :], (128, D)))
    lam = np.ascontiguousarray(np.concatenate([
        np.broadcast_to(f32(inputs[k])[None, :], (128, DH))
        for k in ("L0_attn_lambda_q1", "L0_attn_lambda_k1",
                  "L0_attn_lambda_q2", "L0_attn_lambda_k2")], axis=1))
    tile_gu = lambda w: np.ascontiguousarray(
        w.reshape(-1, NC, 128, NF, 128).transpose(0, 3, 1, 2, 4))
    wg_t = tile_gu(f32(inputs["L0_ffn_w_gate"]))
    wu_t = tile_gu(f32(inputs["L0_ffn_w_up"]))
    swg_t = np.ascontiguousarray(
        f32(inputs["L0_ffn_sw_gate"]).reshape(NC, 128, NF, 128).transpose(2, 0, 1, 3))
    swu_t = np.ascontiguousarray(
        f32(inputs["L0_ffn_sw_up"]).reshape(NC, 128, NF, 128).transpose(2, 0, 1, 3))
    common = dict(wqkv=f32(inputs["L0_attn_Wqkv"]),
                  wproj=f32(inputs["L0_attn_Wproj"]),
                  rmsw_in=rmsw, lam_in=lam,
                  routw_in=f32(inputs["L0_ffn_router_w"]),
                  wg_in=wg_t, wu_in=wu_t,
                  wd_in=f32(inputs["L0_ffn_w_down"]),
                  swg_in=swg_t, swu_in=swu_t,
                  swd_in=f32(inputs["L0_ffn_sw_down"]))
    return x, common


def kernel(**inputs):
    x, common = _prep_host(inputs)
    B = x.shape[0]
    nc = build(T=x.shape[1])
    in_maps = []
    for b in range(B):
        m = dict(common)
        m["x_in"] = np.ascontiguousarray(x[b])
        in_maps.append(m)
    res = run_bass_kernel_spmd(nc, in_maps, core_ids=list(range(B)))
    out = np.stack([np.asarray(res.results[b]["out"]) for b in range(B)], axis=0)
    return out.astype(np.float32)


if __name__ == "__main__":
    nc = build()
    print("instructions:", nc.n_instructions())


# version 19
# speedup vs baseline: 1.1575x; 1.0417x over previous
import math
import numpy as np
import concourse.bass as bass
import concourse.mybir as mybir
from concourse.bass_utils import run_bass_kernel_spmd

F32 = mybir.dt.float32
BF16 = mybir.dt.bfloat16
AF = mybir.ActivationFunctionType
ALU = mybir.AluOpType
AX = mybir.AxisListType

D = 2048
NH = 16
DH = 64
DV = 128
FF = 1024
NE = 8
NF = FF // 128
NC = D // 128
LAMBDA_INIT = 0.8 - 0.6 * math.exp(-0.3)
EPS_LN = 1e-5
EPS_RMS_ATTN = 1e-5
EPS_RMS_FFN = 1e-6
VCH = 272


class Ch:
    def __init__(self, nc, name, step=1):
        self.sem = nc.alloc_semaphore(name)
        self.n = 0
        self.step = step

    def bump(self):
        self.n += self.step
        return self.n


def build(T=2048, dumps=()):
    nc = bass.Bass("TRN2", target_bir_lowering=False)
    NT = T // 128
    JW = min(512, T)
    NJ = T // JW
    HALF = T // 2
    NTH = HALF // 128
    TCW = min(512, HALF)
    n_tch = HALF // TCW
    QH = D // 2

    x_in = nc.dram_tensor("x_in", [T, D], F32, kind="ExternalInput")
    qkw_in = nc.dram_tensor("qkw_in", [8, 128, NC, 4, 128], F32, kind="ExternalInput")
    vw_in = nc.dram_tensor("vw_in", [8, 128, NC, 256], F32, kind="ExternalInput")
    wproj = nc.dram_tensor("wproj", [128, 4, NC, 512], F32, kind="ExternalInput")
    rmsw_in = nc.dram_tensor("rmsw_in", [128, D], F32, kind="ExternalInput")
    lam_in = nc.dram_tensor("lam_in", [128, 4 * DH], F32, kind="ExternalInput")
    routw_in = nc.dram_tensor("routw_in", [D, NE], F32, kind="ExternalInput")
    wg_in = nc.dram_tensor("wg_in", [NE, NF, 128, NC, 128], F32, kind="ExternalInput")
    wu_in = nc.dram_tensor("wu_in", [NE, NF, 128, NC, 128], F32, kind="ExternalInput")
    wd_in = nc.dram_tensor("wd_in", [NE, FF, D], F32, kind="ExternalInput")
    swg_in = nc.dram_tensor("swg_in", [NF, 128, NC, 128], F32, kind="ExternalInput")
    swu_in = nc.dram_tensor("swu_in", [NF, 128, NC, 128], F32, kind="ExternalInput")
    swd_in = nc.dram_tensor("swd_in", [FF, D], F32, kind="ExternalInput")
    out_t = nc.dram_tensor("out", [T, D], F32, kind="ExternalOutput")

    sk = lambda nm: "ExternalOutput" if nm in dumps else "Internal"
    lnT_hbm = nc.dram_tensor("lnT_hbm", [NC, 128, T], BF16, kind=sk("lnT_hbm"))
    dnT_hbm = nc.dram_tensor("dnT_hbm", [NC, 128, T], BF16, kind=sk("dnT_hbm"))
    hT_hbm = nc.dram_tensor("hT_hbm", [NC, 128, T], BF16, kind=sk("hT_hbm"))
    x1_hbm = nc.dram_tensor("x1_hbm", [T, D], F32, kind=sk("x1_hbm"))
    dump_sroute = None
    if "s_route" in dumps:
        dump_sroute = nc.dram_tensor("s_route_d", [128, NT * 8], F32,
                                     kind="ExternalOutput")

    ident = nc.alloc_sbuf_tensor("ident", [128, 128], BF16).ap()
    mask01 = nc.alloc_sbuf_tensor("mask01", [128, 128], BF16).ap()
    rmsw = nc.alloc_sbuf_tensor("rmsw", [128, D], F32).ap()
    lamv = nc.alloc_sbuf_tensor("lamv", [128, 4 * DH], F32).ap()
    tiny = nc.alloc_sbuf_tensor("tiny", [128, 128], F32).ap()
    i8 = nc.alloc_sbuf_tensor("i8", [128, 48], F32).ap()
    routw = nc.alloc_sbuf_tensor("routw", [128, NC * 8], BF16).ap()
    s_route = nc.alloc_sbuf_tensor("s_route", [128, NT * 8], F32).ap()
    NSL = 16
    dnscr = [nc.alloc_sbuf_tensor(f"dnscr{i}", [128, 128], BF16).ap() for i in range(NSL)]
    asb = [nc.alloc_sbuf_tensor(f"asb{i}", [128, 264], F32).ap() for i in range(NSL)]
    sqjunk = nc.alloc_sbuf_tensor("sqjunk", [128, D], BF16).ap()

    GCOL = dict(lam=0, dot1=1, dot2=2, e1=3, e2=4, m1=5, m2=6, dlt=7, w1=8,
                w2=9, eps_ln=10, eps_ffn=11, eps11=12)
    LNCOL = dict(sum=0, sumsq=1, varD=2, rstd=3, bias=4, vv=5, ta=6, q=7, r1b=8)

    def tc(name, par=0):
        if name in GCOL:
            c = GCOL[name]
        else:
            c = 32 + 16 * par + LNCOL[name]
        return tiny[:, c:c + 1]

    def ln1c(name, p):
        c = 64 + 16 * p + LNCOL[name]
        return tiny[:, c:c + 1]

    def t2(name, sl):
        c = 64 + 4 * sl + {"r1": 0, "r2": 1, "ms": 2, "rstd2": 3}[name]
        return tiny[:, c:c + 1]

    base = nc.sbuf_base
    avail = nc.sbuf_top - base

    def at(name, shape, dtype, off):
        sz = shape[1] * (4 if dtype == F32 else 2)
        assert off >= 0 and off + sz <= avail, (name, off, sz, avail)
        return nc.alloc_sbuf_tensor_at(name, shape, dtype,
                                       offset=base + off).ap()

    lnTsz = NC * JW * 2
    o = 0
    lnTc = [at(f"lnTc{i}", [128, NC * JW], BF16, o + i * lnTsz) for i in range(2)]
    o += 2 * lnTsz
    o_wqk = o
    wqk_st = [at(f"wqk{i}", [128, NC * 512], BF16, o + i * NC * 1024) for i in range(2)]
    o += 2 * NC * 1024
    o_wv = o
    wv_st = [at(f"wv{i}", [128, NC * 256], BF16, o + i * NC * 512) for i in range(2)]
    o += 2 * NC * 512
    o_qkT = o
    qkT = at("qkT", [128, 4 * T], BF16, o)
    o += 4 * T * 2
    o_pT = o
    pT = []
    for a in range(2):
        pT.append(at(f"pT{a}", [128, NT * 2 * JW], BF16, o))
        o += NT * 2 * JW * 2
    o_vbuf = o
    vbuf = at("vbuf", [128, NT * VCH], BF16, o)
    o += NT * VCH * 2
    dstage = at("dstage", [128, 2 * JW], BF16, o)
    o += 2 * JW * 2
    p12_top = o
    if 4 * D * 2 <= 2 * NT * JW * 2:
        o_lnscr = o_pT + 2 * NT * JW * 2
    else:
        o_lnscr = p12_top
        p12_top += 2 * D * 2
    lnscr = [at(f"lnscr{i}", [128, D], BF16, o_lnscr + i * D * 2) for i in range(2)]

    wproj_sb = at("wproj_sb", [128, 4 * NC * 512], BF16, 0)
    o3 = 4 * NC * 512 * 2
    o_dT = o3
    dTbuf = [at(f"dT{i}", [128, NC * JW], BF16, o3 + i * lnTsz) for i in range(2)]
    o3 += 2 * lnTsz
    o_hth = o3
    hT_half = at("hT_half", [128, NC * HALF], BF16, o3)
    o3 += NC * HALF * 2
    o_x1t = o3
    x1t = [at(f"x1t{i}", [128, D], F32, o3 + i * D * 4) for i in range(2)]
    o3 += 2 * D * 4
    hbuf = [at(f"hbuf{i}", [128, D], BF16, o3 + i * D * 2) for i in range(2)]
    o3 += 2 * D * 2
    o_hTs = o3
    hTstage = [at(f"hTs{i}", [128, NC * 128], BF16, o3 + i * NC * 256) for i in range(2)]
    o3 += 2 * NC * 256
    p3_top = o3

    ysz = NTH * D * 4
    assert ysz <= o_dT, (ysz, o_dT)
    ybuf = at("ybuf", [128, NTH * D], F32, 0)
    asz = NF * HALF * 2
    sgsz = 2 * TCW * 2
    if asz + 4 * NC * 256 <= o_hth - o_dT:
        o_a = o_dT
        o_wgu = o_dT + asz
        o_wd = o_x1t
        if o_wgu + 4 * NC * 256 + sgsz <= o_hth:
            o_sg = o_wgu + 4 * NC * 256
        else:
            o_sg = o_wd + NF * D * 2
        assert o_wgu + 4 * NC * 256 <= o_hth, (o_wgu, o_hth)
        assert o_wd >= o_hth + NC * HALF * 2
    else:
        o_wd = o_x1t
        o_sg = o_wd + NF * D * 2
        o_a = max(p12_top, p3_top, o_sg + sgsz)
        o_wgu = o_a + asz
    abuf = at("abuf", [128, NF * HALF], BF16, o_a)
    wgu_st = []
    for nmi, nm in enumerate(("wgst", "wust")):
        wgu_st.append([at(f"{nm}{i}", [128, NC * 128], BF16,
                          o_wgu + (2 * nmi + i) * NC * 256) for i in range(2)])
    wd_sb = at("wd_sb", [128, NF * D], BF16, o_wd)
    sgscr = [at(f"sg{i}", [128, TCW], BF16, o_sg + i * TCW * 2) for i in range(2)]
    p4_top = max(o_sg + sgsz, o_wgu + 4 * NC * 256, o_wd + NF * D * 2)
    if 4 * D * 4 <= 2 * NT * 2 * JW * 2 and o_pT == o_hth and 4 * D * 4 <= NC * HALF * 2:
        o_xb = o_pT
    else:
        o_xb = avail - 4 * D * 4
        assert o_xb >= max(p12_top, p3_top, p4_top), \
            (o_xb, p12_top, p3_top, p4_top)
    xbuf = [at(f"xbuf{i}", [128, D], F32, o_xb + i * D * 4) for i in range(4)]
    assert max(p12_top, p3_top, p4_top) <= avail, (p12_top, p3_top, p4_top, avail)

    pA = nc.alloc_psum_tensor("pA", [128, 1024], F32).ap()
    pB = nc.alloc_psum_tensor("pB", [128, 1024], F32).ap()
    pCD = nc.alloc_psum_tensor("pCD", [128, 1024], F32).ap()
    pbf = [nc.alloc_psum_tensor(f"pbf{i}", [128, 1024], BF16).ap() for i in range(2)]
    bank512 = [pA[:, 0:512], pA[:, 512:1024], pB[:, 0:512], pB[:, 512:1024],
               pCD[:, 0:512], pCD[:, 512:1024]]
    spair = [pA, pB]

    s_init = Ch(nc, "s_init")
    s_tn = Ch(nc, "s_tn")
    s_vp = Ch(nc, "s_vp")
    s_sp = Ch(nc, "s_sp")
    dch = {}
    for nm in ("xl0", "xl1", "xl2", "xl3", "lns0", "lns1", "lnc0", "lnc1", "wq0", "wq1",
               "dst0", "dst1", "wp", "rw", "dT0", "dT1", "x30", "x31", "x32", "x33", "x1s0",
               "x1s1", "hts0", "hts1", "hth", "wg0", "wg1", "wd", "yini",
               "yst", "pre"):
        dch[nm] = Ch(nc, "d_" + nm, 16)

    ops = {k: [] for k in ("tensor", "vector", "scalar", "gpsimd", "sync")}

    def em(eng, fn):
        ops[eng].append(fn)

    last_wait = {}

    def wait(eng, ch, val):
        assert val <= ch.n, (eng, val, ch.n)
        key = (eng, ch.sem.num)
        if val <= last_wait.get(key, 0):
            return
        last_wait[key] = val
        em(eng, lambda e, s=ch.sem, v=val: e.wait_ge(s, v))

    def g_init(g):
        steps = [
            lambda: g.memset(ident[:, :], 0.0),
            lambda: g.affine_select(out=ident[:, :], in_=ident[:, :],
                                    compare_op=ALU.not_equal, fill=1.0, base=0,
                                    pattern=[[-1, 128]], channel_multiplier=1),
            lambda: g.memset(mask01[:, :], 0.0),
            lambda: g.affine_select(out=mask01[:, :], in_=mask01[:, :],
                                    compare_op=ALU.is_gt, fill=1.0, base=0,
                                    pattern=[[-1, 128]], channel_multiplier=1),
        ]
        n = 0
        for st in steps:
            st().then_inc(s_init.sem, 1)
            n += 1
            g.wait_ge(s_init.sem, n)
    em("gpsimd", g_init)
    for _ in range(4):
        s_init.bump()
    init_done = s_init.n

    def sy_pre(sy):
        sy.dma_start(rmsw[:, :], rmsw_in.ap()).then_inc(dch["pre"].sem, 16)
        sy.dma_start(lamv[:, :], lam_in.ap()).then_inc(dch["pre"].sem, 16)
    em("sync", sy_pre)
    dch["pre"].bump(); dch["pre"].bump()
    pre_done = dch["pre"].n

    def g_rw(g):
        g.dma_start(routw.rearrange("p (c e) -> p c e", e=8),
                    routw_in.ap().rearrange("(c p) e -> p c e", p=128)
                    ).then_inc(dch["rw"].sem, 16)
    em("gpsimd", g_rw)
    dch["rw"].bump()
    routw_done = dch["rw"].n

    def v_consts(v):
        v.memset(tc("eps_ln"), EPS_LN)
        v.memset(tc("eps_ffn"), EPS_RMS_FFN)
        v.memset(tc("eps11"), EPS_LN * EPS_RMS_FFN).then_inc(s_vp.sem, 1)
    em("vector", v_consts); s_vp.bump()
    wait("vector", s_vp, s_vp.n)

    wait("vector", dch["pre"], pre_done)
    wait("vector", s_init, init_done)

    def vchain(v_ops):
        for fn in v_ops:
            em("vector", fn)
            s_vp.bump()
            wait("vector", s_vp, s_vp.n)

    vchain([
        lambda v: v.tensor_mul(lamv[:, 0:DH], lamv[:, 0:DH],
                               lamv[:, DH:2 * DH]).then_inc(s_vp.sem, 1),
        lambda v: v.reduce_sum(tc("dot1"), lamv[:, 0:DH], axis=AX.X).then_inc(s_vp.sem, 1),
        lambda v: v.tensor_mul(lamv[:, 2 * DH:3 * DH], lamv[:, 2 * DH:3 * DH],
                               lamv[:, 3 * DH:4 * DH]).then_inc(s_vp.sem, 1),
        lambda v: v.reduce_sum(tc("dot2"), lamv[:, 2 * DH:3 * DH],
                               axis=AX.X).then_inc(s_vp.sem, 1),
    ])
    wait("scalar", s_vp, s_vp.n)

    def s_lam(s):
        s.activation(tc("e1"), tc("dot1"), AF.Exp).then_inc(s_sp.sem, 1)
        s.activation(tc("e2"), tc("dot2"), AF.Exp).then_inc(s_sp.sem, 1)
    em("scalar", s_lam); s_sp.bump(); s_sp.bump()
    wait("vector", s_sp, s_sp.n)
    vchain([
        lambda v: v.tensor_sub(tc("lam"), tc("e1"), tc("e2")).then_inc(s_vp.sem, 1),
        lambda v: v.tensor_single_scalar(tc("lam"), tc("lam"), LAMBDA_INIT, ALU.add).then_inc(s_vp.sem, 1),
    ])
    lam_done_v = s_vp.n

    tp_n = [0]
    tpe_done = []

    def transpose_tile(src, dst_stage, first_evac_guards=(), dst_fn=None):
        for c0 in range(0, NC, 8):
            nn = min(8, NC - c0)
            k = tp_n[0]
            if k >= 2:
                wait("tensor", s_vp, tpe_done[k - 2])
            for ci in range(nn):
                def t_tr(t, src=src, c=c0 + ci, ci=ci, k=k, last=(ci == nn - 1)):
                    ins = t.transpose(pbf[k % 2][:, ci * 128:(ci + 1) * 128],
                                      src[:, c * 128:(c + 1) * 128], ident[:, :])
                    if last:
                        ins.then_inc(s_tn.sem, 1)
                em("tensor", t_tr)
            s_tn.bump()
            wait("vector", s_tn, s_tn.n)
            if c0 == 0:
                for g in first_evac_guards:
                    wait("vector", *g)
            if dst_fn is not None:
                for ci in range(nn):
                    def v_ev(v, dst=dst_fn(c0 + ci), ci=ci, k=k,
                             last=(ci == nn - 1)):
                        ins = v.tensor_copy(dst, pbf[k % 2][:, ci * 128:(ci + 1) * 128])
                        if last:
                            ins.then_inc(s_vp.sem, 1)
                    em("vector", v_ev)
                s_vp.bump()
            else:
                def v_ev(v, dst=dst_stage[:, c0 * 128:(c0 + nn) * 128], k=k, nn=nn):
                    v.tensor_copy(dst, pbf[k % 2][:, 0:nn * 128]).then_inc(s_vp.sem, 1)
                em("vector", v_ev); s_vp.bump()
            tpe_done.append(s_vp.n)
            tp_n[0] += 1

    wait("tensor", s_init, init_done)
    app_done = {}
    red_done = {}
    lnscr_free = {}
    lns_guard = [0, 0]
    lnTc_ready = [0, 0, 0, 0]
    xl_n = {}
    bsn1 = {}

    def p1_loads(grp):
        for i in grp:
            chx = dch["xl%d" % (i % 4)]
            if i >= 4:
                wait("sync", s_vp, app_done[i - 4])
                wait("sync", s_vp, red_done[i - 4])
            def sy_x(sy, i=i, ch=chx):
                sy.dma_start(xbuf[i % 4][:, :],
                             x_in.ap()[i * 128:(i + 1) * 128, :]).then_inc(ch.sem, 16)
            em("sync", sy_x); chx.bump()
            xl_n[i] = (chx, chx.n)

    def p1_stats(grp):
        sqn = {}; stv = {}
        for i in grp:
            wait("scalar", *xl_n[i])
            if i > 0:
                wait("scalar", s_sp, s_sp.n)
            def s_sq(s, i=i):
                s.activation(sqjunk[:, :], xbuf[i % 4][:, :], AF.Square,
                             accum_out=ln1c("sumsq", i % 4)).then_inc(s_sp.sem, 1)
            em("scalar", s_sq); s_sp.bump()
            sqn[i] = s_sp.n
        for i in grp:
            wait("vector", *xl_n[i])
            def v_red(v, i=i):
                v.reduce_sum(ln1c("sum", i % 4), xbuf[i % 4][:, :],
                             axis=AX.X).then_inc(s_vp.sem, 1)
            em("vector", v_red); s_vp.bump()
            red_done[i] = s_vp.n
        for i in grp:
            wait("vector", s_vp, red_done[i])
            def v_msq(v, i=i):
                v.tensor_scalar(ln1c("varD", i % 4), ln1c("sum", i % 4),
                                ln1c("sum", i % 4), 1.0 / D, ALU.mult,
                                ALU.mult).then_inc(s_vp.sem, 1)
            em("vector", v_msq); s_vp.bump()
            stv[i] = s_vp.n
        for i in grp:
            wait("vector", s_vp, stv[i])
            wait("vector", s_sp, sqn[i])
            def v_var(v, i=i):
                v.tensor_sub(ln1c("varD", i % 4), ln1c("sumsq", i % 4),
                             ln1c("varD", i % 4)).then_inc(s_vp.sem, 1)
            em("vector", v_var); s_vp.bump()
            stv[i] = s_vp.n
        lnn = {}
        for i in grp:
            wait("scalar", s_vp, stv[i])
            def s_ln(s, i=i):
                s.activation(ln1c("rstd", i % 4), ln1c("varD", i % 4), AF.Ln,
                             bias=tc("eps_ln"), scale=1.0 / D).then_inc(s_sp.sem, 1)
            em("scalar", s_ln); s_sp.bump()
            lnn[i] = s_sp.n
        for i in grp:
            wait("scalar", s_sp, lnn[i])
            def s_ex(s, i=i):
                s.activation(ln1c("rstd", i % 4), ln1c("rstd", i % 4), AF.Exp,
                             scale=-0.5).then_inc(s_sp.sem, 1)
            em("scalar", s_ex); s_sp.bump()
            lnn[i] = s_sp.n
        for i in grp:
            wait("vector", s_sp, lnn[i])
            def v_bias(v, i=i):
                v.tensor_scalar(ln1c("bias", i % 4), ln1c("sum", i % 4),
                                ln1c("rstd", i % 4), -1.0 / D, ALU.mult,
                                ALU.mult).then_inc(s_vp.sem, 1)
            em("vector", v_bias); s_vp.bump()
            bsn1[i] = s_vp.n

    lnsc = {}

    def p1_apply_tr(grp):
        for i in grp:
            gr = i // 4
            wait("vector", s_vp, bsn1[i])
            if i >= 2:
                wait("vector", s_tn, lnscr_free[i - 2])
            def v_ap(v, i=i):
                v.tensor_scalar(lnscr[i % 2][:, :], xbuf[i % 4][:, :],
                                ln1c("rstd", i % 4), ln1c("bias", i % 4),
                                ALU.mult, ALU.add).then_inc(s_vp.sem, 1)
            em("vector", v_ap); s_vp.bump()
            app_done[i] = s_vp.n
            wait("tensor", s_vp, app_done[i])
            g = []
            if gr >= 2:
                g = [(s_tn, lnc_guard[gr % 2]),
                     (dch["lns0"], lnsc[gr - 2][0]),
                     (dch["lns1"], lnsc[gr - 2][1])]
            transpose_tile(
                lnscr[i % 2], None, first_evac_guards=g,
                dst_fn=lambda c, i=i: lnTc[(i // 4) % 2][
                    :, c * JW + (i % 4) * 128: c * JW + (i % 4) * 128 + 128])
            lnTc_ready[i // 4] = s_vp.n
            lnscr_free[i] = s_tn.n
            if i % 4 == 3:
                gr4 = i // 4
                wait("sync", s_vp, s_vp.n)
                chs = dch["lns%d" % (gr4 % 2)]
                def sy_st(sy, gr4=gr4, ch=chs):
                    dst = lnT_hbm.ap()[:, :, gr4 * JW:(gr4 + 1) * JW].rearrange("c p w -> p c w")
                    src = lnTc[gr4 % 2].rearrange("q (c w) -> q c w", w=JW)
                    sy.dma_start(dst, src).then_inc(ch.sem, 16)
                em("sync", sy_st); chs.bump()
                lns_guard[gr4 % 2] = chs.n
        lnsc[grp[0] // 4] = (dch["lns0"].n, dch["lns1"].n)

    def g_vinit(g):
        vr = vbuf.rearrange("p (c w) -> p c w", w=VCH)
        for i, st in enumerate([
            lambda: g.memset(vbuf[:, :], 0.0),
            lambda: g.memset(vr[:, :, 128:129], 1.0),
            lambda: g.memset(vr[:, :, 264:265], 1.0),
        ]):
            st().then_inc(s_init.sem, 1)
            g.wait_ge(s_init.sem, init_done + i + 1)
    em("gpsimd", g_vinit)
    for _ in range(3):
        s_init.bump()
    vbuf_ready = s_init.n

    col0 = {"q1": 0, "k1": QH, "q2": 2 * QH, "k2": 3 * QH}
    wq_pair_n = []
    wfree_n = []
    lnc_guard = [0, 0]
    qk_evac_n = []
    v_evac_n = []
    s_ev_n = []
    s_cnt = [0]
    qk_cnt = [0]
    v_cnt = [0]
    pv_hist = [0]
    vd1_n = [0]
    dtr_slot = {i: 0 for i in range(NSL)}
    ln5_guard = {i: 0 for i in range(NSL)}
    st7_guard = {i: 0 for i in range(NSL)}
    dst_guard = [0, 0]
    dstc = {}
    pv_groups = [0]
    gslot = [0]
    pend = []
    pend_sc = [None]

    def flush_chunk(hbase, pj, dnn, slots):
        ntile = JW // 128
        k = tp_n[0]
        if k >= 2:
            wait("tensor", s_vp, tpe_done[k - 2])
        wait("tensor", s_vp, dnn[-1])
        nsl_c = len(slots)
        for hl in range(2):
            for tl in range(ntile):
                sl = slots[tl * 2 + hl]
                si = hl * ntile + tl
                def t_dtr(t, sl=sl, si=si, k=k, last=(hl == 1 and tl == ntile - 1)):
                    ins = t.transpose(pbf[k % 2][:, si * 128:(si + 1) * 128],
                                      dnscr[sl][:, :], ident[:, :])
                    if last:
                        ins.then_inc(s_tn.sem, 1)
                em("tensor", t_dtr)
                dtr_slot[sl] = s_tn.n + 1
        s_tn.bump()
        wait("vector", s_tn, s_tn.n)
        wait("vector", dch["dst0"], dst_guard[0])
        wait("vector", dch["dst1"], dst_guard[1])
        def v_dst(v, k=k, nsl_c=nsl_c):
            v.tensor_copy(dstage[:, 0:nsl_c * 128],
                          pbf[k % 2][:, 0:nsl_c * 128]).then_inc(s_vp.sem, 1)
        em("vector", v_dst); s_vp.bump()
        tpe_done.append(s_vp.n)
        tp_n[0] += 1
        wait("gpsimd", s_vp, s_vp.n)
        for hl in range(2):
            chd = dch["dst%d" % hl]
            def sy_dst(sy, hbase=hbase, hl=hl, pj=pj, ch=chd, ntile=ntile):
                sy.dma_start(dnT_hbm.ap()[hbase + hl, :, pj * JW:(pj + 1) * JW],
                             dstage[:, hl * ntile * 128:(hl + 1) * ntile * 128]
                             ).then_inc(ch.sem, 16)
            em("gpsimd", sy_dst); chd.bump()
            dst_guard[hl] = chd.n
        if hbase == 14:
            dstc[pj] = (dch["dst0"].n, dch["dst1"].n)

    def emit_chain_tail(hbase, j, slots, st4v):
        nidx = len(slots)
        ln5 = []
        for ix in range(nidx):
            sl = slots[ix]
            wait("scalar", s_vp, st4v[ix])
            wait("scalar", s_vp, st7_guard[sl])
            def s5(s, sl=sl):
                s.activation(t2("rstd2", sl), t2("ms", sl), AF.Ln,
                             bias=tc("eps_ln"), scale=1.0 / DV).then_inc(s_sp.sem, 1)
            em("scalar", s5); s_sp.bump()
            ln5_guard[sl] = s_sp.n
            ln5.append(s_sp.n)
        ex6 = []
        for ix in range(nidx):
            sl = slots[ix]
            wait("scalar", s_sp, ln5[ix])
            def s6(s, sl=sl):
                s.activation(t2("rstd2", sl), t2("rstd2", sl), AF.Exp,
                             scale=-0.5).then_inc(s_sp.sem, 1)
            em("scalar", s6); s_sp.bump()
            ex6.append(s_sp.n)
        dnn = []
        for ix in range(nidx):
            sl = slots[ix]
            wait("vector", s_sp, ex6[ix])
            wait("vector", s_tn, dtr_slot[sl])
            def v7(v, sl=sl):
                v.tensor_scalar(dnscr[sl][:, :], asb[sl][:, 0:128],
                                t2("rstd2", sl), 1.0 - LAMBDA_INIT,
                                ALU.mult, ALU.mult).then_inc(s_vp.sem, 1)
            em("vector", v7); s_vp.bump()
            st7_guard[sl] = s_vp.n
            dnn.append(s_vp.n)
        pend.append((hbase, j, dnn, slots))

    def emit_wq(p):
        chw = dch["wq%d" % (p % 2)]
        if p >= 2:
            wait("gpsimd", s_tn, wfree_n[p - 2])
        def g_wq(g, p=p, ch=chw):
            g.dma_start(wqk_st[p % 2], qkw_in.ap()[p].rearrange("q c j f -> q (c j f)")
                        ).then_inc(ch.sem, 16)
            g.dma_start(wv_st[p % 2], vw_in.ap()[p].rearrange("q c f -> q (c f)")
                        ).then_inc(ch.sem, 16)
        em("gpsimd", g_wq)
        for _ in range(2):
            chw.bump()
        wq_pair_n.append((chw, chw.n))

    emit_wq(0)
    emit_wq(1)
    lncn = {}

    def emit_lnc_load(g):
        if g >= 8 * NJ or g < 4:
            return
        tcx_g = g % NJ
        chl = dch["lnc%d" % (g % 2)]
        wait("sync", s_tn, lnc_guard[g % 2])
        sgrp = 2 + (g % 2) if g < 6 else 3
        wait("sync", dch["lns0"], lnsc[sgrp][0])
        wait("sync", dch["lns1"], lnsc[sgrp][1])
        def sy_ln(sy, tcx=tcx_g, g=g, ch=chl):
            src = lnT_hbm.ap()[:, :, tcx * JW:(tcx + 1) * JW].rearrange("c p w -> p c w")
            dst = lnTc[g % 2].rearrange("q (c w) -> q c w", w=JW)
            sy.dma_start(dst, src).then_inc(ch.sem, 16)
        em("sync", sy_ln); chl.bump()
        lncn[g] = chl.n

    def emit_qkv_chunk(p, tcx):
        if True:
            glob_tc = p * NJ + tcx
            direct = glob_tc < 4
            chl = dch["lnc%d" % (glob_tc % 2)]
            if tcx == 0:
                wait("tensor", *wq_pair_n[p])
            if not direct:
                wait("tensor", chl, lncn[glob_tc])
            else:
                wait("tensor", s_vp, lnTc_ready[glob_tc])
            for jt in range(4):
                k = qk_cnt[0]
                bank = bank512[k % 2]
                if k >= 2:
                    wait("tensor", s_vp, qk_evac_n[k - 2])
                for c in range(NC):
                    def t_qk(t, jt=jt, c=c, gp=glob_tc % 2, bank=bank, p=p):
                        ins = t.matmul(bank[:, 0:JW],
                                       wqk_st[p % 2][:, c * 512 + jt * 128: c * 512 + jt * 128 + 128],
                                       lnTc[gp][:, c * JW:(c + 1) * JW],
                                       start=(c == 0), stop=(c == NC - 1))
                        if c == NC - 1:
                            ins.then_inc(s_tn.sem, 1)
                    em("tensor", t_qk)
                s_tn.bump()
                wait("vector", s_tn, s_tn.n)
                sc = 0.125 if jt in (0, 2) else 1.0
                def v_qke(v, jt=jt, tcx=tcx, bank=bank, sc=sc):
                    dst = qkT[:, jt * T + tcx * JW: jt * T + tcx * JW + JW]
                    if sc == 1.0:
                        v.tensor_copy(dst, bank[:, 0:JW]).then_inc(s_vp.sem, 1)
                    else:
                        v.tensor_single_scalar(dst, bank[:, 0:JW], sc,
                                               ALU.mult).then_inc(s_vp.sem, 1)
                em("vector", v_qke); s_vp.bump()
                qk_evac_n.append(s_vp.n)
                qk_cnt[0] += 1
            for tl in range(JW // 128):
                ti = tcx * (JW // 128) + tl
                k = v_cnt[0]
                bank = bank512[2 + k % 2]
                if k >= 2:
                    wait("tensor", s_vp, v_evac_n[k - 2])
                for c in range(NC):
                    def t_v(t, c=c, tl=tl, gp=glob_tc % 2, bank=bank, p=p):
                        ins = t.matmul(bank[:, 0:256],
                                       lnTc[gp][:, c * JW + tl * 128: c * JW + tl * 128 + 128],
                                       wv_st[p % 2][:, c * 256:(c + 1) * 256],
                                       start=(c == 0), stop=(c == NC - 1))
                        if c == NC - 1:
                            ins.then_inc(s_tn.sem, 1)
                    em("tensor", t_v)
                s_tn.bump()
                wait("vector", s_tn, s_tn.n)
                if p == 0 and v_cnt[0] == 0:
                    wait("vector", s_init, vbuf_ready)
                def v_ev(v, ti=ti, bank=bank):
                    v.tensor_copy(vbuf[:, ti * VCH: ti * VCH + 128],
                                  bank[:, 0:128]).then_inc(s_vp.sem, 1)
                    v.tensor_copy(vbuf[:, ti * VCH + 136: ti * VCH + 264],
                                  bank[:, 128:256]).then_inc(s_vp.sem, 1)
                em("vector", v_ev); s_vp.bump(); s_vp.bump()
                v_evac_n.append(s_vp.n)
                v_cnt[0] += 1
            lnc_guard[glob_tc % 2] = s_tn.n
            emit_lnc_load(glob_tc + 2)

    assert NJ == 4 and NT == 16
    grps = [list(range(g0, g0 + 4)) for g0 in range(0, NT, 4)]
    p1_loads(grps[0]); p1_stats(grps[0])
    p1_apply_tr(grps[0])
    p1_loads(grps[1]); p1_stats(grps[1])
    emit_qkv_chunk(0, 0)
    p1_apply_tr(grps[1])
    p1_loads(grps[2]); p1_stats(grps[2])
    emit_qkv_chunk(0, 1)
    p1_apply_tr(grps[2])
    p1_loads(grps[3]); p1_stats(grps[3])
    emit_qkv_chunk(0, 2)
    p1_apply_tr(grps[3])
    emit_qkv_chunk(0, 3)
    phase1_vp_end = s_vp.n
    wfree_n.append(s_tn.n)
    emit_wq(2)

    for p in range(8):
        if p > 0:
            wait("tensor", *wq_pair_n[p])
            wait("tensor", s_vp, qk_evac_n[-1])
            for tcx in range(NJ):
                emit_qkv_chunk(p, tcx)
            wfree_n.append(s_tn.n)
            if p + 2 < 8:
                emit_wq(p + 2)

        wait("tensor", s_vp, qk_evac_n[-1])
        wait("tensor", s_vp, v_evac_n[-1])
        if p == 0:
            wait("tensor", s_init, vbuf_ready)
            wait("scalar", s_vp, red_done[NT - 1])
        for j in range(NJ):
            cmax = min(4 * j + 3, NT - 1)
            exp_n = {}
            mask_n = {}
            pv_guard = pv_hist[0]
            for cc in range(cmax + 1):
                for a in range(2):
                    qs = 0 if a == 0 else 2
                    ks = 1 if a == 0 else 3
                    k = s_cnt[0]
                    bankp = spair[k % 2]
                    if k >= 2:
                        wait("tensor", s_sp, s_ev_n[k - 2])
                    sc0 = max(0, cc - 4 * j) * 128
                    for hl in range(2):
                        def t_s(t, cc=cc, j=j, qs=qs, ks=ks, hl=hl, sc0=sc0,
                                bankp=bankp, last=(hl == 1)):
                            qoff = 64 * hl
                            ins = t.matmul(bankp[:, hl * JW + sc0: (hl + 1) * JW],
                                           qkT[qoff:qoff + 64, ks * T + cc * 128: ks * T + cc * 128 + 128],
                                           qkT[qoff:qoff + 64, qs * T + j * JW + sc0: qs * T + (j + 1) * JW],
                                           start=True, stop=True,
                                           tile_position=(64 * hl, 0))
                            if last:
                                ins.then_inc(s_tn.sem, 1)
                        em("tensor", t_s)
                    s_tn.bump()
                    wait("scalar", s_tn, s_tn.n)
                    if pv_guard:
                        wait("scalar", s_tn, pv_guard)
                        pv_guard = 0
                    def s_exp(s, a=a, cc=cc, sc0=sc0, bankp=bankp):
                        if sc0 == 0:
                            s.activation(pT[a][:, cc * 2 * JW: (cc + 1) * 2 * JW],
                                         bankp[:, 0:2 * JW],
                                         AF.Exp).then_inc(s_sp.sem, 1)
                        else:
                            src_v = bankp.rearrange("p (t w) -> p t w", w=JW)[:, :, sc0:]
                            dst_v = pT[a][:, cc * 2 * JW: (cc + 1) * 2 * JW].rearrange(
                                "p (t w) -> p t w", w=JW)[:, :, sc0:]
                            s.activation(dst_v, src_v, AF.Exp).then_inc(s_sp.sem, 1)
                    em("scalar", s_exp); s_sp.bump()
                    s_ev_n.append(s_sp.n)
                    s_cnt[0] += 1
                    exp_n[(a, cc)] = s_sp.n
                    if 4 * j <= cc and (cc - 4 * j) * 128 < JW:
                        s0 = cc - 4 * j
                        wait("vector", s_sp, s_sp.n)
                        for hl in range(2):
                            def v_mask(v, a=a, cc=cc, s0=s0, hl=hl):
                                sl_ = pT[a][:, cc * 2 * JW + hl * JW + s0 * 128:
                                            cc * 2 * JW + hl * JW + s0 * 128 + 128]
                                v.tensor_mul(sl_, sl_, mask01[:, :]).then_inc(s_vp.sem, 1)
                            em("vector", v_mask); s_vp.bump()
                        mask_n[(a, cc)] = s_vp.n
            ntile = JW // 128
            copy_n = []
            slots = []
            for tl in range(ntile):
                gidx = j * ntile + tl
                for hl in range(2):
                    voff = 136 * hl
                    apar = pv_groups[0] % 2
                    base = apar * 512
                    if len(vd1_n) >= 2:
                        wait("tensor", s_vp, vd1_n[-2])
                    for a in range(2):
                        wait("tensor", s_sp, exp_n[(a, gidx)])
                        wait("tensor", s_vp, mask_n[(a, gidx)])
                        for cc in range(gidx + 1):
                            def t_pv(t, a=a, cc=cc, tl=tl, hl=hl, voff=voff,
                                     base=base, gidx=gidx,
                                     last=(a == 1 and cc == gidx)):
                                ins = t.matmul(pCD[:, base + a * 132: base + a * 132 + 129],
                                               pT[a][:, cc * 2 * JW + hl * JW + tl * 128:
                                                      cc * 2 * JW + hl * JW + tl * 128 + 128],
                                               vbuf[:, cc * VCH + voff: cc * VCH + voff + 129],
                                               start=(cc == 0), stop=(cc == gidx))
                                if last:
                                    ins.then_inc(s_tn.sem, 1)
                            em("tensor", t_pv)
                    s_tn.bump()
                    pv_groups[0] += 1
                    pv_hist[0] = s_tn.n
                    wait("vector", s_tn, s_tn.n)
                    sl = gslot[0] % NSL
                    gslot[0] += 1
                    slots.append(sl)
                    def v_cp(v, sl=sl, base=base):
                        v.tensor_copy(
                            asb[sl].rearrange("p (t c) -> p t c", c=132)[:, :, 0:129],
                            pCD[:, base:base + 264].rearrange("p (t c) -> p t c", c=132)[:, :, 0:129]
                        ).then_inc(s_vp.sem, 1)
                    em("vector", v_cp); s_vp.bump()
                    vd1_n.append(s_vp.n)
                    copy_n.append(s_vp.n)

            if len(pend) >= 2:
                flush_chunk(*pend.pop(0))

            nidx = len(slots)
            st_n = [list(copy_n)]
            cur = []
            for ix in range(nidx):
                sl = slots[ix]
                wait("vector", s_vp, st_n[0][ix])
                def v0(v, sl=sl):
                    v.reciprocal(tiny[:, 64 + 4 * sl: 64 + 4 * sl + 2],
                                 asb[sl].rearrange("p (t c) -> p t c", c=132)[:, :, 128:129]
                                 ).then_inc(s_vp.sem, 1)
                em("vector", v0); s_vp.bump()
                cur.append(s_vp.n)
            st_n.append(cur)
            cur = []
            for ix in range(nidx):
                sl = slots[ix]
                wait("vector", s_vp, st_n[1][ix])
                def v1(v, sl=sl):
                    v.tensor_scalar(t2("r2", sl), t2("r2", sl), tc("lam"), -1.0,
                                    ALU.mult, ALU.mult).then_inc(s_vp.sem, 1)
                em("vector", v1); s_vp.bump()
                cur.append(s_vp.n)
            st_n.append(cur)
            cur = []
            for ix in range(nidx):
                sl = slots[ix]
                wait("vector", s_vp, st_n[1][ix])
                def v2(v, sl=sl):
                    v.tensor_scalar(asb[sl][:, 0:128], asb[sl][:, 0:128],
                                    t2("r1", sl), None, ALU.mult).then_inc(s_vp.sem, 1)
                em("vector", v2); s_vp.bump()
                cur.append(s_vp.n)
            st_n.append(cur)
            cur = []
            for ix in range(nidx):
                sl = slots[ix]
                wait("vector", s_vp, max(st_n[2][ix], st_n[3][ix]))
                def v3(v, sl=sl):
                    v.scalar_tensor_tensor(asb[sl][:, 0:128], asb[sl][:, 132:260],
                                           t2("r2", sl), asb[sl][:, 0:128],
                                           ALU.mult, ALU.add).then_inc(s_vp.sem, 1)
                em("vector", v3); s_vp.bump()
                cur.append(s_vp.n)
            st_n.append(cur)
            cur = []
            for ix in range(nidx):
                sl = slots[ix]
                wait("vector", s_vp, st_n[4][ix])
                wait("vector", s_sp, ln5_guard[sl])
                def v4(v, sl=sl):
                    v.scalar_tensor_tensor(asb[sl][:, 132:260], asb[sl][:, 0:128],
                                           1.0, asb[sl][:, 0:128], ALU.mult,
                                           ALU.mult,
                                           accum_out=t2("ms", sl)).then_inc(s_vp.sem, 1)
                em("vector", v4); s_vp.bump()
                cur.append(s_vp.n)
            st_n.append(cur)
            if pend_sc[0] is not None:
                emit_chain_tail(*pend_sc[0])
            pend_sc[0] = (2 * p, j, slots, list(st_n[5]))

    while len(pend) >= 2:
        flush_chunk(*pend.pop(0))
    if pend_sc[0] is not None:
        emit_chain_tail(*pend_sc[0])
        pend_sc[0] = None
    while pend:
        flush_chunk(*pend.pop(0))
    phase2_tn_end = s_tn.n
    phase2_vp_end = s_vp.n
    phase2_sp_end = s_sp.n

    wait("gpsimd", s_tn, wfree_n[-1])
    def g_wp(g):
        src = wproj.ap().rearrange("q dc c d -> q (dc c d)")
        g.dma_start(wproj_sb, src).then_inc(dch["wp"].sem, 16)
    em("gpsimd", g_wp)
    dch["wp"].bump()
    wp_done = dch["wp"].n

    projev_n = []
    proj_last_tn = 0
    dT_guard = [0, 0]
    dT_n = None
    x1s_guard = [0, 0]
    hts_guard = [0, 0]
    rtv_n = []
    h_sp_done = {}
    h_vp_done = {}
    bias3_n = {}
    htr3_done = {}

    def apply3(pj):
        par = pj % 2
        wait("scalar", s_vp, bias3_n[pj])
        def s3_ap(s, par=par, xpar=pj % 4):
            s.activation(xbuf[xpar][:, :], x1t[par][:, :], AF.Identity,
                         bias=tc("bias", par),
                         scale=tc("rstd", par)).then_inc(s_sp.sem, 1)
        em("scalar", s3_ap); s_sp.bump()
        h_sp_done[pj] = s_sp.n
        wait("vector", s_sp, s_sp.n)
        wait("vector", dch["pre"], pre_done)
        if pj >= 2:
            wait("vector", s_tn, htr3_done[pj - 2])
        def v3_h(v, par=par, xpar=pj % 4):
            v.tensor_mul(hbuf[par][:, :], xbuf[xpar][:, :],
                         rmsw[:, :]).then_inc(s_vp.sem, 1)
        em("vector", v3_h); s_vp.bump()
        h_vp_done[pj] = s_vp.n

    def flush3(pj):
        wait("tensor", s_vp, h_vp_done[pj])
        g = [(dch["hts%d" % (pj % 2)], hts_guard[pj % 2])] if pj >= 2 else []
        transpose_tile(hbuf[pj % 2], hTstage[pj % 2], first_evac_guards=g)
        htr3_done[pj] = s_tn.n
        hT_ready = s_vp.n
        wait("sync", s_vp, hT_ready)
        chh = dch["hts%d" % (pj % 2)]
        def sy_hT(sy, pj=pj, ch=chh):
            dst = hT_hbm.ap()[:, :, pj * 128:(pj + 1) * 128].rearrange("c p w -> p c w")
            src = hTstage[pj % 2].rearrange("p (c w) -> p c w", w=128)
            sy.dma_start(dst, src).then_inc(ch.sem, 16)
        em("sync", sy_hT); chh.bump()
        hts_guard[pj % 2] = chh.n
        wait("tensor", s_vp, hT_ready)
        wait("tensor", dch["rw"], routw_done)
        if pj >= 1:
            wait("tensor", s_vp, rtv_n[pj - 1])
        for c in range(NC):
            def t_rt(t, c=c, pj=pj):
                ins = t.matmul(pCD[:, 0:8], hTstage[pj % 2][:, c * 128:(c + 1) * 128],
                               routw[:, c * 8:(c + 1) * 8],
                               start=(c == 0), stop=(c == NC - 1))
                if c == NC - 1:
                    ins.then_inc(s_tn.sem, 1)
            em("tensor", t_rt)
        s_tn.bump()
        wait("vector", s_tn, s_tn.n)
        vchain([
            lambda v: v.reduce_max(tc("m1"), pCD[:, 0:8], axis=AX.X).then_inc(s_vp.sem, 1),
            lambda v: v.tensor_scalar(i8[:, 0:8], pCD[:, 0:8], tc("m1"), None,
                                      ALU.is_ge).then_inc(s_vp.sem, 1),
            lambda v: v.scalar_tensor_tensor(i8[:, 8:16], i8[:, 0:8], -1e30,
                                             pCD[:, 0:8], ALU.mult,
                                             ALU.add).then_inc(s_vp.sem, 1),
            lambda v: v.reduce_max(tc("m2"), i8[:, 8:16], axis=AX.X).then_inc(s_vp.sem, 1),
            lambda v: v.tensor_sub(tc("dlt"), tc("m1"), tc("m2")).then_inc(s_vp.sem, 1),
        ])
        def v_i2(v):
            v.tensor_scalar(i8[:, 16:24], i8[:, 8:16], tc("m2"), None,
                            ALU.is_ge).then_inc(s_vp.sem, 1)
        em("vector", v_i2); s_vp.bump()
        wait("scalar", s_vp, s_vp.n)
        def s_w1(s):
            s.activation(tc("w1"), tc("dlt"), AF.Exp,
                         scale=-1.0).then_inc(s_sp.sem, 1)
        em("scalar", s_w1); s_sp.bump()
        wait("vector", s_sp, s_sp.n)
        vchain([
            lambda v: v.tensor_single_scalar(tc("w1"), tc("w1"), 1.0,
                                             ALU.add).then_inc(s_vp.sem, 1),
            lambda v: v.reciprocal(tc("w1"), tc("w1")).then_inc(s_vp.sem, 1),
            lambda v: v.tensor_scalar(tc("w2"), tc("w1"), -1.0, 1.0, ALU.mult,
                                      ALU.add).then_inc(s_vp.sem, 1),
            lambda v: v.tensor_scalar(i8[:, 24:32], i8[:, 0:8], tc("w1"), None,
                                      ALU.mult).then_inc(s_vp.sem, 1),
            lambda v, pj=pj: v.scalar_tensor_tensor(s_route[:, pj * 8:(pj + 1) * 8],
                                                    i8[:, 16:24], tc("w2"),
                                                    i8[:, 24:32], ALU.mult,
                                                    ALU.add).then_inc(s_vp.sem, 1),
        ])
        rtv_n.append(s_vp.n)

    n_dch = NT // (JW // 128)
    dT_loads = {}

    def emit_dT_load(cx, extra_tn):
        chD = dch["dT%d" % (cx % 2)]
        wait("sync", dch["dst0"], dstc[cx][0])
        wait("sync", dch["dst1"], dstc[cx][1])
        wait("sync", s_tn, extra_tn)
        def sy_dT(sy, cx=cx, ch=chD):
            src = dnT_hbm.ap()[:, :, cx * JW:(cx + 1) * JW].rearrange("c p w -> p c w")
            dst = dTbuf[cx % 2].rearrange("q (c w) -> q c w", w=JW)
            sy.dma_start(dst, src).then_inc(ch.sem, 16)
        em("sync", sy_dT); chD.bump()
        dT_loads[cx] = (chD, chD.n)

    emit_dT_load(0, wfree_n[-1])
    if n_dch > 1:
        emit_dT_load(1, phase2_tn_end)
    for j in range(NT):
        cx = j // (JW // 128)
        tl = j % (JW // 128)
        if tl == 0:
            if cx + 1 < n_dch and (cx + 1) not in dT_loads:
                emit_dT_load(cx + 1, dT_guard[(cx + 1) % 2])
            dT_n = dT_loads[cx]
        chx = dch["x3%d" % (j % 4)]
        if j >= 4:
            wait("sync", s_vp, projev_n[4 * (j - 4) + 3])
            wait("sync", s_vp, h_vp_done[j - 4])
            wait("sync", s_sp, h_sp_done[j - 4])
        def sy_x(sy, j=j, ch=chx):
            sy.dma_start(xbuf[j % 4][:, :],
                         x_in.ap()[j * 128:(j + 1) * 128, :]).then_inc(ch.sem, 16)
        em("sync", sy_x); chx.bump()
        x3j = (chx, chx.n)
        wait("tensor", dT_n[0], dT_n[1])
        if j == 0:
            wait("tensor", s_vp, phase2_vp_end)
            wait("tensor", s_sp, phase2_sp_end)
        for dcx in range(4):
            kk = 4 * j + dcx
            wait("tensor", dch["wp"], wp_done)
            bank = bank512[kk % 2]
            if kk >= 2:
                wait("tensor", s_vp, projev_n[kk - 2])
            for c in range(NC):
                def t_pj(t, c=c, dcx=dcx, tl=tl, cx=cx, bank=bank):
                    ins = t.matmul(bank[:, 0:512],
                                   dTbuf[cx % 2][:, c * JW + tl * 128: c * JW + tl * 128 + 128],
                                   wproj_sb[:, dcx * NC * 512 + c * 512: dcx * NC * 512 + (c + 1) * 512],
                                   start=(c == 0), stop=(c == NC - 1))
                    if c == NC - 1:
                        ins.then_inc(s_tn.sem, 1)
                em("tensor", t_pj)
            s_tn.bump()
            wait("vector", s_tn, s_tn.n)
            wait("vector", *x3j)
            if j >= 2 and dcx == 0:
                wait("vector", dch["x1s%d" % (j % 2)], x1s_guard[j % 2])
                wait("vector", s_sp, h_sp_done[j - 2])
            def v_pe(v, j=j, dcx=dcx, bank=bank):
                v.tensor_add(x1t[j % 2][:, dcx * 512:(dcx + 1) * 512], bank[:, 0:512],
                             xbuf[j % 4][:, dcx * 512:(dcx + 1) * 512]).then_inc(s_vp.sem, 1)
            em("vector", v_pe); s_vp.bump()
            projev_n.append(s_vp.n)
        dT_guard[cx % 2] = s_tn.n
        if j == NT - 1:
            proj_end_tn = s_tn.n
        wait("sync", s_vp, projev_n[-1])
        chs = dch["x1s%d" % (j % 2)]
        def sy_x1(sy, j=j, ch=chs):
            sy.dma_start(x1_hbm.ap()[j * 128:(j + 1) * 128, :],
                         x1t[j % 2][:, :]).then_inc(ch.sem, 16)
        em("sync", sy_x1); chs.bump()
        x1s_guard[j % 2] = chs.n
        if j == NT // 2 - 1:
            x1s_half = (dch["x1s0"].n, dch["x1s1"].n)
        if j >= 2:
            flush3(j - 2)
        if j >= 1:
            apply3(j - 1)
        wait("scalar", s_vp, projev_n[-1])
        wait("vector", s_vp, projev_n[-1])
        par = j % 2
        def s3_sq(s, j=j, par=par):
            s.activation(sqjunk[:, :], x1t[par][:, :], AF.Square,
                         accum_out=tc("sumsq", par)).then_inc(s_sp.sem, 1)
        em("scalar", s3_sq); s_sp.bump()
        sq3_n = s_sp.n
        def v3_red(v, j=j, par=par):
            v.reduce_sum(tc("sum", par), x1t[par][:, :], axis=AX.X).then_inc(s_vp.sem, 1)
        em("vector", v3_red); s_vp.bump()
        wait("vector", s_vp, s_vp.n)
        def v3_msq(v, par=par):
            v.tensor_scalar(tc("varD", par), tc("sum", par), tc("sum", par),
                            1.0 / D, ALU.mult, ALU.mult).then_inc(s_vp.sem, 1)
        em("vector", v3_msq); s_vp.bump()
        wait("vector", s_vp, s_vp.n)
        wait("vector", s_sp, sq3_n)
        def v3_var(v, par=par):
            v.tensor_sub(tc("varD", par), tc("sumsq", par),
                         tc("varD", par)).then_inc(s_vp.sem, 1)
        em("vector", v3_var); s_vp.bump()
        wait("scalar", s_vp, s_vp.n)
        def s3_ln(s, par=par):
            s.activation(tc("rstd", par), tc("varD", par), AF.Ln,
                         bias=tc("eps11"),
                         scale=(1.0 + EPS_RMS_FFN) / D).then_inc(s_sp.sem, 1)
        em("scalar", s3_ln); s_sp.bump()
        wait("scalar", s_sp, s_sp.n)
        def s3_ex(s, par=par):
            s.activation(tc("rstd", par), tc("rstd", par), AF.Exp,
                         scale=-0.5).then_inc(s_sp.sem, 1)
        em("scalar", s3_ex); s_sp.bump()
        wait("vector", s_sp, s_sp.n)
        def v3_bias(v, par=par):
            v.tensor_scalar(tc("bias", par), tc("sum", par), tc("rstd", par),
                            -1.0 / D, ALU.mult, ALU.mult).then_inc(s_vp.sem, 1)
        em("vector", v3_bias); s_vp.bump()
        bias3_n[j] = s_vp.n
    apply3(NT - 1)
    flush3(NT - 2)
    flush3(NT - 1)

    phase3_tn_end = s_tn.n
    phase3_vp_end = s_vp.n
    phase3_sp_end = s_sp.n
    if dump_sroute is not None:
        wait("sync", s_vp, s_vp.n)
        def sy_sr(sy):
            sy.dma_start(dump_sroute.ap(), s_route[:, :]).then_inc(dch["pre"].sem, 16)
        em("sync", sy_sr); dch["pre"].bump()

    g_n = []
    u_n = []
    sil_n = []
    amul_n = []
    dn_n = []
    dnev_n = []
    yst_n = [0, 0]
    u_half_end = 0

    def wg_src(e, ft, which):
        if e < NE:
            b = wg_in if which == 0 else wu_in
            return b.ap()[e, ft]
        b = swg_in if which == 0 else swu_in
        return b.ap()[ft]

    hth2_n = [0]
    yini0_n = [0]
    for hf in range(2):
        if hf == 0:
            chh = dch["hth"]
            wait("sync", dch["hts0"], dch["hts0"].n)
            wait("sync", dch["hts1"], dch["hts1"].n)
            def sy_hth(sy, ch=chh):
                src = hT_hbm.ap()[:, :, 0:HALF].rearrange("c p w -> p c w")
                dst = hT_half.rearrange("q (c w) -> q c w", w=HALF)
                sy.dma_start(dst, src).then_inc(ch.sem, 16)
            em("sync", sy_hth); chh.bump()
            hth_n = chh.n
        else:
            hth_n = hth2_n[0]
        if hf == 0:
            yini_n = None
        else:
            wait("sync", dch["x1s0"], dch["x1s0"].n)
            wait("sync", dch["x1s1"], dch["x1s1"].n)
            wait("sync", dch["yst"], yst_n[0])
            def sy_yi(sy, hf=hf):
                src = x1_hbm.ap()[hf * HALF:(hf + 1) * HALF, :]
                src = src.rearrange("(tt p) d -> p tt d", p=128)
                dst = ybuf.rearrange("p (tt d) -> p tt d", d=D)
                sy.dma_start(dst, src).then_inc(dch["yini"].sem, 16)
            em("sync", sy_yi); dch["yini"].bump()
            yini_n = dch["yini"].n

        for e in range(NE + 1):
            eg = hf * (NE + 1) + e
            if eg == 0:
                wait("gpsimd", s_tn, proj_end_tn)
                wait("gpsimd", dch["dT0"], dch["dT0"].n)
                wait("gpsimd", dch["dT1"], dch["dT1"].n)
            NWC = 4

            def emit_wd_chunk(ci, e=e, eg=eg):
                if ci == 0:
                    if eg == 0:
                        wait("gpsimd", s_tn, phase3_tn_end)
                        wait("gpsimd", s_sp, phase3_sp_end)
                        wait("gpsimd", s_vp, phase3_vp_end)
                        wait("gpsimd", dch["x1s0"], dch["x1s0"].n)
                        wait("gpsimd", dch["x1s1"], dch["x1s1"].n)
                        wait("gpsimd", dch["hts0"], dch["hts0"].n)
                        wait("gpsimd", dch["hts1"], dch["hts1"].n)
                    else:
                        wait("gpsimd", s_tn, dn_n[-1])
                fc0 = ci * (NF // NWC)
                fc1 = (ci + 1) * (NF // NWC)
                def g_wd(g, e=e, fc0=fc0, fc1=fc1):
                    if e < NE:
                        src = wd_in.ap()[e].rearrange("(fc p) d -> p fc d", p=128)
                    else:
                        src = swd_in.ap().rearrange("(fc p) d -> p fc d", p=128)
                    dst = wd_sb.rearrange("p (fc d) -> p fc d", d=D)
                    g.dma_start(dst[:, fc0:fc1, :],
                                src[:, fc0:fc1, :]).then_inc(dch["wd"].sem, 16)
                em("gpsimd", g_wd); dch["wd"].bump()
            for ft in range(NF):
                blk = eg * NF + ft
                chb = dch["wg%d" % (blk % 2)]
                if blk >= 2:
                    wait("gpsimd", s_tn, u_n[(blk - 2 + 1) * n_tch - 1])
                def g_blk(g, e=e, ft=ft, blk=blk, ch=chb):
                    g.dma_start(wgu_st[0][blk % 2].rearrange("p (c f) -> p c f", f=128),
                                wg_src(e, ft, 0)).then_inc(ch.sem, 16)
                    g.dma_start(wgu_st[1][blk % 2].rearrange("p (c f) -> p c f", f=128),
                                wg_src(e, ft, 1)).then_inc(ch.sem, 16)
                em("gpsimd", g_blk)
                chb.bump(); chb.bump()
                blk_n = chb.n
                if eg == 0 and ft == 0:
                    wait("gpsimd", dch["x1s0"], x1s_half[0])
                    wait("gpsimd", dch["x1s1"], x1s_half[1])
                    def g_yi0(g):
                        src = x1_hbm.ap()[0:HALF, :]
                        src = src.rearrange("(tt p) d -> p tt d", p=128)
                        dst = ybuf.rearrange("p (tt d) -> p tt d", d=D)
                        g.dma_start(dst, src).then_inc(dch["yini"].sem, 16)
                    em("gpsimd", g_yi0); dch["yini"].bump()
                    yini0_n[0] = dch["yini"].n
                if 1 <= ft <= NWC:
                    emit_wd_chunk(ft - 1)
                wait("tensor", chb, blk_n)
                wait("tensor", dch["hth"], hth_n)
                for tcx in range(n_tch):
                    kk = blk * n_tch + tcx
                    bg = bank512[kk % 2]
                    bu = bank512[2 + kk % 2]
                    if len(sil_n) >= 2:
                        wait("tensor", s_sp, sil_n[-2])
                    if len(amul_n) >= 2:
                        wait("tensor", s_vp, amul_n[-2])
                    if kk == 0:
                        wait("tensor", s_vp, phase3_vp_end)
                        wait("tensor", s_sp, phase3_sp_end)
                    for c in range(NC):
                        def t_g(t, c=c, tcx=tcx, blk=blk, bg=bg):
                            ins = t.matmul(bg[:, 0:TCW],
                                           wgu_st[0][blk % 2][:, c * 128:(c + 1) * 128],
                                           hT_half[:, c * HALF + tcx * TCW: c * HALF + tcx * TCW + TCW],
                                           start=(c == 0), stop=(c == NC - 1))
                            if c == NC - 1:
                                ins.then_inc(s_tn.sem, 1)
                        em("tensor", t_g)
                    s_tn.bump()
                    g_n.append(s_tn.n)
                    for c in range(NC):
                        def t_u(t, c=c, tcx=tcx, blk=blk, bu=bu):
                            ins = t.matmul(bu[:, 0:TCW],
                                           wgu_st[1][blk % 2][:, c * 128:(c + 1) * 128],
                                           hT_half[:, c * HALF + tcx * TCW: c * HALF + tcx * TCW + TCW],
                                           start=(c == 0), stop=(c == NC - 1))
                            if c == NC - 1:
                                ins.then_inc(s_tn.sem, 1)
                        em("tensor", t_u)
                    s_tn.bump()
                    u_n.append(s_tn.n)
                    wait("scalar", s_tn, g_n[-1])
                    spar = len(sil_n) % 2
                    if not sil_n:
                        wait("scalar", dch["hts0"], dch["hts0"].n)
                        wait("scalar", dch["hts1"], dch["hts1"].n)
                    if len(sil_n) >= 2:
                        wait("scalar", s_vp, amul_n[-2])
                    def s_sil(s, spar=spar, bg=bg):
                        s.activation(sgscr[spar][:, :], bg[:, 0:TCW],
                                     AF.Silu).then_inc(s_sp.sem, 1)
                    em("scalar", s_sil); s_sp.bump()
                    sil_n.append(s_sp.n)
                    wait("vector", s_sp, sil_n[-1])
                    wait("vector", s_tn, u_n[-1])
                    if eg >= 1 and ft == 0 and tcx == 0:
                        wait("vector", s_tn, dn_n[-1])
                    if eg == 0 and ft == 0 and tcx == 0:
                        wait("vector", dch["hts0"], dch["hts0"].n)
                        wait("vector", dch["hts1"], dch["hts1"].n)
                        wait("vector", s_tn, phase3_tn_end)
                        wait("vector", dch["dT0"], dch["dT0"].n)
                        wait("vector", dch["dT1"], dch["dT1"].n)
                    def v_am(v, ft=ft, tcx=tcx, spar=spar, bu=bu):
                        v.tensor_mul(abuf[:, ft * HALF + tcx * TCW: ft * HALF + tcx * TCW + TCW],
                                     sgscr[spar][:, :], bu[:, 0:TCW]).then_inc(s_vp.sem, 1)
                    em("vector", v_am); s_vp.bump()
                    amul_n.append(s_vp.n)
            a_ready = s_vp.n
            wait("tensor", dch["wd"], dch["wd"].n)
            wait("tensor", s_vp, a_ready)
            for tl in range(NTH):
                for dcx in range(4):
                    kk = len(dn_n)
                    bank = bank512[4 + kk % 2]
                    if kk >= 2:
                        wait("tensor", s_vp, dnev_n[kk - 2])
                    if kk == 0:
                        wait("tensor", s_vp, phase3_vp_end)
                    for fc in range(NF):
                        def t_dn(t, fc=fc, tl=tl, dcx=dcx, eg=eg, bank=bank):
                            ins = t.matmul(bank[:, 0:512],
                                           abuf[:, fc * HALF + tl * 128: fc * HALF + tl * 128 + 128],
                                           wd_sb[:, fc * D + dcx * 512: fc * D + (dcx + 1) * 512],
                                           start=(fc == 0), stop=(fc == NF - 1))
                            if fc == NF - 1:
                                ins.then_inc(s_tn.sem, 1)
                        em("tensor", t_dn)
                    s_tn.bump()
                    dn_n.append(s_tn.n)
                    wait("vector", s_tn, dn_n[-1])
                    if e == 0 and tl == 0 and dcx == 0:
                        wait("vector", dch["yini"],
                             yini0_n[0] if yini_n is None else yini_n)
                    def v_dne(v, e=e, tl=tl, dcx=dcx, hf=hf, bank=bank):
                        ysl = ybuf[:, tl * D + dcx * 512: tl * D + (dcx + 1) * 512]
                        if e < NE:
                            cix = (hf * NTH + tl) * 8 + e
                            v.scalar_tensor_tensor(ysl, bank[:, 0:512],
                                                   s_route[:, cix:cix + 1], ysl,
                                                   ALU.mult, ALU.add).then_inc(s_vp.sem, 1)
                        else:
                            v.tensor_add(ysl, ysl, bank[:, 0:512]).then_inc(s_vp.sem, 1)
                    em("vector", v_dne); s_vp.bump()
                    dnev_n.append(s_vp.n)
            if hf == 0 and e == NE:
                u_half_end = u_n[-1]
        if hf == 0:
            chh2 = dch["hth"]
            wait("sync", s_tn, u_n[-1])
            def sy_hth2(sy, ch=chh2):
                src_ = hT_hbm.ap()[:, :, HALF:2 * HALF].rearrange("c p w -> p c w")
                dst_ = hT_half.rearrange("q (c w) -> q c w", w=HALF)
                sy.dma_start(dst_, src_).then_inc(ch.sem, 16)
            em("sync", sy_hth2); chh2.bump()
            hth2_n[0] = chh2.n
        wait("sync", s_vp, dnev_n[-1])
        def sy_yst(sy, hf=hf):
            src = ybuf.rearrange("p (tt d) -> p tt d", d=D)
            dst = out_t.ap()[hf * HALF:(hf + 1) * HALF, :]
            dst = dst.rearrange("(tt p) d -> p tt d", p=128)
            sy.dma_start(dst, src).then_inc(dch["yst"].sem, 16)
        em("sync", sy_yst); dch["yst"].bump()
        yst_n[hf] = dch["yst"].n

    for nm, ch in dch.items():
        if nm != "yst" and ch.n > 0:
            wait("sync", ch, ch.n)
    wait("sync", dch["yst"], dch["yst"].n)

    with nc.Block() as block:
        @block.tensor
        def _(e):
            for f in ops["tensor"]:
                f(e)

        @block.vector
        def _(e):
            for f in ops["vector"]:
                f(e)

        @block.scalar
        def _(e):
            for f in ops["scalar"]:
                f(e)

        @block.gpsimd
        def _(e):
            for f in ops["gpsimd"]:
                f(e)

        @block.sync
        def _(e):
            for f in ops["sync"]:
                f(e)

    return nc


def _prep_host(inputs):
    f32 = lambda a: np.ascontiguousarray(np.asarray(a, dtype=np.float32))
    x = f32(inputs["L0_attn_x"])
    rmsw = np.ascontiguousarray(
        np.broadcast_to(f32(inputs["L0_ffn_rms_weight"])[None, :], (128, D)))
    lam = np.ascontiguousarray(np.concatenate([
        np.broadcast_to(f32(inputs[k])[None, :], (128, DH))
        for k in ("L0_attn_lambda_q1", "L0_attn_lambda_k1",
                  "L0_attn_lambda_q2", "L0_attn_lambda_k2")], axis=1))
    wqkv = f32(inputs["L0_attn_Wqkv"])
    QH_ = D // 2
    qkw = np.ascontiguousarray(
        wqkv[:, :4 * QH_].reshape(NC, 128, 4, 8, 128).transpose(3, 1, 0, 2, 4))
    vw = np.ascontiguousarray(
        wqkv[:, 4 * QH_:].reshape(NC, 128, 8, 256).transpose(2, 1, 0, 3))
    wproj_t = np.ascontiguousarray(
        f32(inputs["L0_attn_Wproj"]).reshape(NC, 128, 4, 512).transpose(1, 2, 0, 3))
    tile_gu = lambda w: np.ascontiguousarray(
        w.reshape(-1, NC, 128, NF, 128).transpose(0, 3, 2, 1, 4))
    wg_t = tile_gu(f32(inputs["L0_ffn_w_gate"]))
    wu_t = tile_gu(f32(inputs["L0_ffn_w_up"]))
    swg_t = np.ascontiguousarray(
        f32(inputs["L0_ffn_sw_gate"]).reshape(NC, 128, NF, 128).transpose(2, 1, 0, 3))
    swu_t = np.ascontiguousarray(
        f32(inputs["L0_ffn_sw_up"]).reshape(NC, 128, NF, 128).transpose(2, 1, 0, 3))
    common = dict(qkw_in=qkw, vw_in=vw,
                  wproj=wproj_t,
                  rmsw_in=rmsw, lam_in=lam,
                  routw_in=f32(inputs["L0_ffn_router_w"]),
                  wg_in=wg_t, wu_in=wu_t,
                  wd_in=f32(inputs["L0_ffn_w_down"]),
                  swg_in=swg_t, swu_in=swu_t,
                  swd_in=f32(inputs["L0_ffn_sw_down"]))
    return x, common


def kernel(**inputs):
    x, common = _prep_host(inputs)
    B = x.shape[0]
    nc = build(T=x.shape[1])
    in_maps = []
    for b in range(B):
        m = dict(common)
        m["x_in"] = np.ascontiguousarray(x[b])
        in_maps.append(m)
    res = run_bass_kernel_spmd(nc, in_maps, core_ids=list(range(B)))
    out = np.stack([np.asarray(res.results[b]["out"]) for b in range(B)], axis=0)
    return out.astype(np.float32)


if __name__ == "__main__":
    nc = build()
    print("instructions:", nc.n_instructions())
